# version 01
# baseline (speedup 1.0000x reference)
import numpy as np
import concourse.bass as bass
import concourse.mybir as mybir

F32 = mybir.dt.float32
BF16 = mybir.dt.bfloat16
FP16 = mybir.dt.float16
I32 = mybir.dt.int32
AF = mybir.ActivationFunctionType
ALU = mybir.AluOpType
AXX = mybir.AxisListType.X

PI = float(np.pi)
LN2 = float(np.log(2.0))
LN1E4 = float(np.log(10000.0))

T = 8192
E = 1024
NH = 4
HD = 64
NP = 16
NT = T // 128
EPS = 1e-5
KB = 1024


def consts_np():
    perm = np.zeros((128, 128), dtype=np.float32)
    for i in range(64):
        perm[2 * i, 2 * i + 1] = -1.0
        perm[2 * i + 1, 2 * i] = 1.0
    p_ = np.arange(128).reshape(128, 1)
    n_ = np.arange(512).reshape(1, 512)
    return {
        "ident": np.eye(128, dtype=np.float32),
        "permT": perm.T.copy(),
        "ones_row": np.ones((1, 128), dtype=np.float32),
        "ones_col": np.ones((128, 1), dtype=np.float32),
        "pmod32": (np.arange(128) % 32).astype(np.float32).reshape(128, 1),
        "pk2d": (p_ + 128.0 * np.arange(64).reshape(1, 64)).astype(np.float32),
        "tki": np.broadcast_to(np.arange(512, dtype=np.float32), (128, 512)).copy(),
        "drelu": np.concatenate(
            [np.maximum(p_ - n_ + c, 0).astype(np.float16) for c in (0, 128, 256, 384)],
            axis=1),
        "ospl": np.concatenate([np.concatenate([np.ones((64, 1)), np.zeros((64, 1))]),
                                np.concatenate([np.zeros((64, 1)), np.ones((64, 1))])],
                               axis=1).astype(np.float32),
        "eh4": np.concatenate([np.where(np.arange(4).reshape(4, 1) == h,
                                        np.ones((4, 128)), np.zeros((4, 128)))
                               for h in range(4)], axis=1).astype(np.float32),
    }


class Em:
    def __init__(self, nc, sems, target, engobj):
        self.nc, self.sems, self.target, self.engobj = nc, sems, target, engobj
        self.c = {k: 0 for k in sems}

    def do(self, eng, fn, inc=None, n=1):
        if self.target == eng:
            inst = fn(self.engobj)
            if inc is not None:
                inst.then_inc(self.sems[inc], n)
        if inc is not None:
            self.c[inc] += n
            return self.c[inc]
        return None

    def wait(self, eng, sem, val):
        assert val <= self.c[sem], (sem, val, self.c[sem])
        if self.target == eng:
            self.engobj.wait_ge(self.sems[sem], val)

    def _bump(self, sem):
        self.c[sem] += 1
        return self.c[sem]


def build_l1(dbg=False, stop=99):
    nc = bass.Bass("TRN2", target_bir_lowering=False)
    CN = consts_np()

    def din(name, shape):
        return nc.dram_tensor(name, list(shape), F32, kind="ExternalInput").ap()

    xb = din("xb", (T, E))
    wqT = din("wqT", (E, 256)); wkT = din("wkT", (E, 256))
    wvT = din("wvT", (E, 256)); wgT = din("wgT", (E, 256))
    woTb = din("woTb", (128, 2048))
    wdT = din("wdT", (E, 4))
    bq_f = din("bq_f", (128, 2)); bk_f = din("bk_f", (128, 2)); bg_f = din("bg_f", (128, 2))
    bv_row = din("bv_row", (1, 256))
    bd_col = din("bd_col", (4, 1))
    gam_pk = din("gam_pk", (128, 2)); bet_pk = din("bet_pk", (128, 2))
    c_fmaj = din("c_fmaj", (128, 8))
    wm_sc = din("wm_sc", (E, 2048))

    zout = nc.dram_tensor("zpart", [T, E], F32, kind="ExternalOutput").ap()
    dbg_o = {}
    if dbg:
        for nm, sh in [("t_q", (128, 512)), ("t_k", (128, 512)), ("t_v", (128, 260)),
                       ("t_g", (128, 512)), ("t_h", (128, 512)), ("t_og", (128, 512)),
                       ("t_m", (128, 16))]:
            dbg_o[nm] = nc.dram_tensor(nm, list(sh), F32, kind="ExternalOutput").ap()

    ic = {k: nc.inline_tensor(np.ascontiguousarray(v), f"c_{k}").ap() for k, v in CN.items()}

    A = lambda name, shape, dtype: nc.alloc_sbuf_tensor(name, list(shape), dtype, side="left")

    Vp = A("Vp", (128, NT * 260), BF16)
    gf = A("gf", (128, 2 * T), BF16)
    wq_b = A("wq_b", (128, 2048), BF16); wk_b = A("wk_b", (128, 2048), BF16)
    wv_b = A("wv_b", (128, 2048), BF16); wg_b = A("wg_b", (128, 2048), BF16)
    wo_b = A("wo_b", (128, 2048), BF16); wd_b = A("wd_b", (128, 32), BF16)
    identt = A("identt", (128, 128), F32)
    permt = A("permt", (128, 128), F32)
    onesr = A("onesr", (1, 128), F32)
    onesc = A("onesc", (128, 1), F32)
    pmod32t = A("pmod32t", (128, 1), F32)
    pk2dt = A("pk2dt", (128, 64), F32)
    tkit = A("tkit", (128, 512), F32)
    osplt = A("osplt", (128, 2), F32)
    eh4t = A("eh4t", (4, 512), F32)
    bqf = A("bqf", (128, 2), F32); bkf = A("bkf", (128, 2), F32); bgf = A("bgf", (128, 2), F32)
    gampk = A("gampk", (128, 2), F32); betpk = A("betpk", (128, 2), F32)
    bdc = A("bdc", (4, 1), F32)
    cfm = A("cfm", (128, 8), F32)
    fcol = A("fcol", (128, 1), F32)
    silc = A("silc", (128, 8), F32)
    hsum = A("hsum", (128, 8), F32)
    hmean_b = A("hmean_b", (128, 8), BF16)
    bv_bc = A("bv_bc", (128, 256), F32)
    pdld = A("pdld", (4, 4), F32)
    ldbc = A("ldbc", (128, 4), F32)
    nbbc = A("nbbc", (128, 4), F32)
    ldkb = A("ldkb", (128, 4 * 64), F32)
    n2q = A("n2q", (2, 8), F32)
    n2k = A("n2k", (2, 8), F32)
    n2t = A("n2t", (2, 2), F32)
    s1a = A("s1a", (128, 4), F32)
    s2a = A("s2a", (128, 4), F32)
    abcol = A("abcol", (128, 8), F32)
    ogscr = A("ogscr", (64, 512), BF16)
    sqt = A("sqt", (128, 1024), F32)
    lncol = A("lncol", (128, 16), F32)
    zcol = A("zcol", (128, 1), F32)
    zc4 = A("zc4", (4, 1), F32)
    c5l2 = A("c5l2", (4, 1), F32)
    b2c = A("b2c", (2, 4), F32)
    bnd2 = A("bnd2", (2, 2), F32)
    tofff = A("tofff", (128, 1), F32)
    negld = A("negld", (128, 1), F32)
    nlq = A("nlq", (128, 1), F32)
    Qf = A("Qf", (128, 2 * T), BF16)
    kf0, _ = nc.bump_sbuf(32 * KB, side="left")
    Kf = nc.alloc_sbuf_tensor_at("Kf", [128, 2 * T], BF16, offset=kf0)
    if dbg:
        dbm = A("dbm", (128, 16), F32)

    ARENA = 44 * KB
    ar0, _ = nc.bump_sbuf(ARENA, side="left")
    def AT(name, shape, dtype, off):
        return nc.alloc_sbuf_tensor_at(name, list(shape), dtype, offset=ar0 + off)
    wmt = AT("wmt", (128, 2048), F32, 0)
    modrow = AT("modrow", (1, 2048), F32, 8 * KB)
    bvr = AT("bvr", (1, 256), F32, 16 * KB)
    sa_bc = AT("sa_bc", (128, 1024), F32, 18 * KB)
    capc_bc = AT("capc_bc", (128, 1024), F32, 22 * KB)
    tmp1k = AT("tmp1k", (128, 1024), F32, 0)
    q0r = AT("q0r", (128, 1024), F32, 4 * KB)
    rtmp = AT("rtmp", (128, 512), F32, 8 * KB)
    sint = AT("sint", (128, 512), F32, 10 * KB)
    cost = AT("cost", (128, 512), F32, 12 * KB)
    angt = AT("angt", (128, 512), F32, 14 * KB)
    rrt = AT("rrt", (128, 512), F32, 16 * KB)
    xr = AT("xr", (128, 2048), F32, 26 * KB)
    hp = AT("hp", (128, 4096), BF16, 34 * KB)
    ki32t = AT("ki32t", (128, 512), I32, 42 * KB)
    og = AT("og", (128, 2 * T), BF16, 0)
    drt = AT("drt", (128, 2048), FP16, 32 * KB)
    Pr = AT("Pr", (128, 1536), BF16, 36 * KB)
    ldqt = AT("ldqt", (128, 512), F32, 39 * KB)
    rbct = AT("rbct", (64, 512), F32, 39 * KB)
    rrow = AT("rrow", (1, 512), F32, 41 * KB)
    stcol = AT("stcol", (64, 4), F32, 43 * KB)
    fincol = AT("fincol", (1, 16), F32, 43 * KB + 256)
    zev = nc.alloc_sbuf_tensor_at("zev", [128, 2048], F32, offset=kf0)
    ont = nc.alloc_sbuf_tensor_at("ont", [128, 512], F32, offset=kf0 + 8 * KB)

    psA = nc.alloc_psum_tensor("psA", [128, 512], F32)
    psB = nc.alloc_psum_tensor("psB", [128, 512], F32)
    psC = nc.alloc_psum_tensor("psC", [128, 512], F32)
    psD = nc.alloc_psum_tensor("psD", [128, 512], F32)
    psE = nc.alloc_psum_tensor("psE", [128, 512], F32)
    psF = nc.alloc_psum_tensor("psF", [65, 512], F32)
    psG = nc.alloc_psum_tensor("psG", [65, 512], F32)
    psH = nc.alloc_psum_tensor("psH", [64, 512], F32)

    sems = {k: nc.alloc_semaphore(k) for k in ["ds", "dg", "dz", "t0", "v0", "s0", "vv", "sv2"]}

    def program(em: Em):
        W, D = em.wait, em.do

        def DV(fn, inc=None):
            sem = inc or "vv"
            tok = D("vector", fn, sem)
            W("vector", sem, tok)
            return tok

        def DS(fn, inc=None):
            sem = inc or "sv2"
            tok = D("scalar", fn, sem)
            W("scalar", sem, tok)
            return tok

        for nm, tile in [("ident", identt), ("permT", permt), ("ones_row", onesr),
                         ("ones_col", onesc), ("pmod32", pmod32t), ("pk2d", pk2dt),
                         ("tki", tkit), ("ospl", osplt), ("eh4", eh4t)]:
            D("sync", lambda sp, nm=nm, tile=tile: sp.dma_start(out=tile[:], in_=ic[nm]), "ds", 16)
        for src, tile in [(bq_f, bqf), (bk_f, bkf), (bg_f, bgf), (gam_pk, gampk),
                          (bet_pk, betpk), (bd_col, bdc), (c_fmaj, cfm)]:
            D("sync", lambda sp, src=src, tile=tile: sp.dma_start(out=tile[:], in_=src), "ds", 16)
        D("sync", lambda sp: sp.dma_start(out=bvr[:], in_=bv_row), "ds", 16)
        c_consts = em.c["ds"]

        for src, tile in [(wqT, wq_b), (wkT, wk_b), (wvT, wv_b), (wgT, wg_b)]:
            D("gpsimd", lambda gp, src=src, tile=tile: gp.dma_start(
                out=tile[:].rearrange("p (fc o) -> p fc o", fc=8),
                in_=src.rearrange("(fc p) o -> p fc o", p=128)), "dg", 16)
        D("gpsimd", lambda gp: gp.dma_start(
            out=wd_b[:].rearrange("p (fc o) -> p fc o", fc=8),
            in_=wdT.rearrange("(fc p) o -> p fc o", p=128)), "dg", 16)
        D("gpsimd", lambda gp: gp.dma_start(out=wo_b[:], in_=woTb), "dg", 16)
        c_wg = em.c["dg"]

        D("vector", lambda v: v.memset(zcol[:], 0.0))
        D("vector", lambda v: v.memset(zc4[:], 0.0))
        D("vector", lambda v: v.memset(c5l2[:], -5.0 * LN2))
        D("vector", lambda v: v.memset(hsum[:], 0.0))
        D("vector", lambda v: v.memset(n2q[:], 0.0))
        D("vector", lambda v: v.memset(n2k[:], 0.0))
        D("vector", lambda v: v.memset(s1a[:], 0.0))
        D("vector", lambda v: v.memset(s2a[:], 0.0))
        cv_ones = D("vector", lambda v: v.memset(
            Vp[:].rearrange("p (kth c) -> p kth c", c=65)[:, :, 64:65], 1.0), "v0")

        W("scalar", "ds", c_consts)
        D("scalar", lambda s: s.activation(fcol[:], pmod32t[:], AF.Exp, bias=zcol[:],
                                           scale=-LN1E4 / 32.0), "s0")
        cs_sil = D("scalar", lambda s: s.activation(silc[:], cfm[:], AF.Silu, bias=zcol[:]), "s0")

        mod_ps = [psA, psB, psC, psD]
        for fc in range(8):
            cw = D("sync", lambda sp, fc=fc: sp.dma_start(
                out=wmt[:], in_=wm_sc[128 * fc:128 * (fc + 1), :]), "ds", 16)
            W("tensor", "ds", cw)
            if fc == 0:
                W("tensor", "s0", cs_sil)
            for half in range(4):
                D("tensor", lambda t, fc=fc, half=half: t.matmul(
                    mod_ps[half][0:1, :], silc[:, fc:fc + 1], wmt[:, 512 * half:512 * (half + 1)],
                    start=(fc == 0), stop=(fc == 7)), "t0" if half == 3 else None)
            if fc < 7:
                W("sync", "t0", em.c["t0"])
        c_modmm = em.c["t0"]
        W("vector", "t0", c_modmm)
        for half in range(4):
            DV(lambda v, half=half: v.tensor_copy(
                modrow[0:1, 512 * half:512 * (half + 1)], mod_ps[half][0:1, :]))
        cv1 = DV(lambda v: v.tensor_scalar(
            modrow[0:1, 1024:2048], modrow[0:1, 1024:2048], 1.0, 1.0, ALU.mult, ALU.add), "v0")
        W("tensor", "v0", cv1)
        for half in range(4):
            dst = sa_bc if half < 2 else capc_bc
            cbt = D("tensor", lambda t, half=half: t.matmul(
                psE[:], onesr[0:1, :], modrow[0:1, 512 * half:512 * (half + 1)],
                start=True, stop=True), "t0")
            W("vector", "t0", cbt)
            cvh = D("vector", lambda v, dst=dst, half=half: v.tensor_copy(
                dst[:, 512 * (half % 2):512 * (half % 2 + 1)], psE[:]), "v0")
            W("tensor", "v0", cvh)
        cbt = D("tensor", lambda t: t.matmul(psE[:, 0:256], onesr[0:1, :], bvr[0:1, :],
                                             start=True, stop=True), "t0")
        W("vector", "t0", cbt)
        cv_bv = D("vector", lambda v: v.tensor_copy(bv_bc[:], psE[:, 0:256]), "v0")
        W("vector", "ds", c_consts)
        cv_bq8 = DV(lambda v: v.tensor_scalar(bqf[:], bqf[:], 0.125, None, ALU.mult), "v0")

        if stop < 1:
            W("sync", "ds", em.c["ds"]); W("gpsimd", "dg", em.c["dg"])
            W("vector", "v0", em.c["v0"]); W("scalar", "s0", em.c["s0"])
            W("tensor", "t0", em.c["t0"])
            return
        tok_x, tok_hTM, tok_hT, tok_hfm = {}, {}, {}, {}
        tok_qk, tok_gev, tok_nrm, tok_nmm = {}, {}, {}, {}

        def xload(ti):
            return D("sync", lambda sp, ti=ti: sp.dma_start(
                out=xr[:, 1024 * (ti % 2):1024 * (ti % 2 + 1)],
                in_=xb[128 * ti:128 * (ti + 1), :]), "ds", 16)

        tok_x[0] = xload(0)
        tok_x[1] = xload(1)

        for tt in range(NP):
            for st in range(4):
                ti = 4 * tt + st
                xsl = xr[:, 1024 * (ti % 2):1024 * (ti % 2 + 1)]
                W("vector", "ds", tok_x[ti])
                if tt == 0 and st == 0:
                    W("vector", "v0", cv_bv)
                DV(lambda v, xsl=xsl: v.reduce_sum(lncol[:, 0:1], xsl, axis=AXX))
                W("scalar", "ds", tok_x[ti])
                if tok_nmm.get("last"):
                    W("scalar", "t0", tok_nmm["last"])
                csq = D("scalar", lambda s, xsl=xsl: s.activation(sqt[:], xsl, AF.Square, bias=zcol[:]), "s0")
                W("vector", "s0", csq)
                DV(lambda v: v.reduce_sum(lncol[:, 1:2], sqt[:], axis=AXX))
                DV(lambda v: v.tensor_scalar(lncol[:, 2:3], lncol[:, 0:1], 1.0 / E, None, ALU.mult))
                DV(lambda v: v.tensor_scalar(lncol[:, 3:4], lncol[:, 1:2], 1.0 / E, None, ALU.mult))
                DV(lambda v: v.tensor_tensor(lncol[:, 4:5], lncol[:, 2:3], lncol[:, 2:3], ALU.mult))
                DV(lambda v: v.tensor_scalar(lncol[:, 4:5], lncol[:, 3:4], lncol[:, 4:5], None, ALU.subtract))
                DV(lambda v: v.tensor_scalar(lncol[:, 4:5], lncol[:, 4:5], 1.0, EPS, ALU.mult, ALU.add))
                cvv = DV(lambda v: v.reciprocal(lncol[:, 5:6], lncol[:, 4:5]), "v0")
                W("scalar", "v0", cvv)
                cs2 = D("scalar", lambda s: s.activation(lncol[:, 6:7], lncol[:, 5:6], AF.Sqrt, bias=zcol[:]), "s0")
                W("vector", "s0", cs2)
                DV(lambda v: v.tensor_scalar(lncol[:, 7:8], lncol[:, 2:3], lncol[:, 6:7], None, ALU.mult))
                DV(lambda v: v.tensor_scalar(lncol[:, 7:8], lncol[:, 7:8], -1.0, None, ALU.mult))
                DV(lambda v, xsl=xsl: v.tensor_scalar(
                    tmp1k[:], xsl, lncol[:, 6:7], lncol[:, 7:8], ALU.mult, ALU.add))
                DV(lambda v: v.tensor_tensor(tmp1k[:], tmp1k[:], capc_bc[:], ALU.mult))
                tok_hTM[ti] = DV(lambda v, xsl=xsl: v.tensor_tensor(
                    xsl, tmp1k[:], sa_bc[:], ALU.add), "v0")

                W("tensor", "v0", tok_hTM[ti])
                for fc in range(8):
                    bank = psA if fc < 4 else psB
                    D("tensor", lambda t, xsl=xsl, fc=fc, bank=bank: t.transpose(
                        bank[:, 128 * (fc % 4):128 * (fc % 4 + 1)],
                        xsl.rearrange("p (fc q) -> p fc q", fc=8)[:, fc, :],
                        identt[:]), "t0" if fc in (3, 7) else None)
                tok_hT[ti] = em.c["t0"]
                W("vector", "t0", tok_hT[ti])
                hp3 = hp[:].rearrange("p (fc q) -> p fc q", fc=8)
                D("vector", lambda v, st=st, hp3=hp3: v.tensor_copy(
                    hp3[:, 0:4, 128 * st:128 * (st + 1)],
                    psA[:].rearrange("p (fc q) -> p fc q", fc=4)))
                tok_hfm[ti] = D("vector", lambda v, st=st, hp3=hp3: v.tensor_copy(
                    hp3[:, 4:8, 128 * st:128 * (st + 1)],
                    psB[:].rearrange("p (fc q) -> p fc q", fc=4)), "v0")
                if ti + 2 < 64:
                    W("sync", "t0", tok_hT[ti])
                    tok_x[ti + 2] = xload(ti + 2)

            W("vector", "v0", tok_hfm[4 * tt + 3])
            for fc in range(8):
                DV(lambda v, fc=fc: v.reduce_sum(
                    lncol[:, 8:9], hp[:, 512 * fc:512 * (fc + 1)], axis=AXX))
                DV(lambda v, fc=fc: v.tensor_scalar(
                    hsum[:, fc:fc + 1], hsum[:, fc:fc + 1], lncol[:, 8:9], None, ALU.add),
                   "v0" if fc == 7 else None)

            DV(lambda v, tt=tt: v.tensor_scalar(tofff[:], fcol[:], float(512 * tt), None, ALU.mult))
            for which in ("sin", "cos"):
                if which == "sin":
                    DV(lambda v: v.tensor_scalar(angt[:], tkit[:], fcol[:], tofff[:], ALU.mult, ALU.add))
                else:
                    W("vector", "s0", em.c["s0"])
                    DV(lambda v: v.tensor_scalar(angt[:], angt[:], PI / 2.0, None, ALU.add))
                DV(lambda v: v.tensor_scalar(rrt[:], angt[:], 1.0 / (2 * PI), None, ALU.mult))
                DV(lambda v: v.tensor_copy(ki32t[:], rrt[:]))
                DV(lambda v: v.tensor_copy(rrt[:], ki32t[:]))
                cvx = DV(lambda v: v.scalar_tensor_tensor(
                    rrt[:], rrt[:], -2 * PI, angt[:], ALU.mult, ALU.add), "v0")
                W("scalar", "v0", cvx)
                dstv = sint if which == "sin" else cost
                D("scalar", lambda s, dstv=dstv: s.activation(dstv[:], rrt[:], AF.Sin, bias=zcol[:]), "s0")
            tok_sc = em.c["s0"]

            for wnm, wtile, dsttile, bias_t, scale in (("q", wq_b, Qf, bqf, 0.125),
                                                       ("k", wk_b, Kf, bkf, 1.0)):
                for ch in range(2):
                    bank = psC if ch == 0 else psD
                    if wnm == "q" and ch == 0:
                        W("tensor", "v0", tok_hfm[4 * tt + 3])
                        if tt > 0:
                            W("tensor", "s0", tok_gev[(tt - 1, 1)])
                    for fc in range(8):
                        D("tensor", lambda t, wtile=wtile, fc=fc, ch=ch, bank=bank: t.matmul(
                            bank[:], wtile[:, 256 * fc + 128 * ch:256 * fc + 128 * (ch + 1)],
                            hp[:, 512 * fc:512 * (fc + 1)],
                            start=(fc == 0), stop=(fc == 7)), "t0" if fc == 7 else None)
                    tok_mm = em.c["t0"]
                    W("scalar", "t0", tok_mm)
                    if tt == 0 and wnm == "q" and ch == 0:
                        W("scalar", "v0", cv_bq8)
                    q0sl = q0r[:, 512 * ch:512 * (ch + 1)]
                    cse = D("scalar", lambda s, bank=bank, bias_t=bias_t, ch=ch, q0sl=q0sl, scale=scale:
                            s.activation(q0sl, bank[:], AF.Identity, bias=bias_t[:, ch:ch + 1], scale=scale), "s0")
                    prevn = tok_nrm.get(("last",))
                    if prevn:
                        W("tensor", "v0", prevn)
                    W("tensor", "s0", cse)
                    W("tensor", "s0", tok_sc)
                    cpm = D("tensor", lambda t, q0sl=q0sl: t.matmul(
                        psE[:], permt[:], q0sl, start=True, stop=True), "t0")
                    W("vector", "t0", cpm)
                    D("vector", lambda v: v.tensor_tensor(rtmp[:], psE[:], sint[:], ALU.mult))
                    DV(lambda v, q0sl=q0sl: v.tensor_tensor(q0sl, q0sl, cost[:], ALU.mult))
                    tok_qk[(tt, wnm, ch)] = DV(
                        lambda v, dsttile=dsttile, ch=ch, tt=tt, q0sl=q0sl:
                        v.tensor_tensor(dsttile[:, T * ch + 512 * tt: T * ch + 512 * (tt + 1)],
                                        rtmp[:], q0sl, ALU.add), "v0")
                    W("scalar", "v0", tok_qk[(tt, wnm, ch)])
                    csq = D("scalar", lambda s, dsttile=dsttile, ch=ch, tt=tt: s.activation(
                        sqt[:, 0:512], dsttile[:, T * ch + 512 * tt: T * ch + 512 * (tt + 1)],
                        AF.Square, bias=zcol[:]), "s0")
                    W("tensor", "s0", csq)
                    cmm = D("tensor", lambda t: t.matmul(psE[0:2, :], osplt[:], sqt[:, 0:512],
                                                         start=True, stop=True), "t0")
                    tok_nmm["last"] = cmm
                    W("vector", "t0", cmm)
                    DV(lambda v: v.reduce_max(n2t[0:2, 0:1], psE[0:2, :], axis=AXX))
                    dstn = n2q if wnm == "q" else n2k
                    tok_nrm[("last",)] = DV(lambda v, dstn=dstn, ch=ch: v.tensor_scalar(
                        dstn[0:2, ch:ch + 1], dstn[0:2, ch:ch + 1], n2t[0:2, 0:1], None, ALU.max), "v0")

            tok_vev = {}
            for st in range(4):
                bank = psC if st % 2 == 0 else psD
                if st >= 2:
                    W("tensor", "v0", tok_vev[st - 2])
                for fc in range(8):
                    D("tensor", lambda t, st=st, fc=fc, bank=bank: t.matmul(
                        bank[:, 0:256],
                        hp[:, 512 * fc + 128 * st:512 * fc + 128 * (st + 1)],
                        wv_b[:, 256 * fc:256 * (fc + 1)],
                        start=(fc == 0), stop=(fc == 7)), "t0" if fc == 7 else None)
                tok_mm = em.c["t0"]
                W("vector", "t0", tok_mm)
                kt = 4 * tt + st
                tok_vev[st] = D("vector", lambda v, bank=bank, kt=kt: v.tensor_tensor(
                    Vp[:].rearrange("p (kt2 h c) -> p kt2 h c", h=4, c=65)[:, kt, :, 0:64],
                    bank[:, 0:256].rearrange("p (h c) -> p h c", h=4),
                    bv_bc[:].rearrange("p (h c) -> p h c", h=4), ALU.add), "v0")

            for ch in range(2):
                bank = psC if ch == 0 else psD
                W("tensor", "v0", tok_vev[2 + ch])
                for fc in range(8):
                    D("tensor", lambda t, fc=fc, ch=ch, bank=bank: t.matmul(
                        bank[:], wg_b[:, 256 * fc + 128 * ch:256 * fc + 128 * (ch + 1)],
                        hp[:, 512 * fc:512 * (fc + 1)],
                        start=(fc == 0), stop=(fc == 7)), "t0" if fc == 7 else None)
                tok_mm = em.c["t0"]
                W("scalar", "t0", tok_mm)
                tok_gev[(tt, ch)] = D("scalar", lambda s, bank=bank, ch=ch, tt=tt: s.activation(
                    gf[:, T * ch + 512 * tt: T * ch + 512 * (tt + 1)], bank[:], AF.Silu,
                    bias=bgf[:, ch:ch + 1]), "s0")

        c_ph1_v, c_ph1_s, c_ph1_t = em.c["v0"], em.c["s0"], em.c["t0"]
        if dbg:
            W("gpsimd", "v0", c_ph1_v)
            D("gpsimd", lambda gp: gp.dma_start(out=dbg_o["t_h"], in_=hp[:, 0:512]), "dg", 16)

        if stop < 2:
            W("sync", "ds", em.c["ds"]); W("gpsimd", "dg", em.c["dg"])
            W("vector", "v0", em.c["v0"]); W("scalar", "s0", em.c["s0"])
            W("tensor", "t0", em.c["t0"])
            return
        DV(lambda v: v.tensor_scalar(hmean_b[:], hsum[:], 1.0 / T, None, ALU.mult), "v0")
        cv_hm = em.c["v0"]
        W("tensor", "v0", cv_hm)
        for fc in range(8):
            D("tensor", lambda t, fc=fc: t.matmul(
                psA[0:4, 0:1], wd_b[:, 4 * fc:4 * (fc + 1)], hmean_b[:, fc:fc + 1],
                start=(fc == 0), stop=(fc == 7)), "t0" if fc == 7 else None)
        c_pdmm = em.c["t0"]
        W("scalar", "t0", c_pdmm)
        cs_pd = DS(lambda s: s.activation(pdld[:, 0:1], psA[0:4, 0:1], AF.Sigmoid, bias=bdc[:]), "s0")
        cs_e2 = DS(lambda s: s.activation(pdld[:, 1:2], pdld[:, 0:1], AF.Exp,
                                          bias=c5l2[:], scale=-8.0 * LN2), "s0")
        W("vector", "s0", cs_e2)
        cv3 = DV(lambda v: v.tensor_scalar(pdld[:, 1:2], pdld[:, 1:2], -1.0, 1.0, ALU.mult, ALU.add), "v0")
        W("scalar", "v0", cv3)
        cs_ld = D("scalar", lambda s: s.activation(pdld[:, 2:3], pdld[:, 1:2], AF.Ln, bias=zc4[:]), "s0")
        W("vector", "s0", cs_ld)
        for ch in range(2):
            DV(lambda v, ch=ch: v.reduce_max(b2c[0:2, 0:1], n2q[0:2, ch:ch + 1], axis=AXX))
            DV(lambda v, ch=ch: v.reduce_max(b2c[0:2, 1:2], n2k[0:2, ch:ch + 1], axis=AXX))
            cvb = DV(lambda v: v.tensor_scalar(b2c[0:2, 2:3], b2c[0:2, 0:1], b2c[0:2, 1:2], None, ALU.mult), "v0")
            W("scalar", "v0", cvb)
            D("scalar", lambda s, ch=ch: s.activation(bnd2[0:2, ch:ch + 1], b2c[0:2, 2:3], AF.Sqrt,
                                                      bias=zcol[0:2, :]), "s0")
        cs_b = em.c["s0"]
        W("tensor", "s0", cs_b)
        prev_tok = None
        for h in range(NH):
            if prev_tok:
                W("tensor", "v0", prev_tok)
            cm1 = D("tensor", lambda t, h=h: t.matmul(psA[:, 0:1], eh4t[:, 128 * h:128 * (h + 1)],
                                                      pdld[:, 2:3], start=True, stop=True), "t0")
            cm2 = D("tensor", lambda t, h=h: t.matmul(psB[:, 0:1], eh4t[0:2, 128 * (h % 2):128 * (h % 2) + 128],
                                                      bnd2[0:2, h // 2:h // 2 + 1], start=True, stop=True), "t0")
            W("vector", "t0", cm2)
            DV(lambda v, h=h: v.tensor_copy(ldbc[:, h:h + 1], psA[:, 0:1]))
            DV(lambda v, h=h: v.tensor_scalar(nbbc[:, h:h + 1], psB[:, 0:1], -1.0, None, ALU.mult))
            prev_tok = DV(lambda v, h=h: v.tensor_scalar(
                ldkb[:, 64 * h:64 * (h + 1)], pk2dt[:], ldbc[:, h:h + 1], nbbc[:, h:h + 1],
                ALU.mult, ALU.add), "v0")
        c_ph2_v = em.c["v0"]

        if stop < 3:
            W("sync", "ds", em.c["ds"]); W("gpsimd", "dg", em.c["dg"])
            W("vector", "v0", em.c["v0"]); W("scalar", "s0", em.c["s0"])
            W("tensor", "t0", em.c["t0"])
            return
        W("sync", "t0", c_ph1_t)
        if dbg:
            W("sync", "dg", em.c["dg"])
        c_dr = D("sync", lambda sp: sp.dma_start(out=drt[:], in_=ic["drelu"]), "ds", 16)
        tok_P, tok_avt, tok_oge = {}, {}, {}
        it = -1
        for h in range(NH):
            ch, rw = h // 2, 64 * (h % 2)
            for qt in range(NP):
                it += 1
                O2 = psF if it % 2 == 0 else psG
                if NT - (4 * qt + 4) > 0:
                    W("vector", "v0", c_ph2_v)
                    DV(lambda v, h=h: v.tensor_scalar(negld[:], ldbc[:, h:h + 1], -1.0, None, ALU.mult))
                    DV(lambda v, qt=qt: v.tensor_scalar(nlq[:], negld[:], float(512 * qt), None, ALU.mult))
                    DV(lambda v: v.tensor_scalar(
                        ldqt[:], tkit[:], negld[:], nlq[:], ALU.mult, ALU.add), "v0")
                for kb in range(NT):
                    bank = psA if kb % 2 == 0 else psB
                    if kb >= 2:
                        W("tensor", "s0", tok_P[(h, qt, kb - 2)])
                    elif it > 0:
                        W("tensor", "s0", tok_P[divmod_prev(h, qt, NT - 2 + kb)])
                    else:
                        W("tensor", "v0", c_ph2_v)
                        W("tensor", "ds", c_dr)
                    tok_mm = D("tensor", lambda t, ch=ch, rw=rw, kb=kb, qt=qt, bank=bank: t.matmul(
                        bank[:], Kf[rw:rw + 64, T * ch + 128 * kb: T * ch + 128 * (kb + 1)],
                        Qf[rw:rw + 64, T * ch + 512 * qt: T * ch + 512 * (qt + 1)],
                        start=True, stop=True), "t0")
                    kind = "past" if kb < 4 * qt else ("diag" if kb < 4 * qt + 4 else "fut")
                    if kind == "fut":
                        W("vector", "t0", tok_mm)
                        cvf = D("vector", lambda v, bank=bank: v.tensor_tensor(
                            bank[:], bank[:], ldqt[:], ALU.add), "v0")
                        W("scalar", "v0", cvf)
                        bias = ldkb[:, 64 * h + kb:64 * h + kb + 1]
                    elif kind == "diag":
                        W("vector", "t0", tok_mm)
                        ci = kb - 4 * qt
                        cvf = D("vector", lambda v, bank=bank, ci=ci, h=h: v.scalar_tensor_tensor(
                            bank[:], drt[:, 512 * ci:512 * (ci + 1)], ldbc[:, h:h + 1], bank[:],
                            ALU.mult, ALU.add), "v0")
                        W("scalar", "v0", cvf)
                        bias = nbbc[:, h:h + 1]
                    else:
                        W("scalar", "t0", tok_mm)
                        bias = nbbc[:, h:h + 1]
                    if kb >= 3:
                        W("scalar", "t0", tok_avt[(h, qt, kb - 3)])
                    elif it > 0:
                        W("scalar", "t0", tok_avt[divmod_prev(h, qt, NT - 3 + kb)])
                    psl = Pr[:, 512 * (kb % 3):512 * (kb % 3 + 1)]
                    tok_P[(h, qt, kb)] = D("scalar", lambda s, bank=bank, psl=psl, bias=bias:
                                           s.activation(psl, bank[:], AF.Exp, bias=bias), "s0")
                    W("tensor", "s0", tok_P[(h, qt, kb)])
                    if kb == 0 and it >= 2:
                        W("tensor", "v0", tok_oge[it - 2])
                    tok_avt[(h, qt, kb)] = D("tensor", lambda t, kb=kb, h=h, psl=psl, O2=O2: t.matmul(
                        O2[:], Vp[:, 260 * kb + 65 * h: 260 * kb + 65 * (h + 1)], psl,
                        start=(kb == 0), stop=(kb == NT - 1)), "t0")
                W("vector", "t0", tok_avt[(h, qt, NT - 1)])
                cvr = D("vector", lambda v, O2=O2: v.reciprocal(rrow[0:1, :], O2[64:65, :]), "v0")
                W("tensor", "v0", cvr)
                cmm = D("tensor", lambda t: t.matmul(psH[:], onesr[0:1, 0:64], rrow[0:1, :],
                                                     start=True, stop=True), "t0")
                W("vector", "t0", cmm)
                DV(lambda v: v.tensor_copy(rbct[:, :], psH[0:64, :]))
                ogdst = (og[0:64, T * ch + 512 * qt: T * ch + 512 * (qt + 1)] if rw == 0
                         else ogscr[:, :])
                cvo = DV(lambda v, O2=O2, ogdst=ogdst: v.tensor_tensor(
                    ogdst, O2[0:64, :], rbct[:, :], ALU.mult), "v0")
                if rw != 0:
                    W("gpsimd", "v0", cvo)
                    D("gpsimd", lambda gp, ch=ch, qt=qt: gp.dma_start(
                        out=og[64:128, T * ch + 512 * qt: T * ch + 512 * (qt + 1)],
                        in_=ogscr[:, :]), "dg", 16)
                DV(lambda v, ogdst=ogdst: v.reduce_sum(stcol[:, 0:1], ogdst, axis=AXX))
                DV(lambda v, h=h: v.tensor_scalar(s1a[0:64, h:h + 1], s1a[0:64, h:h + 1],
                                                  stcol[:, 0:1], None, ALU.add))
                W("scalar", "v0", cvo)
                csq = D("scalar", lambda s, ogdst=ogdst: s.activation(
                    sqt[0:64, 0:512], ogdst, AF.Square, bias=zcol[0:64, :]), "s0")
                W("vector", "s0", csq)
                DV(lambda v: v.reduce_sum(stcol[:, 1:2], sqt[0:64, 0:512], axis=AXX))
                tok_oge[it] = DV(lambda v, h=h: v.tensor_scalar(
                    s2a[0:64, h:h + 1], s2a[0:64, h:h + 1], stcol[:, 1:2], None, ALU.add), "v0")

        c_ph3_v, c_ph3_t, c_ph3_dg, c_ph3_s = em.c["v0"], em.c["t0"], em.c["dg"], em.c["s0"]

        if dbg:
            W("gpsimd", "v0", c_ph3_v)
            c_tap = D("gpsimd", lambda gp: gp.dma_start(out=dbg_o["t_og"], in_=og[:, 0:512]), "dg", 16)
            D("gpsimd", lambda gp: gp.dma_start(out=dbg_o["t_q"], in_=Qf[:, 0:512]), "dg", 16)
            D("gpsimd", lambda gp: gp.dma_start(out=dbg_o["t_k"], in_=Kf[:, 0:512]), "dg", 16)
            D("gpsimd", lambda gp: gp.dma_start(out=dbg_o["t_v"], in_=Vp[:, 0:260]), "dg", 16)
            D("gpsimd", lambda gp: gp.dma_start(out=dbg_o["t_g"], in_=gf[:, 0:512]), "dg", 16)
        else:
            c_tap = None

        if stop < 4:
            W("sync", "ds", em.c["ds"]); W("gpsimd", "dg", em.c["dg"])
            W("vector", "v0", em.c["v0"]); W("scalar", "s0", em.c["s0"])
            W("tensor", "t0", em.c["t0"])
            return
        prev_tok = None
        tok_gn = {}
        for h in range(NH):
            ch, rw = h // 2, 64 * (h % 2)
            W("tensor", "v0", c_ph3_v if prev_tok is None else prev_tok)
            cm1 = D("tensor", lambda t, h=h: t.matmul(psA[0:1, 0:1], onesc[0:64, :], s1a[0:64, h:h + 1],
                                                      start=True, stop=True), "t0")
            cm2 = D("tensor", lambda t, h=h: t.matmul(psA[0:1, 1:2], onesc[0:64, :], s2a[0:64, h:h + 1],
                                                      start=True, stop=True), "t0")
            W("vector", "t0", cm2)
            NN = float(T * HD)
            DV(lambda v: v.tensor_scalar(fincol[0:1, 0:1], psA[0:1, 0:1], 1.0 / NN, None, ALU.mult))
            DV(lambda v: v.tensor_scalar(fincol[0:1, 1:2], psA[0:1, 1:2], 1.0 / NN, None, ALU.mult))
            DV(lambda v: v.tensor_tensor(fincol[0:1, 2:3], fincol[0:1, 0:1], fincol[0:1, 0:1], ALU.mult))
            DV(lambda v: v.tensor_scalar(fincol[0:1, 2:3], fincol[0:1, 1:2], fincol[0:1, 2:3], None, ALU.subtract))
            DV(lambda v: v.tensor_scalar(fincol[0:1, 2:3], fincol[0:1, 2:3], 1.0, EPS, ALU.mult, ALU.add))
            cvf = DV(lambda v: v.reciprocal(fincol[0:1, 3:4], fincol[0:1, 2:3]), "v0")
            W("scalar", "v0", cvf)
            csA = D("scalar", lambda s: s.activation(fincol[0:1, 4:5], fincol[0:1, 3:4], AF.Sqrt,
                                                     bias=zcol[0:1, :]), "s0")
            W("vector", "s0", csA)
            DV(lambda v: v.tensor_scalar(fincol[0:1, 5:6], fincol[0:1, 0:1], fincol[0:1, 4:5], None, ALU.mult))
            cvB = DV(lambda v: v.tensor_scalar(fincol[0:1, 5:6], fincol[0:1, 5:6], -1.0, None, ALU.mult), "v0")
            W("tensor", "v0", cvB)
            cb1 = D("tensor", lambda t: t.matmul(psA[:, 2:3], onesr[0:1, :], fincol[0:1, 4:5],
                                                 start=True, stop=True), "t0")
            cb2 = D("tensor", lambda t: t.matmul(psB[:, 2:3], onesr[0:1, :], fincol[0:1, 5:6],
                                                 start=True, stop=True), "t0")
            W("vector", "t0", cb2)
            DV(lambda v, h=h: v.tensor_copy(abcol[:, h:h + 1], psA[:, 2:3]))
            prev_tok = DV(lambda v, h=h: v.tensor_copy(abcol[:, 4 + h:5 + h], psB[:, 2:3]), "v0")
            if rw != 0:
                W("scalar", "dg", c_ph3_dg)
            if dbg and h == 0:
                W("scalar", "dg", c_tap)
            W("scalar", "v0", prev_tok)
            for tq in range(NP):
                if tq > 0:
                    W("scalar", "v0", tok_gn[(h, tq - 1)])
                elif h > 0:
                    W("scalar", "v0", tok_gn[(h - 1, NP - 1)])
                sl = (slice(rw, rw + 64), slice(T * ch + 512 * tq, T * ch + 512 * (tq + 1)))
                csn = D("scalar", lambda s, sl=sl, h=h, rw=rw: s.activation(
                    ont[rw:rw + 64, :], og[sl], AF.Identity,
                    bias=abcol[rw:rw + 64, 4 + h:5 + h], scale=abcol[rw:rw + 64, h:h + 1]), "s0")
                W("vector", "s0", csn)
                DV(lambda v, h=h, rw=rw: v.tensor_scalar(
                    ont[rw:rw + 64, :], ont[rw:rw + 64, :], gampk[rw:rw + 64, h // 2:h // 2 + 1],
                    betpk[rw:rw + 64, h // 2:h // 2 + 1], ALU.mult, ALU.add))
                tok_gn[(h, tq)] = DV(lambda v, sl=sl, h=h, rw=rw, ch=ch, tq=tq: v.tensor_tensor(
                    og[sl], ont[rw:rw + 64, :], gf[sl], ALU.mult), "v0")
        c_gn = em.c["v0"]
        if stop < 5:
            W("sync", "ds", em.c["ds"]); W("gpsimd", "dg", em.c["dg"])
            W("vector", "v0", em.c["v0"]); W("scalar", "s0", em.c["s0"])
            W("tensor", "t0", em.c["t0"])
            return

        tok_zev = {}
        for ti in range(NT):
            for oh in range(2):
                bank = psA if oh == 0 else psB
                if ti >= 1:
                    if oh == 0:
                        W("tensor", "v0", tok_zev[ti - 1])
                    if ti >= 2 and oh == 0:
                        W("vector", "dz", 16 * (ti - 1))
                elif ti == 0 and oh == 0:
                    W("tensor", "v0", c_gn)
                for ch in range(2):
                    D("tensor", lambda t, ti=ti, oh=oh, ch=ch, bank=bank: t.matmul(
                        bank[:],
                        og[:, T * ch + 128 * ti: T * ch + 128 * (ti + 1)],
                        wo_b[:, 1024 * ch + 512 * oh: 1024 * ch + 512 * (oh + 1)],
                        start=(ch == 0), stop=(ch == 1)), "t0" if ch == 1 else None)
                cmm = em.c["t0"]
                W("vector", "t0", cmm)
                D("vector", lambda v, ti=ti, oh=oh, bank=bank: v.tensor_copy(
                    zev[:, 1024 * (ti % 2) + 512 * oh: 1024 * (ti % 2) + 512 * (oh + 1)], bank[:]),
                  "v0" if oh == 1 else None)
            tok_zev[ti] = em.c["v0"]
            W("sync", "v0", tok_zev[ti])
            D("sync", lambda sp, ti=ti: sp.dma_start(
                out=zout[128 * ti:128 * (ti + 1), :],
                in_=zev[:, 1024 * (ti % 2):1024 * (ti % 2 + 1)]), "dz", 16)

        if stop < 6:
            W("sync", "ds", em.c["ds"]); W("sync", "dz", em.c["dz"]); W("gpsimd", "dg", em.c["dg"])
            W("vector", "v0", em.c["v0"]); W("scalar", "s0", em.c["s0"])
            W("tensor", "t0", em.c["t0"])
            return
        if dbg:
            W("vector", "v0", c_gn)
            D("vector", lambda v: v.tensor_copy(dbm[:, 0:4], ldbc[:]))
            D("vector", lambda v: v.tensor_copy(dbm[:, 4:8], nbbc[:]))
            cvm = D("vector", lambda v: v.tensor_copy(dbm[:, 8:16], abcol[:]), "v0")
            W("sync", "v0", cvm)
            D("sync", lambda sp: sp.dma_start(out=dbg_o["t_m"], in_=dbm[:]), "dz", 16)

        W("sync", "dz", em.c["dz"])
        W("gpsimd", "dg", em.c["dg"])

    def divmod_prev(h, qt, kb):
        if qt > 0:
            return (h, qt - 1, kb)
        return (h - 1, NP - 1, kb)

    with nc.Block() as block:
        block.sync(lambda e: program(Em(nc, sems, "sync", e)))
        block.gpsimd(lambda e: program(Em(nc, sems, "gpsimd", e)))
        block.tensor(lambda e: program(Em(nc, sems, "tensor", e)))
        block.vector(lambda e: program(Em(nc, sems, "vector", e)))
        block.scalar(lambda e: program(Em(nc, sems, "scalar", e)))

    return nc


def l1_inputs(core, ins):
    b, g = core // 4, core % 4
    gs = slice(256 * g, 256 * (g + 1))
    hs = slice(4 * g, 4 * (g + 1))
    f32 = np.float32
    wq = np.asarray(ins["L0_attn_wq"], f32); wk = np.asarray(ins["L0_attn_wk"], f32)
    wv = np.asarray(ins["L0_attn_wv"], f32); wg = np.asarray(ins["L0_attn_wg"], f32)
    wo = np.asarray(ins["L0_attn_wo"], f32); wd = np.asarray(ins["L0_attn_wd"], f32)
    woT = np.ascontiguousarray(wo.T[gs, :])
    woTb = np.zeros((128, 2048), f32)
    for hh in range(4):
        woTb[64 * (hh % 2):64 * (hh % 2) + 64, 1024 * (hh // 2):1024 * (hh // 2) + 1024] = \
            woT[64 * hh:64 * (hh + 1), :]
    def fmaj(vec):
        return np.ascontiguousarray(np.asarray(vec, f32)[gs].reshape(2, 128).T)
    out = {
        "xb": np.ascontiguousarray(np.asarray(ins["L0_attn_x"], f32)[b]),
        "wqT": np.ascontiguousarray(wq.T[:, gs]),
        "wkT": np.ascontiguousarray(wk.T[:, gs]),
        "wvT": np.ascontiguousarray(wv.T[:, gs]),
        "wgT": np.ascontiguousarray(wg.T[:, gs]),
        "woTb": woTb,
        "wdT": np.ascontiguousarray(wd.T[:, hs]),
        "bq_f": fmaj(ins["L0_attn_bq"]),
        "bk_f": fmaj(ins["L0_attn_bk"]),
        "bg_f": fmaj(ins["L0_attn_bg"]),
        "bv_row": np.ascontiguousarray(np.asarray(ins["L0_attn_bv"], f32)[gs].reshape(1, 256)),
        "bd_col": np.ascontiguousarray(np.asarray(ins["L0_attn_bd"], f32)[hs].reshape(4, 1)),
        "gam_pk": fmaj(ins["L0_attn_gamma"]),
        "bet_pk": fmaj(ins["L0_attn_beta"]),
        "c_fmaj": np.ascontiguousarray(np.asarray(ins["L0_c"], f32)[b].reshape(8, 128).T),
        "wm_sc": np.ascontiguousarray(np.asarray(ins["L0_wmod"], f32)[:, 0:2048]),
    }
    return out


TS = 2048
E = 1024
HID = 2048
NE = 8
NSH = 2
EPS = 1e-5
KB = 1024
NTT = TS // 128


def build_l2(dbg=False, stop=99):
    nc = bass.Bass("TRN2", target_bir_lowering=False)
    CN = consts_np()

    def din(name, shape):
        return nc.dram_tensor(name, list(shape), F32, kind="ExternalInput").ap()

    zp4 = din("zp4", (4, TS, E))
    x_sl = din("x_sl", (TS, E))
    c_fmaj = din("c_fmaj", (128, 8))
    wm2 = din("wm2", (E, 4096))
    bo_row = din("bo_row", (1, E))
    routerT = din("routerT", (E, 8))
    eb_row = din("eb_row", (1, 8))
    w1T = din("w1T", (NE, E, HID))
    w3T = din("w3T", (NE, E, HID))
    w2T = din("w2T", (NE, HID, E))
    sw1T = din("sw1T", (NSH, E, HID))
    sw3T = din("sw3T", (NSH, E, HID))
    sw2T = din("sw2T", (NSH, HID, E))

    oout = nc.dram_tensor("oslice", [TS, E], F32, kind="ExternalOutput").ap()
    x1d = nc.dram_tensor("x1d", [TS, E], F32, kind="Internal").ap()
    dbg_o = {}
    if dbg:
        for nm, sh in [("d_h2", (128, 512)), ("d_cmb", (128, 16)), ("d_mod", (128, 16)),
                       ("d_x1", (128, 1024)), ("d_y", (128, 1024)), ("d_gb", (128, 1024)),
                       ("d_lg", (128, 16)), ("d_o0", (128, 1024)), ("d_xr0", (128, 1024))]:
            dbg_o[nm] = nc.dram_tensor(nm, list(sh), F32, kind="ExternalOutput").ap()

    ic = {k: nc.inline_tensor(np.ascontiguousarray(v), f"c2_{k}").ap()
          for k, v in CN.items() if k in ("ident", "ones_row")}

    A = lambda name, shape, dtype: nc.alloc_sbuf_tensor(name, list(shape), dtype, side="left")

    h2 = A("h2", (128, 8 * TS), BF16)
    yac = A("yac", (128, NTT * E), F32)
    Au = A("Au", (128, 16 * TS), BF16)
    cmb = A("cmb", (128, NTT * 8), F32)
    gbb = A("gbb", (128, E), F32)
    identt = A("identt", (128, 128), F32)
    onesr = A("onesr", (1, 128), F32)
    cfm = A("cfm", (128, 8), F32)
    silc = A("silc", (128, 8), F32)
    rt_b = A("rt_b", (128, 64), BF16)
    ebb = A("ebb", (128, 8), F32)
    lncol = A("lncol", (128, 16), F32)
    zcol = A("zcol", (128, 1), F32)
    ltile = A("ltile", (128, 8), F32)
    etile = A("etile", (128, 8), F32)
    ktile = A("ktile", (128, 8), F32)
    mcols = A("mcols", (128, 4), F32)
    dlg_t = A("dlg_t", (128, 16), F32)

    ARENA = 40 * KB
    ar0, _ = nc.bump_sbuf(ARENA, side="left")
    def AT(name, shape, dtype, off):
        return nc.alloc_sbuf_tensor_at(name, list(shape), dtype, offset=ar0 + off)
    wmt = AT("wmt", (128, 2048), F32, 0)
    modrow = AT("modrow", (1, 2048), F32, 8 * KB)
    gab = AT("gab", (128, E), F32, 16 * KB)
    bob = AT("bob", (128, E), F32, 20 * KB)
    capcb = AT("capcb", (128, E), F32, 24 * KB)
    sbb = AT("sbb", (128, E), F32, 28 * KB)
    zst = AT("zst", (128, 4096), F32, 0)
    xt1 = AT("xt1", (128, 1024), F32, 32 * KB)
    sqt = AT("sqt", (128, 1024), F32, 36 * KB)
    w2f = AT("w2f", (128, 16 * 1024), BF16, 0)
    w1r = AT("w1r", (128, 2 * 1024), BF16, 32 * KB)
    w3r = AT("w3r", (128, 2 * 1024), BF16, 36 * KB)
    outr = AT("outr", (128, 2048), F32, 0)
    x1rd = AT("x1rd", (128, 2048), F32, 8 * KB)

    psA = nc.alloc_psum_tensor("psA", [128, 512], F32)
    psB = nc.alloc_psum_tensor("psB", [128, 512], F32)
    psC = nc.alloc_psum_tensor("psC", [128, 512], F32)
    psD = nc.alloc_psum_tensor("psD", [128, 512], F32)
    psE = nc.alloc_psum_tensor("psE", [128, 8], F32)

    sems = {k: nc.alloc_semaphore(k) for k in ["ds", "dg", "dz", "t0", "v0", "s0", "vv", "sv2"]}

    def program(em: Em):
        W, D = em.wait, em.do

        def DV(fn, inc=None):
            sem = inc or "vv"
            tok = D("vector", fn, sem)
            W("vector", sem, tok)
            return tok

        def DS(fn, inc=None):
            sem = inc or "sv2"
            tok = D("scalar", fn, sem)
            W("scalar", sem, tok)
            return tok

        for nm, tile in [("ident", identt), ("ones_row", onesr)]:
            D("sync", lambda sp, nm=nm, tile=tile: sp.dma_start(out=tile[:], in_=ic[nm]), "ds", 16)
        D("sync", lambda sp: sp.dma_start(out=cfm[:], in_=c_fmaj), "ds", 16)
        c_consts = em.c["ds"]
        D("gpsimd", lambda gp: gp.dma_start(
            out=rt_b[:].rearrange("p (fc o) -> p fc o", fc=8),
            in_=routerT.rearrange("(fc p) o -> p fc o", p=128)), "dg", 16)
        c_rt = em.c["dg"]

        D("vector", lambda v: v.memset(zcol[:], 0.0))
        D("vector", lambda v: v.memset(yac[:], 0.0), "v0")
        cv_y0 = em.c["v0"]
        W("scalar", "ds", c_consts)
        cs_sil = D("scalar", lambda s: s.activation(silc[:], cfm[:], AF.Silu, bias=zcol[:]), "s0")

        mod_ps = [psA, psB, psC, psD]
        cv_h = None
        for rnd in range(2):
            for fc in range(8):
                cw = D("sync", lambda sp, rnd=rnd, fc=fc: sp.dma_start(
                    out=wmt[:], in_=wm2[128 * fc:128 * (fc + 1), 2048 * rnd:2048 * (rnd + 1)]), "ds", 16)
                W("tensor", "ds", cw)
                if rnd == 0 and fc == 0:
                    W("tensor", "s0", cs_sil)
                for half in range(4):
                    D("tensor", lambda t, fc=fc, half=half: t.matmul(
                        mod_ps[half][0:1, :], silc[:, fc:fc + 1], wmt[:, 512 * half:512 * (half + 1)],
                        start=(fc == 0), stop=(fc == 7)), "t0" if half == 3 else None)
                if fc < 7:
                    W("sync", "t0", em.c["t0"])
            cmod = em.c["t0"]
            W("vector", "t0", cmod)
            for half in range(4):
                DV(lambda v, half=half: v.tensor_copy(
                    modrow[0:1, 512 * half:512 * (half + 1)], mod_ps[half][0:1, :]))
            clamp_base = 0 if rnd == 0 else 1024
            scr1 = gab[0:1, :] if rnd == 0 else bob[0:1, :]
            scr2 = bob[0:1, :] if rnd == 0 else capcb[0:1, :]
            scr3 = capcb[0:1, :] if rnd == 0 else gbb[0:1, :]
            DV(lambda v, b=clamp_base, scr1=scr1: v.tensor_scalar(
                scr1, modrow[0:1, b:b + E], 0.25, None, ALU.max))
            DV(lambda v, b=clamp_base, scr2=scr2: v.tensor_scalar(
                scr2, modrow[0:1, b:b + E], -0.25, None, ALU.min))
            DV(lambda v, b=clamp_base, scr3=scr3: v.tensor_scalar(
                scr3, modrow[0:1, b:b + E], 0.0, None, ALU.is_ge))
            DV(lambda v, scr1=scr1, scr2=scr2: v.tensor_tensor(scr1, scr1, scr2, ALU.subtract))
            DV(lambda v, scr1=scr1, scr3=scr3: v.tensor_tensor(scr1, scr3, scr1, ALU.mult))
            DV(lambda v, b=clamp_base, scr1=scr1, scr2=scr2: v.tensor_tensor(
                modrow[0:1, b:b + E], scr2, scr1, ALU.add))
            if rnd == 1:
                DV(lambda v: v.tensor_scalar(
                    modrow[0:1, 0:E], modrow[0:1, 0:E], 1.0, 1.0, ALU.mult, ALU.add))
            cv_mod = DV(lambda v: v.tensor_copy(lncol[0:1, 1:2], modrow[0:1, 0:1]), "v0")
            W("tensor", "v0", cv_mod)
            dsts = ((gab, 0), (sbb, 1024)) if rnd == 0 else ((capcb, 0), (gbb, 1024))
            for dst, lo in dsts:
                for half in range(2):
                    cbt = D("tensor", lambda t, lo=lo, half=half: t.matmul(
                        psA[:], onesr[0:1, :], modrow[0:1, lo + 512 * half:lo + 512 * (half + 1)],
                        start=True, stop=True), "t0")
                    W("vector", "t0", cbt)
                    cvh = DV(lambda v, dst=dst, half=half: v.tensor_copy(
                        dst[:, 512 * half:512 * (half + 1)], psA[:]), "v0")
                    W("tensor", "v0", cvh)
            if rnd == 0:
                W("sync", "v0", em.c["v0"])

        W("sync", "v0", em.c["v0"])
        cbo = D("sync", lambda sp: sp.dma_start(out=modrow[0:1, 0:E], in_=bo_row), "ds", 16)
        ceb = D("sync", lambda sp: sp.dma_start(out=modrow[0:1, E:E + 8], in_=eb_row), "ds", 16)
        W("tensor", "ds", ceb)
        for dst, src, n in ((bob, modrow[0:1, 0:E], 2), (ebb, modrow[0:1, E:E + 8], 1)):
            for half in range(n):
                wsl = 512 * half
                wid = 512 if dst is bob else 8
                cbt = D("tensor", lambda t, src=src, wsl=wsl, wid=wid: t.matmul(
                    psA[:, 0:wid], onesr[0:1, :], src[0:1, wsl:wsl + wid],
                    start=True, stop=True), "t0")
                W("vector", "t0", cbt)
                cvh = DV(lambda v, dst=dst, wsl=wsl, wid=wid: v.tensor_copy(
                    dst[:, wsl:wsl + wid], psA[:, 0:wid]), "v0")
                W("tensor", "v0", cvh)
        c_s0v = em.c["v0"]
        c_s0t = em.c["t0"]

        tok_z, tok_x = {}, {}
        tok_h2 = {}
        tok_nmm = {}

        def zload(ti):
            toks = []
            for j in range(4):
                toks.append(D("sync", lambda sp, ti=ti, j=j: sp.dma_start(
                    out=zst[:, 1024 * j:1024 * (j + 1)],
                    in_=zp4[j, 128 * ti:128 * (ti + 1), :]), "ds", 16))
            toks.append(D("sync", lambda sp, ti=ti: sp.dma_start(
                out=xt1[:], in_=x_sl[128 * ti:128 * (ti + 1), :]), "ds", 16))
            return toks[-1]

        tok_h2tm = {}
        tok_lt0 = {}
        for ti in range(NTT):
            if ti > 0:
                W("sync", "v0", tok_h2tm[ti - 1])
                W("sync", "dz", 16 * ti)
            else:
                W("sync", "t0", c_s0t)
            tok_x[ti] = zload(ti)
            W("vector", "ds", tok_x[ti])
            if ti == 0:
                W("vector", "v0", c_s0v)
            DV(lambda v: v.tensor_tensor(sqt[:], zst[:, 0:1024], zst[:, 1024:2048], ALU.add))
            DV(lambda v: v.tensor_tensor(zst[:, 0:1024], zst[:, 2048:3072], zst[:, 3072:4096], ALU.add))
            DV(lambda v: v.tensor_tensor(sqt[:], sqt[:], zst[:, 0:1024], ALU.add))
            DV(lambda v: v.tensor_tensor(sqt[:], sqt[:], bob[:], ALU.add))
            DV(lambda v: v.tensor_tensor(sqt[:], sqt[:], gab[:], ALU.mult))
            cx1 = DV(lambda v: v.tensor_tensor(xt1[:], xt1[:], sqt[:], ALU.add), "v0")
            W("sync", "v0", cx1)
            D("sync", lambda sp, ti=ti: sp.dma_start(
                out=x1d[128 * ti:128 * (ti + 1), :], in_=xt1[:]), "dz", 16)
            DV(lambda v: v.reduce_sum(lncol[:, 0:1], xt1[:], axis=AXX))
            if tok_nmm.get("last"):
                W("scalar", "t0", tok_nmm["last"])
            W("scalar", "v0", cx1)
            csq = D("scalar", lambda s: s.activation(sqt[:], xt1[:], AF.Square, bias=zcol[:]), "s0")
            W("vector", "s0", csq)
            DV(lambda v: v.reduce_sum(lncol[:, 1:2], sqt[:], axis=AXX))
            DV(lambda v: v.tensor_scalar(lncol[:, 2:3], lncol[:, 0:1], 1.0 / E, None, ALU.mult))
            DV(lambda v: v.tensor_scalar(lncol[:, 3:4], lncol[:, 1:2], 1.0 / E, None, ALU.mult))
            DV(lambda v: v.tensor_tensor(lncol[:, 4:5], lncol[:, 2:3], lncol[:, 2:3], ALU.mult))
            DV(lambda v: v.tensor_scalar(lncol[:, 4:5], lncol[:, 3:4], lncol[:, 4:5], None, ALU.subtract))
            DV(lambda v: v.tensor_scalar(lncol[:, 4:5], lncol[:, 4:5], 1.0, EPS, ALU.mult, ALU.add))
            cvv = DV(lambda v: v.reciprocal(lncol[:, 5:6], lncol[:, 4:5]), "v0")
            W("scalar", "v0", cvv)
            cs2 = D("scalar", lambda s: s.activation(lncol[:, 6:7], lncol[:, 5:6], AF.Sqrt, bias=zcol[:]), "s0")
            W("vector", "s0", cs2)
            DV(lambda v: v.tensor_scalar(lncol[:, 7:8], lncol[:, 2:3], lncol[:, 6:7], None, ALU.mult))
            DV(lambda v: v.tensor_scalar(lncol[:, 7:8], lncol[:, 7:8], -1.0, None, ALU.mult))
            DV(lambda v: v.tensor_scalar(sqt[:], xt1[:], lncol[:, 6:7], lncol[:, 7:8], ALU.mult, ALU.add))
            DV(lambda v: v.tensor_tensor(sqt[:], sqt[:], capcb[:], ALU.mult))
            tok_h2tm[ti] = DV(lambda v: v.tensor_tensor(sqt[:], sqt[:], sbb[:], ALU.add), "v0")
            W("tensor", "v0", tok_h2tm[ti])
            if ti > 0:
                W("tensor", "v0", tok_lt0[ti - 1])
            for fc in range(8):
                bank = psA if fc < 4 else psB
                D("tensor", lambda t, fc=fc, bank=bank: t.transpose(
                    bank[:, 128 * (fc % 4):128 * (fc % 4 + 1)],
                    sqt[:].rearrange("p (fc q) -> p fc q", fc=8)[:, fc, :],
                    identt[:]), "t0" if fc in (3, 7) else None)
            tok_tr = em.c["t0"]
            W("vector", "t0", tok_tr)
            h23 = h2[:].rearrange("p (fc t) -> p fc t", fc=8)
            D("vector", lambda v, ti=ti, h23=h23: v.tensor_copy(
                h23[:, 0:4, 128 * ti:128 * (ti + 1)],
                psA[:].rearrange("p (fc q) -> p fc q", fc=4)))
            tok_h2[ti] = D("vector", lambda v, ti=ti, h23=h23: v.tensor_copy(
                h23[:, 4:8, 128 * ti:128 * (ti + 1)],
                psB[:].rearrange("p (fc q) -> p fc q", fc=4)), "v0")
            W("tensor", "v0", tok_h2[ti])
            if ti == 0:
                W("tensor", "dg", c_rt)
            for fc in range(8):
                D("tensor", lambda t, ti=ti, fc=fc: t.matmul(
                    psE[:], h2[:, TS * fc + 128 * ti: TS * fc + 128 * (ti + 1)],
                    rt_b[:, 8 * fc:8 * (fc + 1)],
                    start=(fc == 0), stop=(fc == 7)), "t0" if fc == 7 else None)
            tok_nmm["last"] = em.c["t0"]
            W("vector", "t0", tok_nmm["last"])
            tok_lt = DV(lambda v: v.tensor_tensor(ltile[:], psE[:], ebb[:], ALU.add), "v0")
            DV(lambda v: v.reduce_max(mcols[:, 0:1], ltile[:], axis=AXX))
            DV(lambda v: v.tensor_scalar(mcols[:, 1:2], mcols[:, 0:1], -1.0, None, ALU.mult))
            DV(lambda v: v.tensor_scalar(ktile[:], ltile[:], mcols[:, 0:1], None, ALU.is_ge))
            DV(lambda v: v.scalar_tensor_tensor(etile[:], ktile[:], -1e30, ltile[:], ALU.mult, ALU.add))
            DV(lambda v: v.reduce_max(mcols[:, 2:3], etile[:], axis=AXX))
            cvk = DV(lambda v: v.tensor_scalar(ktile[:], ltile[:], mcols[:, 2:3], None, ALU.is_ge), "v0")
            W("scalar", "v0", cvk)
            cse = DS(lambda s: s.activation(etile[:], ltile[:], AF.Exp, bias=mcols[:, 1:2]), "s0")
            W("vector", "s0", cse)
            DV(lambda v: v.tensor_tensor(etile[:], etile[:], ktile[:], ALU.mult))
            DV(lambda v: v.reduce_sum(mcols[:, 3:4], etile[:], axis=AXX))
            DV(lambda v: v.reciprocal(mcols[:, 1:2], mcols[:, 3:4]))
            tok_lt0[ti] = tok_lt
            if dbg and ti == 0:
                DV(lambda v: v.tensor_copy(dlg_t[:, 0:8], ltile[:]))
                DV(lambda v: v.tensor_copy(dlg_t[:, 8:16], ktile[:]))
            tok_cmb = DV(lambda v, ti=ti: v.tensor_scalar(
                cmb[:, 8 * ti:8 * (ti + 1)], etile[:], mcols[:, 1:2], None, ALU.mult), "v0")
        c_s1v, c_s1t = em.c["v0"], em.c["t0"]

        if dbg:
            W("gpsimd", "v0", c_s1v)
            W("gpsimd", "dz", em.c["dz"])
            D("gpsimd", lambda gp: gp.dma_start(out=dbg_o["d_h2"], in_=h2[:, 0:512]), "dg", 16)
            D("gpsimd", lambda gp: gp.dma_start(out=dbg_o["d_gb"], in_=gbb[:]), "dg", 16)
            D("gpsimd", lambda gp: gp.dma_start(out=dbg_o["d_lg"], in_=dlg_t[:]), "dg", 16)
            D("gpsimd", lambda gp: gp.dma_start(out=dbg_o["d_cmb"], in_=cmb[:, 0:16]), "dg", 16)
            D("gpsimd", lambda gp: gp.dma_start(out=dbg_o["d_x1"], in_=x1d[0:128, :]), "dg", 16)

        if stop < 2:
            W("sync", "ds", em.c["ds"]); W("sync", "dz", em.c["dz"])
            W("gpsimd", "dg", em.c["dg"])
            W("vector", "v0", em.c["v0"]); W("scalar", "s0", em.c["s0"])
            W("tensor", "t0", em.c["t0"])
            return

        def wcol_load(slot, src, e, ht, ring):
            return D("gpsimd", lambda gp, slot=slot, src=src, e=e, ht=ht, ring=ring: gp.dma_start(
                out=ring[:, 1024 * slot:1024 * (slot + 1)].rearrange("p (fc h) -> p fc h", fc=8),
                in_=src[e, :, 128 * ht:128 * (ht + 1)].rearrange("(fc p) h -> p fc h", p=128)),
                "dg", 16)

        def w2_load(src, e, hc):
            return D("gpsimd", lambda gp, src=src, e=e, hc=hc: gp.dma_start(
                out=w2f[:, 1024 * hc:1024 * (hc + 1)],
                in_=src[e, 128 * hc:128 * (hc + 1), :]), "dg", 16)

        units = [(w1T, w3T, w2T, e, "r") for e in range(NE)] + \
                [(sw1T, sw3T, sw2T, e, "s") for e in range(NSH)]

        sil_list = []
        u_list = []
        yev_list = []
        w1_toks, w3_toks, w2_toks = {}, {}, {}
        w1_mmdone, w3_mmdone = {}, {}
        W("gpsimd", "v0", c_s1v)
        w1_toks[(0, 0)] = wcol_load(0, w1T, 0, 0, w1r)
        w3_toks[(0, 0)] = wcol_load(0, w3T, 0, 0, w3r)

        for ui, (s1, s3, s2, e, kind) in enumerate(units):
            for ht in range(16):
                slot = ht % 2
                nxt = (ui, ht + 1) if ht + 1 < 16 else ((ui + 1, 0) if ui + 1 < len(units) else None)
                if nxt is not None:
                    prev_same_slot = (ui, ht - 1) if ht >= 1 else (ui - 1, 15)
                    if prev_same_slot in w1_mmdone:
                        W("gpsimd", "t0", w1_mmdone[prev_same_slot])
                    srcn = s1 if nxt[0] == ui else units[nxt[0]][0]
                    en = e if nxt[0] == ui else units[nxt[0]][3]
                    w1_toks[nxt] = wcol_load((ht + 1) % 2, srcn, en, nxt[1], w1r)
                W("tensor", "dg", w1_toks[(ui, ht)])
                if ui == 0 and ht == 0:
                    W("tensor", "v0", c_s1v)
                for tq in range(4):
                    bank = psA if tq % 2 == 0 else psB
                    if ht == 0 and tq <= 1 and ui > 0:
                        W("tensor", "v0", yev_list[-1])
                    if len(sil_list) >= 2:
                        W("tensor", "s0", sil_list[-2])
                    for fc in range(8):
                        D("tensor", lambda t, ht=ht, tq=tq, fc=fc, bank=bank, slot=slot: t.matmul(
                            bank[:], w1r[:, 1024 * slot + 128 * fc:1024 * slot + 128 * (fc + 1)],
                            h2[:, TS * fc + 512 * tq:TS * fc + 512 * (tq + 1)],
                            start=(fc == 0), stop=(fc == 7)), "t0" if fc == 7 else None)
                    cmm = em.c["t0"]
                    W("scalar", "t0", cmm)
                    sil_list.append(D("scalar", lambda s, ht=ht, tq=tq, bank=bank: s.activation(
                        Au[:, TS * ht + 512 * tq: TS * ht + 512 * (tq + 1)], bank[:], AF.Silu,
                        bias=zcol[:]), "s0"))
                w1_mmdone[(ui, ht)] = em.c["t0"]
            c_sil_done = sil_list[-1]
            for ht in range(16):
                slot = ht % 2
                nxt = (ui, ht + 1) if ht + 1 < 16 else ((ui + 1, 0) if ui + 1 < len(units) else None)
                if nxt is not None:
                    prev_same_slot = (ui, ht - 1) if ht >= 1 else (ui - 1, 15)
                    if prev_same_slot in w3_mmdone:
                        W("gpsimd", "t0", w3_mmdone[prev_same_slot])
                    srcn = s3 if nxt[0] == ui else units[nxt[0]][1]
                    en = e if nxt[0] == ui else units[nxt[0]][3]
                    w3_toks[nxt] = wcol_load((ht + 1) % 2, srcn, en, nxt[1], w3r)
                W("tensor", "dg", w3_toks[(ui, ht)])
                for tq in range(4):
                    bank = psA if tq % 2 == 0 else psB
                    if len(u_list) >= 2:
                        W("tensor", "v0", u_list[-2])
                    elif ui == 0 and ht == 0 and tq == 0:
                        W("tensor", "s0", c_sil_done)
                    for fc in range(8):
                        D("tensor", lambda t, ht=ht, tq=tq, fc=fc, bank=bank, slot=slot: t.matmul(
                            bank[:], w3r[:, 1024 * slot + 128 * fc:1024 * slot + 128 * (fc + 1)],
                            h2[:, TS * fc + 512 * tq:TS * fc + 512 * (tq + 1)],
                            start=(fc == 0), stop=(fc == 7)), "t0" if fc == 7 else None)
                    cmm = em.c["t0"]
                    W("vector", "t0", cmm)
                    sidx = 64 * ui + 4 * ht + tq
                    W("vector", "s0", sil_list[sidx])
                    u_list.append(D("vector", lambda v, ht=ht, tq=tq, bank=bank: v.tensor_tensor(
                        Au[:, TS * ht + 512 * tq: TS * ht + 512 * (tq + 1)],
                        bank[:],
                        Au[:, TS * ht + 512 * tq: TS * ht + 512 * (tq + 1)], ALU.mult), "v0"))
                w3_mmdone[(ui, ht)] = em.c["t0"]
            c_u_done = u_list[-1]
            for hc in range(16):
                if ui > 0 and hc == 0:
                    W("gpsimd", "t0", w2_mmdone)
                w2_toks[(ui, hc)] = w2_load(s2, e, hc)
            W("tensor", "dg", w2_toks[(ui, 15)])
            W("tensor", "v0", c_u_done)
            for tq in range(16):
                for dh in range(2):
                    bank = psA if (2 * tq + dh) % 2 == 0 else psB
                    if len(yev_list) >= 2:
                        W("tensor", "v0", yev_list[-2])
                    for hc in range(16):
                        D("tensor", lambda t, tq=tq, dh=dh, hc=hc, bank=bank: t.matmul(
                            bank[:], Au[:, TS * hc + 128 * tq: TS * hc + 128 * (tq + 1)],
                            w2f[:, 1024 * hc + 512 * dh: 1024 * hc + 512 * (dh + 1)],
                            start=(hc == 0), stop=(hc == 15)), "t0" if hc == 15 else None)
                    cmm = em.c["t0"]
                    W("vector", "t0", cmm)
                    if kind == "r":
                        yev_list.append(D("vector", lambda v, tq=tq, dh=dh, e=e, bank=bank:
                                          v.scalar_tensor_tensor(
                            yac[:, 1024 * tq + 512 * dh: 1024 * tq + 512 * (dh + 1)],
                            bank[:], cmb[:, 8 * tq + e:8 * tq + e + 1],
                            yac[:, 1024 * tq + 512 * dh: 1024 * tq + 512 * (dh + 1)],
                            ALU.mult, ALU.add), "v0"))
                    else:
                        yev_list.append(D("vector", lambda v, tq=tq, dh=dh, bank=bank:
                                          v.scalar_tensor_tensor(
                            yac[:, 1024 * tq + 512 * dh: 1024 * tq + 512 * (dh + 1)],
                            bank[:], 1.0,
                            yac[:, 1024 * tq + 512 * dh: 1024 * tq + 512 * (dh + 1)],
                            ALU.mult, ALU.add), "v0"))
            w2_mmdone = em.c["t0"]
        c_s2v = em.c["v0"]

        tok_out = {}
        for ti in range(NTT):
            if ti >= 2:
                W("sync", "v0", tok_out[ti - 2])
            elif ti == 0:
                W("sync", "dg", em.c["dg"])
                W("sync", "t0", em.c["t0"])
            cx = D("sync", lambda sp, ti=ti: sp.dma_start(
                out=x1rd[:, 1024 * (ti % 2):1024 * (ti % 2 + 1)],
                in_=x1d[128 * ti:128 * (ti + 1), :]), "ds", 16)
            W("vector", "ds", cx)
            if ti == 0:
                W("vector", "v0", c_s2v)
            if ti >= 2:
                W("vector", "dz", 16 * (NTT + ti - 1))
            DV(lambda v, ti=ti: v.tensor_tensor(
                outr[:, 1024 * (ti % 2):1024 * (ti % 2 + 1)],
                yac[:, 1024 * ti:1024 * (ti + 1)], gbb[:], ALU.mult))
            tok_out[ti] = DV(lambda v, ti=ti: v.tensor_tensor(
                outr[:, 1024 * (ti % 2):1024 * (ti % 2 + 1)],
                outr[:, 1024 * (ti % 2):1024 * (ti % 2 + 1)],
                x1rd[:, 1024 * (ti % 2):1024 * (ti % 2 + 1)], ALU.add), "v0")
            W("sync", "v0", tok_out[ti])
            if dbg and ti == 0:
                W("gpsimd", "v0", tok_out[0])
                D("gpsimd", lambda gp: gp.dma_start(out=dbg_o["d_o0"], in_=outr[:, 0:1024]), "dg", 16)
                D("gpsimd", lambda gp: gp.dma_start(out=dbg_o["d_xr0"], in_=x1rd[:, 0:1024]), "dg", 16)
                W("sync", "dg", em.c["dg"])
            D("sync", lambda sp, ti=ti: sp.dma_start(
                out=oout[128 * ti:128 * (ti + 1), :],
                in_=outr[:, 1024 * (ti % 2):1024 * (ti % 2 + 1)]), "dz", 16)
        if dbg:
            W("gpsimd", "v0", em.c["v0"])
            D("gpsimd", lambda gp: gp.dma_start(out=dbg_o["d_y"], in_=yac[:, 0:1024]), "dg", 16)
            cm2 = D("gpsimd", lambda gp: gp.dma_start(out=dbg_o["d_mod"][:, 0:4], in_=mcols[:, 0:4]), "dg", 16)

        W("sync", "dz", em.c["dz"])
        W("gpsimd", "dg", em.c["dg"])

    with nc.Block() as block:
        block.sync(lambda en: program(Em(nc, sems, "sync", en)))
        block.gpsimd(lambda en: program(Em(nc, sems, "gpsimd", en)))
        block.tensor(lambda en: program(Em(nc, sems, "tensor", en)))
        block.vector(lambda en: program(Em(nc, sems, "vector", en)))
        block.scalar(lambda en: program(Em(nc, sems, "scalar", en)))
    return nc


def l2_inputs(core, ins, zparts):
    b, q = core // 4, core % 4
    f32 = np.float32
    sl = slice(2048 * q, 2048 * (q + 1))
    zp4 = np.stack([np.asarray(zparts[4 * b + j][sl], f32) for j in range(4)], axis=0)
    tr = lambda w: np.ascontiguousarray(np.asarray(w, f32).transpose(0, 2, 1))
    return {
        "zp4": np.ascontiguousarray(zp4),
        "x_sl": np.ascontiguousarray(np.asarray(ins["L0_attn_x"], f32)[b, sl]),
        "c_fmaj": np.ascontiguousarray(np.asarray(ins["L0_c"], f32)[b].reshape(8, 128).T),
        "wm2": np.ascontiguousarray(np.asarray(ins["L0_wmod"], f32)[:, 2048:6144]),
        "bo_row": np.ascontiguousarray(np.asarray(ins["L0_attn_bo"], f32).reshape(1, E)),
        "routerT": np.ascontiguousarray(np.asarray(ins["L0_ffn_router_w"], f32).T),
        "eb_row": np.ascontiguousarray(np.asarray(ins["L0_ffn_expert_bias"], f32).reshape(1, 8)),
        "w1T": tr(ins["L0_ffn_w1"]),
        "w3T": tr(ins["L0_ffn_w3"]),
        "w2T": tr(ins["L0_ffn_w2"]),
        "sw1T": tr(ins["L0_ffn_sw1"]),
        "sw3T": tr(ins["L0_ffn_sw3"]),
        "sw2T": tr(ins["L0_ffn_sw2"]),
    }


from concourse.bass_utils import run_bass_kernel_spmd

_CACHE = {}

def kernel(**inputs):
    if "l1" not in _CACHE:
        _CACHE["l1"] = build_l1(dbg=False)
    nc1 = _CACHE["l1"]
    im1 = [l1_inputs(c, inputs) for c in range(8)]
    r1 = run_bass_kernel_spmd(nc1, im1, core_ids=list(range(8)))
    zparts = [r1.results[c]["zpart"] for c in range(8)]

    if "l2" not in _CACHE:
        _CACHE["l2"] = build_l2(dbg=False)
    nc2 = _CACHE["l2"]
    im2 = [l2_inputs(c, inputs, zparts) for c in range(8)]
    r2 = run_bass_kernel_spmd(nc2, im2, core_ids=list(range(8)))

    out = np.empty((2, 8192, 1024), dtype=np.float32)
    for c in range(8):
        b, q = c // 4, c % 4
        out[b, 2048 * q:2048 * (q + 1), :] = r2.results[c]["oslice"]
    return out


# version 02
# speedup vs baseline: 1.4125x; 1.4125x over previous
import numpy as np
import concourse.bass as bass
import concourse.mybir as mybir

F32 = mybir.dt.float32
BF16 = mybir.dt.bfloat16
FP16 = mybir.dt.float16
I32 = mybir.dt.int32
AF = mybir.ActivationFunctionType
ALU = mybir.AluOpType
AXX = mybir.AxisListType.X

PI = float(np.pi)
LN2 = float(np.log(2.0))
LN1E4 = float(np.log(10000.0))

T = 8192
E = 1024
NH = 4
HD = 64
NP = 16
NT = T // 128
EPS = 1e-5
KB = 1024


def consts_np():
    perm = np.zeros((128, 128), dtype=np.float32)
    for i in range(64):
        perm[2 * i, 2 * i + 1] = -1.0
        perm[2 * i + 1, 2 * i] = 1.0
    p_ = np.arange(128).reshape(128, 1)
    n_ = np.arange(512).reshape(1, 512)
    return {
        "ident": np.eye(128, dtype=np.float32),
        "permT": perm.T.copy(),
        "ones_row": np.ones((1, 128), dtype=np.float32),
        "ones_col": np.ones((128, 1), dtype=np.float32),
        "pmod32": (np.arange(128) % 32).astype(np.float32).reshape(128, 1),
        "pk2d": (p_ + 128.0 * np.arange(64).reshape(1, 64)).astype(np.float32),
        "tki": np.broadcast_to(np.arange(512, dtype=np.float32), (128, 512)).copy(),
        "drelu": np.concatenate(
            [np.maximum(p_ - n_ + c, 0).astype(np.float16) for c in (0, 128, 256, 384)],
            axis=1),
        "ospl": np.concatenate([np.concatenate([np.ones((64, 1)), np.zeros((64, 1))]),
                                np.concatenate([np.zeros((64, 1)), np.ones((64, 1))])],
                               axis=1).astype(np.float32),
        "eh4": np.concatenate([np.where(np.arange(4).reshape(4, 1) == h,
                                        np.ones((4, 128)), np.zeros((4, 128)))
                               for h in range(4)], axis=1).astype(np.float32),
    }


class Em:
    def __init__(self, nc, sems, target, engobj):
        self.nc, self.sems, self.target, self.engobj = nc, sems, target, engobj
        self.c = {k: 0 for k in sems}

    def do(self, eng, fn, inc=None, n=1):
        if self.target == eng:
            inst = fn(self.engobj)
            if inc is not None:
                inst.then_inc(self.sems[inc], n)
        if inc is not None:
            self.c[inc] += n
            return self.c[inc]
        return None

    def wait(self, eng, sem, val):
        assert val <= self.c[sem], (sem, val, self.c[sem])
        if self.target == eng:
            self.engobj.wait_ge(self.sems[sem], val)

    def _bump(self, sem):
        self.c[sem] += 1
        return self.c[sem]


def build_l1(dbg=False, stop=99):
    nc = bass.Bass("TRN2", target_bir_lowering=False)
    CN = consts_np()

    def din(name, shape):
        return nc.dram_tensor(name, list(shape), F32, kind="ExternalInput").ap()

    xb = din("xb", (T, E))
    wqT = din("wqT", (E, 256)); wkT = din("wkT", (E, 256))
    wvT = din("wvT", (E, 256)); wgT = din("wgT", (E, 256))
    woTb = din("woTb", (128, 2048))
    wdT = din("wdT", (E, 4))
    bq_f = din("bq_f", (128, 2)); bk_f = din("bk_f", (128, 2)); bg_f = din("bg_f", (128, 2))
    bv_row = din("bv_row", (1, 256))
    bd_col = din("bd_col", (4, 1))
    gam_pk = din("gam_pk", (128, 2)); bet_pk = din("bet_pk", (128, 2))
    c_fmaj = din("c_fmaj", (128, 8))
    wm_sc = din("wm_sc", (E, 2048))

    zout = nc.dram_tensor("zpart", [T, E], F32, kind="ExternalOutput").ap()
    dbg_o = {}
    if dbg:
        for nm, sh in [("t_q", (128, 512)), ("t_k", (128, 512)), ("t_v", (128, 260)),
                       ("t_g", (128, 512)), ("t_h", (128, 512)), ("t_og", (128, 512)),
                       ("t_m", (128, 16))]:
            dbg_o[nm] = nc.dram_tensor(nm, list(sh), F32, kind="ExternalOutput").ap()

    ic = {k: nc.inline_tensor(np.ascontiguousarray(v), f"c_{k}").ap() for k, v in CN.items()}

    A = lambda name, shape, dtype: nc.alloc_sbuf_tensor(name, list(shape), dtype, side="left")

    Vp = A("Vp", (128, NT * 260), BF16)
    gf = A("gf", (128, 2 * T), BF16)
    wq_b = A("wq_b", (128, 2048), BF16); wk_b = A("wk_b", (128, 2048), BF16)
    wv_b = A("wv_b", (128, 2048), BF16); wg_b = A("wg_b", (128, 2048), BF16)
    wo_b = A("wo_b", (128, 2048), BF16); wd_b = A("wd_b", (128, 32), BF16)
    identt = A("identt", (128, 128), F32)
    permt = A("permt", (128, 128), F32)
    onesr = A("onesr", (1, 128), F32)
    onesc = A("onesc", (128, 1), F32)
    pmod32t = A("pmod32t", (128, 1), F32)
    pk2dt = A("pk2dt", (128, 64), F32)
    tkit = A("tkit", (128, 512), F32)
    osplt = A("osplt", (128, 2), F32)
    eh4t = A("eh4t", (4, 512), F32)
    bqf = A("bqf", (128, 2), F32); bkf = A("bkf", (128, 2), F32); bgf = A("bgf", (128, 2), F32)
    gampk = A("gampk", (128, 2), F32); betpk = A("betpk", (128, 2), F32)
    bdc = A("bdc", (4, 1), F32)
    cfm = A("cfm", (128, 8), F32)
    fcol = A("fcol", (128, 1), F32)
    silc = A("silc", (128, 8), F32)
    hsum = A("hsum", (128, 8), F32)
    hmean_b = A("hmean_b", (128, 8), BF16)
    bv_bc = A("bv_bc", (128, 256), F32)
    pdld = A("pdld", (4, 4), F32)
    ldbc = A("ldbc", (128, 4), F32)
    nbbc = A("nbbc", (128, 4), F32)
    ldkb = A("ldkb", (128, 4 * 64), F32)
    n2q = A("n2q", (2, 8), F32)
    n2k = A("n2k", (2, 8), F32)
    n2t = A("n2t", (2, 2), F32)
    s1a = A("s1a", (128, 4), F32)
    s2a = A("s2a", (128, 4), F32)
    abcol = A("abcol", (128, 8), F32)
    ogscr = A("ogscr", (64, 512), BF16)
    sqt = A("sqt", (128, 1024), F32)
    lncol = A("lncol", (128, 16), F32)
    zcol = A("zcol", (128, 1), F32)
    zc4 = A("zc4", (4, 1), F32)
    c5l2 = A("c5l2", (4, 1), F32)
    b2c = A("b2c", (2, 4), F32)
    bnd2 = A("bnd2", (2, 2), F32)
    tofff = A("tofff", (128, 1), F32)
    negld = A("negld", (128, 1), F32)
    nlq = A("nlq", (128, 1), F32)
    Qf = A("Qf", (128, 2 * T), BF16)
    kf0, _ = nc.bump_sbuf(32 * KB, side="left")
    Kf = nc.alloc_sbuf_tensor_at("Kf", [128, 2 * T], BF16, offset=kf0)
    if dbg:
        dbm = A("dbm", (128, 16), F32)

    ARENA = 44 * KB
    ar0, _ = nc.bump_sbuf(ARENA, side="left")
    def AT(name, shape, dtype, off):
        return nc.alloc_sbuf_tensor_at(name, list(shape), dtype, offset=ar0 + off)
    wmt = AT("wmt", (128, 2048), F32, 0)
    modrow = AT("modrow", (1, 2048), F32, 8 * KB)
    bvr = AT("bvr", (1, 256), F32, 16 * KB)
    sa_bc = AT("sa_bc", (128, 1024), F32, 18 * KB)
    capc_bc = AT("capc_bc", (128, 1024), F32, 22 * KB)
    tmp1k = AT("tmp1k", (128, 1024), F32, 0)
    q0r = AT("q0r", (128, 1024), F32, 4 * KB)
    rtmp = AT("rtmp", (128, 512), F32, 8 * KB)
    sint = AT("sint", (128, 512), F32, 10 * KB)
    cost = AT("cost", (128, 512), F32, 12 * KB)
    angt = AT("angt", (128, 512), F32, 14 * KB)
    rrt = AT("rrt", (128, 512), F32, 16 * KB)
    xr = AT("xr", (128, 2048), F32, 26 * KB)
    hp = AT("hp", (128, 4096), BF16, 34 * KB)
    ki32t = AT("ki32t", (128, 512), I32, 42 * KB)
    og = AT("og", (128, 2 * T), BF16, 0)
    drt = AT("drt", (128, 2048), FP16, 32 * KB)
    Pr = AT("Pr", (128, 1536), BF16, 36 * KB)
    ldqt = AT("ldqt", (128, 512), F32, 39 * KB)
    rbct = AT("rbct", (64, 512), F32, 39 * KB)
    rrow = AT("rrow", (1, 512), F32, 41 * KB)
    stcol = AT("stcol", (64, 4), F32, 43 * KB)
    fincol = AT("fincol", (1, 16), F32, 43 * KB + 256)
    zev = nc.alloc_sbuf_tensor_at("zev", [128, 2048], F32, offset=kf0)
    ont = nc.alloc_sbuf_tensor_at("ont", [128, 512], F32, offset=kf0 + 8 * KB)

    psA = nc.alloc_psum_tensor("psA", [128, 512], F32)
    psB = nc.alloc_psum_tensor("psB", [128, 512], F32)
    psC = nc.alloc_psum_tensor("psC", [128, 512], F32)
    psD = nc.alloc_psum_tensor("psD", [128, 512], F32)
    psE = nc.alloc_psum_tensor("psE", [128, 512], F32)
    psF = nc.alloc_psum_tensor("psF", [65, 512], F32)
    psG = nc.alloc_psum_tensor("psG", [65, 512], F32)
    psH = nc.alloc_psum_tensor("psH", [64, 512], F32)

    sems = {k: nc.alloc_semaphore(k) for k in ["ds", "dg", "dz", "t0", "v0", "s0", "vv", "sv2"]}

    def program(em: Em):
        W, D = em.wait, em.do

        def DV(fn, inc=None):
            sem = inc or "vv"
            tok = D("vector", fn, sem)
            W("vector", sem, tok)
            return tok

        def DS(fn, inc=None):
            sem = inc or "sv2"
            tok = D("scalar", fn, sem)
            W("scalar", sem, tok)
            return tok

        for nm, tile in [("ident", identt), ("permT", permt), ("ones_row", onesr),
                         ("ones_col", onesc), ("pmod32", pmod32t), ("pk2d", pk2dt),
                         ("tki", tkit), ("ospl", osplt), ("eh4", eh4t)]:
            D("sync", lambda sp, nm=nm, tile=tile: sp.dma_start(out=tile[:], in_=ic[nm]), "ds", 16)
        for src, tile in [(bq_f, bqf), (bk_f, bkf), (bg_f, bgf), (gam_pk, gampk),
                          (bet_pk, betpk), (bd_col, bdc), (c_fmaj, cfm)]:
            D("sync", lambda sp, src=src, tile=tile: sp.dma_start(out=tile[:], in_=src), "ds", 16)
        D("sync", lambda sp: sp.dma_start(out=bvr[:], in_=bv_row), "ds", 16)
        c_consts = em.c["ds"]

        for src, tile in [(wqT, wq_b), (wkT, wk_b), (wvT, wv_b), (wgT, wg_b)]:
            D("gpsimd", lambda gp, src=src, tile=tile: gp.dma_start(
                out=tile[:].rearrange("p (fc o) -> p fc o", fc=8),
                in_=src.rearrange("(fc p) o -> p fc o", p=128)), "dg", 16)
        D("gpsimd", lambda gp: gp.dma_start(
            out=wd_b[:].rearrange("p (fc o) -> p fc o", fc=8),
            in_=wdT.rearrange("(fc p) o -> p fc o", p=128)), "dg", 16)
        D("gpsimd", lambda gp: gp.dma_start(out=wo_b[:], in_=woTb), "dg", 16)
        c_wg = em.c["dg"]

        D("vector", lambda v: v.memset(zcol[:], 0.0))
        D("vector", lambda v: v.memset(zc4[:], 0.0))
        D("vector", lambda v: v.memset(c5l2[:], -5.0 * LN2))
        D("vector", lambda v: v.memset(hsum[:], 0.0))
        D("vector", lambda v: v.memset(n2q[:], 0.0))
        D("vector", lambda v: v.memset(n2k[:], 0.0))
        D("vector", lambda v: v.memset(s1a[:], 0.0))
        D("vector", lambda v: v.memset(s2a[:], 0.0))
        cv_ones = D("vector", lambda v: v.memset(
            Vp[:].rearrange("p (kth c) -> p kth c", c=65)[:, :, 64:65], 1.0), "v0")

        W("scalar", "ds", c_consts)
        D("scalar", lambda s: s.activation(fcol[:], pmod32t[:], AF.Exp, bias=zcol[:],
                                           scale=-LN1E4 / 32.0), "s0")
        cs_sil = D("scalar", lambda s: s.activation(silc[:], cfm[:], AF.Silu, bias=zcol[:]), "s0")

        mod_ps = [psA, psB, psC, psD]
        for fc in range(8):
            cw = D("sync", lambda sp, fc=fc: sp.dma_start(
                out=wmt[:], in_=wm_sc[128 * fc:128 * (fc + 1), :]), "ds", 16)
            W("tensor", "ds", cw)
            if fc == 0:
                W("tensor", "s0", cs_sil)
            for half in range(4):
                D("tensor", lambda t, fc=fc, half=half: t.matmul(
                    mod_ps[half][0:1, :], silc[:, fc:fc + 1], wmt[:, 512 * half:512 * (half + 1)],
                    start=(fc == 0), stop=(fc == 7)), "t0" if half == 3 else None)
            if fc < 7:
                W("sync", "t0", em.c["t0"])
        c_modmm = em.c["t0"]
        W("vector", "t0", c_modmm)
        for half in range(4):
            DV(lambda v, half=half: v.tensor_copy(
                modrow[0:1, 512 * half:512 * (half + 1)], mod_ps[half][0:1, :]))
        cv1 = DV(lambda v: v.tensor_scalar(
            modrow[0:1, 1024:2048], modrow[0:1, 1024:2048], 1.0, 1.0, ALU.mult, ALU.add), "v0")
        W("tensor", "v0", cv1)
        for half in range(4):
            dst = sa_bc if half < 2 else capc_bc
            cbt = D("tensor", lambda t, half=half: t.matmul(
                psE[:], onesr[0:1, :], modrow[0:1, 512 * half:512 * (half + 1)],
                start=True, stop=True), "t0")
            W("vector", "t0", cbt)
            cvh = D("vector", lambda v, dst=dst, half=half: v.tensor_copy(
                dst[:, 512 * (half % 2):512 * (half % 2 + 1)], psE[:]), "v0")
            W("tensor", "v0", cvh)
        cbt = D("tensor", lambda t: t.matmul(psE[:, 0:256], onesr[0:1, :], bvr[0:1, :],
                                             start=True, stop=True), "t0")
        W("vector", "t0", cbt)
        cv_bv = D("vector", lambda v: v.tensor_copy(bv_bc[:], psE[:, 0:256]), "v0")
        W("vector", "ds", c_consts)
        cv_bq8 = DV(lambda v: v.tensor_scalar(bqf[:], bqf[:], 0.125, None, ALU.mult), "v0")

        if stop < 1:
            W("sync", "ds", em.c["ds"]); W("gpsimd", "dg", em.c["dg"])
            W("vector", "v0", em.c["v0"]); W("scalar", "s0", em.c["s0"])
            W("tensor", "t0", em.c["t0"])
            return
        tok_x, tok_hTM, tok_hT, tok_hfm = {}, {}, {}, {}
        tok_qk, tok_gev, tok_nrm, tok_nmm = {}, {}, {}, {}

        def xload(ti):
            return D("sync", lambda sp, ti=ti: sp.dma_start(
                out=xr[:, 1024 * (ti % 2):1024 * (ti % 2 + 1)],
                in_=xb[128 * ti:128 * (ti + 1), :]), "ds", 16)

        tok_x[0] = xload(0)
        tok_x[1] = xload(1)

        for tt in range(NP):
            for st in range(4):
                ti = 4 * tt + st
                xsl = xr[:, 1024 * (ti % 2):1024 * (ti % 2 + 1)]
                W("vector", "ds", tok_x[ti])
                if tt == 0 and st == 0:
                    W("vector", "v0", cv_bv)
                DV(lambda v, xsl=xsl: v.reduce_sum(lncol[:, 0:1], xsl, axis=AXX))
                W("scalar", "ds", tok_x[ti])
                if tok_nmm.get("last"):
                    W("scalar", "t0", tok_nmm["last"])
                csq = D("scalar", lambda s, xsl=xsl: s.activation(sqt[:], xsl, AF.Square, bias=zcol[:]), "s0")
                W("vector", "s0", csq)
                DV(lambda v: v.reduce_sum(lncol[:, 1:2], sqt[:], axis=AXX))
                DV(lambda v: v.tensor_scalar(lncol[:, 2:3], lncol[:, 0:1], 1.0 / E, None, ALU.mult))
                DV(lambda v: v.tensor_scalar(lncol[:, 3:4], lncol[:, 1:2], 1.0 / E, None, ALU.mult))
                DV(lambda v: v.tensor_tensor(lncol[:, 4:5], lncol[:, 2:3], lncol[:, 2:3], ALU.mult))
                DV(lambda v: v.tensor_scalar(lncol[:, 4:5], lncol[:, 3:4], lncol[:, 4:5], None, ALU.subtract))
                DV(lambda v: v.tensor_scalar(lncol[:, 4:5], lncol[:, 4:5], 1.0, EPS, ALU.mult, ALU.add))
                cvv = DV(lambda v: v.reciprocal(lncol[:, 5:6], lncol[:, 4:5]), "v0")
                W("scalar", "v0", cvv)
                cs2 = D("scalar", lambda s: s.activation(lncol[:, 6:7], lncol[:, 5:6], AF.Sqrt, bias=zcol[:]), "s0")
                W("vector", "s0", cs2)
                DV(lambda v: v.tensor_scalar(lncol[:, 7:8], lncol[:, 2:3], lncol[:, 6:7], None, ALU.mult))
                DV(lambda v: v.tensor_scalar(lncol[:, 7:8], lncol[:, 7:8], -1.0, None, ALU.mult))
                DV(lambda v, xsl=xsl: v.tensor_scalar(
                    tmp1k[:], xsl, lncol[:, 6:7], lncol[:, 7:8], ALU.mult, ALU.add))
                DV(lambda v: v.tensor_tensor(tmp1k[:], tmp1k[:], capc_bc[:], ALU.mult))
                tok_hTM[ti] = DV(lambda v, xsl=xsl: v.tensor_tensor(
                    xsl, tmp1k[:], sa_bc[:], ALU.add), "v0")

                W("tensor", "v0", tok_hTM[ti])
                for fc in range(8):
                    bank = psA if fc < 4 else psB
                    D("tensor", lambda t, xsl=xsl, fc=fc, bank=bank: t.transpose(
                        bank[:, 128 * (fc % 4):128 * (fc % 4 + 1)],
                        xsl.rearrange("p (fc q) -> p fc q", fc=8)[:, fc, :],
                        identt[:]), "t0" if fc in (3, 7) else None)
                tok_hT[ti] = em.c["t0"]
                W("vector", "t0", tok_hT[ti])
                hp3 = hp[:].rearrange("p (fc q) -> p fc q", fc=8)
                D("vector", lambda v, st=st, hp3=hp3: v.tensor_copy(
                    hp3[:, 0:4, 128 * st:128 * (st + 1)],
                    psA[:].rearrange("p (fc q) -> p fc q", fc=4)))
                tok_hfm[ti] = D("vector", lambda v, st=st, hp3=hp3: v.tensor_copy(
                    hp3[:, 4:8, 128 * st:128 * (st + 1)],
                    psB[:].rearrange("p (fc q) -> p fc q", fc=4)), "v0")
                if ti + 2 < 64:
                    W("sync", "t0", tok_hT[ti])
                    tok_x[ti + 2] = xload(ti + 2)

            W("vector", "v0", tok_hfm[4 * tt + 3])
            for fc in range(8):
                DV(lambda v, fc=fc: v.reduce_sum(
                    lncol[:, 8:9], hp[:, 512 * fc:512 * (fc + 1)], axis=AXX))
                DV(lambda v, fc=fc: v.tensor_scalar(
                    hsum[:, fc:fc + 1], hsum[:, fc:fc + 1], lncol[:, 8:9], None, ALU.add),
                   "v0" if fc == 7 else None)

            DV(lambda v, tt=tt: v.tensor_scalar(tofff[:], fcol[:], float(512 * tt), None, ALU.mult))
            for which in ("sin", "cos"):
                if which == "sin":
                    DV(lambda v: v.tensor_scalar(angt[:], tkit[:], fcol[:], tofff[:], ALU.mult, ALU.add))
                else:
                    W("vector", "s0", em.c["s0"])
                    DV(lambda v: v.tensor_scalar(angt[:], angt[:], PI / 2.0, None, ALU.add))
                DV(lambda v: v.tensor_scalar(rrt[:], angt[:], 1.0 / (2 * PI), None, ALU.mult))
                DV(lambda v: v.tensor_copy(ki32t[:], rrt[:]))
                DV(lambda v: v.tensor_copy(rrt[:], ki32t[:]))
                cvx = DV(lambda v: v.scalar_tensor_tensor(
                    rrt[:], rrt[:], -2 * PI, angt[:], ALU.mult, ALU.add), "v0")
                W("scalar", "v0", cvx)
                dstv = sint if which == "sin" else cost
                D("scalar", lambda s, dstv=dstv: s.activation(dstv[:], rrt[:], AF.Sin, bias=zcol[:]), "s0")
            tok_sc = em.c["s0"]

            for wnm, wtile, dsttile, bias_t, scale in (("q", wq_b, Qf, bqf, 0.125),
                                                       ("k", wk_b, Kf, bkf, 1.0)):
                for ch in range(2):
                    bank = psC if ch == 0 else psD
                    if wnm == "q" and ch == 0:
                        W("tensor", "v0", tok_hfm[4 * tt + 3])
                        if tt > 0:
                            W("tensor", "s0", tok_gev[(tt - 1, 1)])
                    for fc in range(8):
                        D("tensor", lambda t, wtile=wtile, fc=fc, ch=ch, bank=bank: t.matmul(
                            bank[:], wtile[:, 256 * fc + 128 * ch:256 * fc + 128 * (ch + 1)],
                            hp[:, 512 * fc:512 * (fc + 1)],
                            start=(fc == 0), stop=(fc == 7)), "t0" if fc == 7 else None)
                    tok_mm = em.c["t0"]
                    W("scalar", "t0", tok_mm)
                    if tt == 0 and wnm == "q" and ch == 0:
                        W("scalar", "v0", cv_bq8)
                    q0sl = q0r[:, 512 * ch:512 * (ch + 1)]
                    cse = D("scalar", lambda s, bank=bank, bias_t=bias_t, ch=ch, q0sl=q0sl, scale=scale:
                            s.activation(q0sl, bank[:], AF.Identity, bias=bias_t[:, ch:ch + 1], scale=scale), "s0")
                    prevn = tok_nrm.get(("last",))
                    if prevn:
                        W("tensor", "v0", prevn)
                    W("tensor", "s0", cse)
                    W("tensor", "s0", tok_sc)
                    cpm = D("tensor", lambda t, q0sl=q0sl: t.matmul(
                        psE[:], permt[:], q0sl, start=True, stop=True), "t0")
                    W("vector", "t0", cpm)
                    D("vector", lambda v: v.tensor_tensor(rtmp[:], psE[:], sint[:], ALU.mult))
                    DV(lambda v, q0sl=q0sl: v.tensor_tensor(q0sl, q0sl, cost[:], ALU.mult))
                    tok_qk[(tt, wnm, ch)] = DV(
                        lambda v, dsttile=dsttile, ch=ch, tt=tt, q0sl=q0sl:
                        v.tensor_tensor(dsttile[:, T * ch + 512 * tt: T * ch + 512 * (tt + 1)],
                                        rtmp[:], q0sl, ALU.add), "v0")
                    W("scalar", "v0", tok_qk[(tt, wnm, ch)])
                    csq = D("scalar", lambda s, dsttile=dsttile, ch=ch, tt=tt: s.activation(
                        sqt[:, 0:512], dsttile[:, T * ch + 512 * tt: T * ch + 512 * (tt + 1)],
                        AF.Square, bias=zcol[:]), "s0")
                    W("tensor", "s0", csq)
                    cmm = D("tensor", lambda t: t.matmul(psE[0:2, :], osplt[:], sqt[:, 0:512],
                                                         start=True, stop=True), "t0")
                    tok_nmm["last"] = cmm
                    W("vector", "t0", cmm)
                    DV(lambda v: v.reduce_max(n2t[0:2, 0:1], psE[0:2, :], axis=AXX))
                    dstn = n2q if wnm == "q" else n2k
                    tok_nrm[("last",)] = DV(lambda v, dstn=dstn, ch=ch: v.tensor_scalar(
                        dstn[0:2, ch:ch + 1], dstn[0:2, ch:ch + 1], n2t[0:2, 0:1], None, ALU.max), "v0")

            tok_vev = {}
            for st in range(4):
                bank = psC if st % 2 == 0 else psD
                if st >= 2:
                    W("tensor", "v0", tok_vev[st - 2])
                for fc in range(8):
                    D("tensor", lambda t, st=st, fc=fc, bank=bank: t.matmul(
                        bank[:, 0:256],
                        hp[:, 512 * fc + 128 * st:512 * fc + 128 * (st + 1)],
                        wv_b[:, 256 * fc:256 * (fc + 1)],
                        start=(fc == 0), stop=(fc == 7)), "t0" if fc == 7 else None)
                tok_mm = em.c["t0"]
                W("vector", "t0", tok_mm)
                kt = 4 * tt + st
                tok_vev[st] = D("vector", lambda v, bank=bank, kt=kt: v.tensor_tensor(
                    Vp[:].rearrange("p (kt2 h c) -> p kt2 h c", h=4, c=65)[:, kt, :, 0:64],
                    bank[:, 0:256].rearrange("p (h c) -> p h c", h=4),
                    bv_bc[:].rearrange("p (h c) -> p h c", h=4), ALU.add), "v0")

            for ch in range(2):
                bank = psC if ch == 0 else psD
                W("tensor", "v0", tok_vev[2 + ch])
                for fc in range(8):
                    D("tensor", lambda t, fc=fc, ch=ch, bank=bank: t.matmul(
                        bank[:], wg_b[:, 256 * fc + 128 * ch:256 * fc + 128 * (ch + 1)],
                        hp[:, 512 * fc:512 * (fc + 1)],
                        start=(fc == 0), stop=(fc == 7)), "t0" if fc == 7 else None)
                tok_mm = em.c["t0"]
                W("scalar", "t0", tok_mm)
                tok_gev[(tt, ch)] = D("scalar", lambda s, bank=bank, ch=ch, tt=tt: s.activation(
                    gf[:, T * ch + 512 * tt: T * ch + 512 * (tt + 1)], bank[:], AF.Silu,
                    bias=bgf[:, ch:ch + 1]), "s0")

        c_ph1_v, c_ph1_s, c_ph1_t = em.c["v0"], em.c["s0"], em.c["t0"]
        if dbg:
            W("gpsimd", "v0", c_ph1_v)
            D("gpsimd", lambda gp: gp.dma_start(out=dbg_o["t_h"], in_=hp[:, 0:512]), "dg", 16)

        if stop < 2:
            W("sync", "ds", em.c["ds"]); W("gpsimd", "dg", em.c["dg"])
            W("vector", "v0", em.c["v0"]); W("scalar", "s0", em.c["s0"])
            W("tensor", "t0", em.c["t0"])
            return
        DV(lambda v: v.tensor_scalar(hmean_b[:], hsum[:], 1.0 / T, None, ALU.mult), "v0")
        cv_hm = em.c["v0"]
        W("tensor", "v0", cv_hm)
        for fc in range(8):
            D("tensor", lambda t, fc=fc: t.matmul(
                psA[0:4, 0:1], wd_b[:, 4 * fc:4 * (fc + 1)], hmean_b[:, fc:fc + 1],
                start=(fc == 0), stop=(fc == 7)), "t0" if fc == 7 else None)
        c_pdmm = em.c["t0"]
        W("scalar", "t0", c_pdmm)
        cs_pd = DS(lambda s: s.activation(pdld[:, 0:1], psA[0:4, 0:1], AF.Sigmoid, bias=bdc[:]), "s0")
        cs_e2 = DS(lambda s: s.activation(pdld[:, 1:2], pdld[:, 0:1], AF.Exp,
                                          bias=c5l2[:], scale=-8.0 * LN2), "s0")
        W("vector", "s0", cs_e2)
        cv3 = DV(lambda v: v.tensor_scalar(pdld[:, 1:2], pdld[:, 1:2], -1.0, 1.0, ALU.mult, ALU.add), "v0")
        W("scalar", "v0", cv3)
        cs_ld = D("scalar", lambda s: s.activation(pdld[:, 2:3], pdld[:, 1:2], AF.Ln, bias=zc4[:]), "s0")
        W("vector", "s0", cs_ld)
        for ch in range(2):
            DV(lambda v, ch=ch: v.reduce_max(b2c[0:2, 0:1], n2q[0:2, ch:ch + 1], axis=AXX))
            DV(lambda v, ch=ch: v.reduce_max(b2c[0:2, 1:2], n2k[0:2, ch:ch + 1], axis=AXX))
            cvb = DV(lambda v: v.tensor_scalar(b2c[0:2, 2:3], b2c[0:2, 0:1], b2c[0:2, 1:2], None, ALU.mult), "v0")
            W("scalar", "v0", cvb)
            D("scalar", lambda s, ch=ch: s.activation(bnd2[0:2, ch:ch + 1], b2c[0:2, 2:3], AF.Sqrt,
                                                      bias=zcol[0:2, :]), "s0")
        cs_b = em.c["s0"]
        W("tensor", "s0", cs_b)
        prev_tok = None
        for h in range(NH):
            if prev_tok:
                W("tensor", "v0", prev_tok)
            cm1 = D("tensor", lambda t, h=h: t.matmul(psA[:, 0:1], eh4t[:, 128 * h:128 * (h + 1)],
                                                      pdld[:, 2:3], start=True, stop=True), "t0")
            cm2 = D("tensor", lambda t, h=h: t.matmul(psB[:, 0:1], eh4t[0:2, 128 * (h % 2):128 * (h % 2) + 128],
                                                      bnd2[0:2, h // 2:h // 2 + 1], start=True, stop=True), "t0")
            W("vector", "t0", cm2)
            DV(lambda v, h=h: v.tensor_copy(ldbc[:, h:h + 1], psA[:, 0:1]))
            DV(lambda v, h=h: v.tensor_scalar(nbbc[:, h:h + 1], psB[:, 0:1], -1.0, None, ALU.mult))
            prev_tok = DV(lambda v, h=h: v.tensor_scalar(
                ldkb[:, 64 * h:64 * (h + 1)], pk2dt[:], ldbc[:, h:h + 1], nbbc[:, h:h + 1],
                ALU.mult, ALU.add), "v0")
        c_ph2_v = em.c["v0"]

        if stop < 3:
            W("sync", "ds", em.c["ds"]); W("gpsimd", "dg", em.c["dg"])
            W("vector", "v0", em.c["v0"]); W("scalar", "s0", em.c["s0"])
            W("tensor", "t0", em.c["t0"])
            return
        W("sync", "t0", c_ph1_t)
        if dbg:
            W("sync", "dg", em.c["dg"])
        c_dr = D("sync", lambda sp: sp.dma_start(out=drt[:], in_=ic["drelu"]), "ds", 16)
        tok_P, tok_avt, tok_oge = {}, {}, {}
        it = -1
        for h in range(NH):
            ch, rw = h // 2, 64 * (h % 2)
            for qt in range(NP):
                it += 1
                O2 = psF if it % 2 == 0 else psG
                if NT - (4 * qt + 4) > 0:
                    W("vector", "v0", c_ph2_v)
                    DV(lambda v, h=h: v.tensor_scalar(negld[:], ldbc[:, h:h + 1], -1.0, None, ALU.mult))
                    DV(lambda v, qt=qt: v.tensor_scalar(nlq[:], negld[:], float(512 * qt), None, ALU.mult))
                    DV(lambda v: v.tensor_scalar(
                        ldqt[:], tkit[:], negld[:], nlq[:], ALU.mult, ALU.add), "v0")
                def emit_scores(kb):
                    bank = psA if kb % 2 == 0 else psB
                    if kb >= 2:
                        W("tensor", "s0", tok_P[(h, qt, kb - 2)])
                    elif it > 0:
                        W("tensor", "s0", tok_P[divmod_prev(h, qt, NT - 2 + kb)])
                    else:
                        W("tensor", "v0", c_ph2_v)
                        W("tensor", "ds", c_dr)
                    tok_mm = D("tensor", lambda t, ch=ch, rw=rw, kb=kb, qt=qt, bank=bank: t.matmul(
                        bank[:], Kf[rw:rw + 64, T * ch + 128 * kb: T * ch + 128 * (kb + 1)],
                        Qf[rw:rw + 64, T * ch + 512 * qt: T * ch + 512 * (qt + 1)],
                        start=True, stop=True), "t0")
                    kind = "past" if kb < 4 * qt else ("diag" if kb < 4 * qt + 4 else "fut")
                    if kind == "fut":
                        W("vector", "t0", tok_mm)
                        cvf = D("vector", lambda v, bank=bank: v.tensor_tensor(
                            bank[:], bank[:], ldqt[:], ALU.add), "v0")
                        W("scalar", "v0", cvf)
                        bias = ldkb[:, 64 * h + kb:64 * h + kb + 1]
                    elif kind == "diag":
                        W("vector", "t0", tok_mm)
                        ci = kb - 4 * qt
                        cvf = D("vector", lambda v, bank=bank, ci=ci, h=h: v.scalar_tensor_tensor(
                            bank[:], drt[:, 512 * ci:512 * (ci + 1)], ldbc[:, h:h + 1], bank[:],
                            ALU.mult, ALU.add), "v0")
                        W("scalar", "v0", cvf)
                        bias = nbbc[:, h:h + 1]
                    else:
                        W("scalar", "t0", tok_mm)
                        bias = nbbc[:, h:h + 1]
                    if kb >= 3:
                        W("scalar", "t0", tok_avt[(h, qt, kb - 3)])
                    elif it > 0:
                        W("scalar", "t0", tok_avt[divmod_prev(h, qt, NT - 3 + kb)])
                    psl = Pr[:, 512 * (kb % 3):512 * (kb % 3 + 1)]
                    tok_P[(h, qt, kb)] = D("scalar", lambda s, bank=bank, psl=psl, bias=bias:
                                           s.activation(psl, bank[:], AF.Exp, bias=bias), "s0")
                    return psl

                psl_map = {}
                psl_map[0] = emit_scores(0)
                psl_map[1] = emit_scores(1)
                for kb in range(NT):
                    psl = psl_map.pop(kb)
                    W("tensor", "s0", tok_P[(h, qt, kb)])
                    if kb == 0 and it >= 2:
                        W("tensor", "v0", tok_oge[it - 2])
                    tok_avt[(h, qt, kb)] = D("tensor", lambda t, kb=kb, h=h, psl=psl, O2=O2: t.matmul(
                        O2[:], Vp[:, 260 * kb + 65 * h: 260 * kb + 65 * (h + 1)], psl,
                        start=(kb == 0), stop=(kb == NT - 1)), "t0")
                    if kb + 2 < NT:
                        psl_map[kb + 2] = emit_scores(kb + 2)
                W("vector", "t0", tok_avt[(h, qt, NT - 1)])
                cvr = D("vector", lambda v, O2=O2: v.reciprocal(rrow[0:1, :], O2[64:65, :]), "v0")
                W("tensor", "v0", cvr)
                cmm = D("tensor", lambda t: t.matmul(psH[:], onesr[0:1, 0:64], rrow[0:1, :],
                                                     start=True, stop=True), "t0")
                W("vector", "t0", cmm)
                DV(lambda v: v.tensor_copy(rbct[:, :], psH[0:64, :]))
                ogdst = (og[0:64, T * ch + 512 * qt: T * ch + 512 * (qt + 1)] if rw == 0
                         else ogscr[:, :])
                cvo = DV(lambda v, O2=O2, ogdst=ogdst: v.tensor_tensor(
                    ogdst, O2[0:64, :], rbct[:, :], ALU.mult), "v0")
                if rw != 0:
                    W("gpsimd", "v0", cvo)
                    D("gpsimd", lambda gp, ch=ch, qt=qt: gp.dma_start(
                        out=og[64:128, T * ch + 512 * qt: T * ch + 512 * (qt + 1)],
                        in_=ogscr[:, :]), "dg", 16)
                DV(lambda v, ogdst=ogdst: v.reduce_sum(stcol[:, 0:1], ogdst, axis=AXX))
                DV(lambda v, h=h: v.tensor_scalar(s1a[0:64, h:h + 1], s1a[0:64, h:h + 1],
                                                  stcol[:, 0:1], None, ALU.add))
                W("scalar", "v0", cvo)
                csq = D("scalar", lambda s, ogdst=ogdst: s.activation(
                    sqt[0:64, 0:512], ogdst, AF.Square, bias=zcol[0:64, :]), "s0")
                W("vector", "s0", csq)
                DV(lambda v: v.reduce_sum(stcol[:, 1:2], sqt[0:64, 0:512], axis=AXX))
                tok_oge[it] = DV(lambda v, h=h: v.tensor_scalar(
                    s2a[0:64, h:h + 1], s2a[0:64, h:h + 1], stcol[:, 1:2], None, ALU.add), "v0")

        c_ph3_v, c_ph3_t, c_ph3_dg, c_ph3_s = em.c["v0"], em.c["t0"], em.c["dg"], em.c["s0"]

        if dbg:
            W("gpsimd", "v0", c_ph3_v)
            c_tap = D("gpsimd", lambda gp: gp.dma_start(out=dbg_o["t_og"], in_=og[:, 0:512]), "dg", 16)
            D("gpsimd", lambda gp: gp.dma_start(out=dbg_o["t_q"], in_=Qf[:, 0:512]), "dg", 16)
            D("gpsimd", lambda gp: gp.dma_start(out=dbg_o["t_k"], in_=Kf[:, 0:512]), "dg", 16)
            D("gpsimd", lambda gp: gp.dma_start(out=dbg_o["t_v"], in_=Vp[:, 0:260]), "dg", 16)
            D("gpsimd", lambda gp: gp.dma_start(out=dbg_o["t_g"], in_=gf[:, 0:512]), "dg", 16)
        else:
            c_tap = None

        if stop < 4:
            W("sync", "ds", em.c["ds"]); W("gpsimd", "dg", em.c["dg"])
            W("vector", "v0", em.c["v0"]); W("scalar", "s0", em.c["s0"])
            W("tensor", "t0", em.c["t0"])
            return
        prev_tok = None
        tok_gn = {}
        for h in range(NH):
            ch, rw = h // 2, 64 * (h % 2)
            W("tensor", "v0", c_ph3_v if prev_tok is None else prev_tok)
            cm1 = D("tensor", lambda t, h=h: t.matmul(psA[0:1, 0:1], onesc[0:64, :], s1a[0:64, h:h + 1],
                                                      start=True, stop=True), "t0")
            cm2 = D("tensor", lambda t, h=h: t.matmul(psA[0:1, 1:2], onesc[0:64, :], s2a[0:64, h:h + 1],
                                                      start=True, stop=True), "t0")
            W("vector", "t0", cm2)
            NN = float(T * HD)
            DV(lambda v: v.tensor_scalar(fincol[0:1, 0:1], psA[0:1, 0:1], 1.0 / NN, None, ALU.mult))
            DV(lambda v: v.tensor_scalar(fincol[0:1, 1:2], psA[0:1, 1:2], 1.0 / NN, None, ALU.mult))
            DV(lambda v: v.tensor_tensor(fincol[0:1, 2:3], fincol[0:1, 0:1], fincol[0:1, 0:1], ALU.mult))
            DV(lambda v: v.tensor_scalar(fincol[0:1, 2:3], fincol[0:1, 1:2], fincol[0:1, 2:3], None, ALU.subtract))
            DV(lambda v: v.tensor_scalar(fincol[0:1, 2:3], fincol[0:1, 2:3], 1.0, EPS, ALU.mult, ALU.add))
            cvf = DV(lambda v: v.reciprocal(fincol[0:1, 3:4], fincol[0:1, 2:3]), "v0")
            W("scalar", "v0", cvf)
            csA = D("scalar", lambda s: s.activation(fincol[0:1, 4:5], fincol[0:1, 3:4], AF.Sqrt,
                                                     bias=zcol[0:1, :]), "s0")
            W("vector", "s0", csA)
            DV(lambda v: v.tensor_scalar(fincol[0:1, 5:6], fincol[0:1, 0:1], fincol[0:1, 4:5], None, ALU.mult))
            cvB = DV(lambda v: v.tensor_scalar(fincol[0:1, 5:6], fincol[0:1, 5:6], -1.0, None, ALU.mult), "v0")
            W("tensor", "v0", cvB)
            cb1 = D("tensor", lambda t: t.matmul(psA[:, 2:3], onesr[0:1, :], fincol[0:1, 4:5],
                                                 start=True, stop=True), "t0")
            cb2 = D("tensor", lambda t: t.matmul(psB[:, 2:3], onesr[0:1, :], fincol[0:1, 5:6],
                                                 start=True, stop=True), "t0")
            W("vector", "t0", cb2)
            DV(lambda v, h=h: v.tensor_copy(abcol[:, h:h + 1], psA[:, 2:3]))
            prev_tok = DV(lambda v, h=h: v.tensor_copy(abcol[:, 4 + h:5 + h], psB[:, 2:3]), "v0")
            if rw != 0:
                W("scalar", "dg", c_ph3_dg)
            if dbg and h == 0:
                W("scalar", "dg", c_tap)
            W("scalar", "v0", prev_tok)
            for tq in range(NP):
                if tq > 0:
                    W("scalar", "v0", tok_gn[(h, tq - 1)])
                elif h > 0:
                    W("scalar", "v0", tok_gn[(h - 1, NP - 1)])
                sl = (slice(rw, rw + 64), slice(T * ch + 512 * tq, T * ch + 512 * (tq + 1)))
                csn = D("scalar", lambda s, sl=sl, h=h, rw=rw: s.activation(
                    ont[rw:rw + 64, :], og[sl], AF.Identity,
                    bias=abcol[rw:rw + 64, 4 + h:5 + h], scale=abcol[rw:rw + 64, h:h + 1]), "s0")
                W("vector", "s0", csn)
                DV(lambda v, h=h, rw=rw: v.tensor_scalar(
                    ont[rw:rw + 64, :], ont[rw:rw + 64, :], gampk[rw:rw + 64, h // 2:h // 2 + 1],
                    betpk[rw:rw + 64, h // 2:h // 2 + 1], ALU.mult, ALU.add))
                tok_gn[(h, tq)] = DV(lambda v, sl=sl, h=h, rw=rw, ch=ch, tq=tq: v.tensor_tensor(
                    og[sl], ont[rw:rw + 64, :], gf[sl], ALU.mult), "v0")
        c_gn = em.c["v0"]
        if stop < 5:
            W("sync", "ds", em.c["ds"]); W("gpsimd", "dg", em.c["dg"])
            W("vector", "v0", em.c["v0"]); W("scalar", "s0", em.c["s0"])
            W("tensor", "t0", em.c["t0"])
            return

        tok_zev = {}
        for ti in range(NT):
            for oh in range(2):
                bank = psA if oh == 0 else psB
                if ti >= 1:
                    if oh == 0:
                        W("tensor", "v0", tok_zev[ti - 1])
                    if ti >= 2 and oh == 0:
                        W("vector", "dz", 16 * (ti - 1))
                elif ti == 0 and oh == 0:
                    W("tensor", "v0", c_gn)
                for ch in range(2):
                    D("tensor", lambda t, ti=ti, oh=oh, ch=ch, bank=bank: t.matmul(
                        bank[:],
                        og[:, T * ch + 128 * ti: T * ch + 128 * (ti + 1)],
                        wo_b[:, 1024 * ch + 512 * oh: 1024 * ch + 512 * (oh + 1)],
                        start=(ch == 0), stop=(ch == 1)), "t0" if ch == 1 else None)
                cmm = em.c["t0"]
                W("vector", "t0", cmm)
                D("vector", lambda v, ti=ti, oh=oh, bank=bank: v.tensor_copy(
                    zev[:, 1024 * (ti % 2) + 512 * oh: 1024 * (ti % 2) + 512 * (oh + 1)], bank[:]),
                  "v0" if oh == 1 else None)
            tok_zev[ti] = em.c["v0"]
            W("sync", "v0", tok_zev[ti])
            D("sync", lambda sp, ti=ti: sp.dma_start(
                out=zout[128 * ti:128 * (ti + 1), :],
                in_=zev[:, 1024 * (ti % 2):1024 * (ti % 2 + 1)]), "dz", 16)

        if stop < 6:
            W("sync", "ds", em.c["ds"]); W("sync", "dz", em.c["dz"]); W("gpsimd", "dg", em.c["dg"])
            W("vector", "v0", em.c["v0"]); W("scalar", "s0", em.c["s0"])
            W("tensor", "t0", em.c["t0"])
            return
        if dbg:
            W("vector", "v0", c_gn)
            D("vector", lambda v: v.tensor_copy(dbm[:, 0:4], ldbc[:]))
            D("vector", lambda v: v.tensor_copy(dbm[:, 4:8], nbbc[:]))
            cvm = D("vector", lambda v: v.tensor_copy(dbm[:, 8:16], abcol[:]), "v0")
            W("sync", "v0", cvm)
            D("sync", lambda sp: sp.dma_start(out=dbg_o["t_m"], in_=dbm[:]), "dz", 16)

        W("sync", "dz", em.c["dz"])
        W("gpsimd", "dg", em.c["dg"])

    def divmod_prev(h, qt, kb):
        if qt > 0:
            return (h, qt - 1, kb)
        return (h - 1, NP - 1, kb)

    with nc.Block() as block:
        block.sync(lambda e: program(Em(nc, sems, "sync", e)))
        block.gpsimd(lambda e: program(Em(nc, sems, "gpsimd", e)))
        block.tensor(lambda e: program(Em(nc, sems, "tensor", e)))
        block.vector(lambda e: program(Em(nc, sems, "vector", e)))
        block.scalar(lambda e: program(Em(nc, sems, "scalar", e)))

    return nc


def l1_inputs(core, ins):
    b, g = core // 4, core % 4
    gs = slice(256 * g, 256 * (g + 1))
    hs = slice(4 * g, 4 * (g + 1))
    f32 = np.float32
    wq = np.asarray(ins["L0_attn_wq"], f32); wk = np.asarray(ins["L0_attn_wk"], f32)
    wv = np.asarray(ins["L0_attn_wv"], f32); wg = np.asarray(ins["L0_attn_wg"], f32)
    wo = np.asarray(ins["L0_attn_wo"], f32); wd = np.asarray(ins["L0_attn_wd"], f32)
    woT = np.ascontiguousarray(wo.T[gs, :])
    woTb = np.zeros((128, 2048), f32)
    for hh in range(4):
        woTb[64 * (hh % 2):64 * (hh % 2) + 64, 1024 * (hh // 2):1024 * (hh // 2) + 1024] = \
            woT[64 * hh:64 * (hh + 1), :]
    def fmaj(vec):
        return np.ascontiguousarray(np.asarray(vec, f32)[gs].reshape(2, 128).T)
    out = {
        "xb": np.ascontiguousarray(np.asarray(ins["L0_attn_x"], f32)[b]),
        "wqT": np.ascontiguousarray(wq.T[:, gs]),
        "wkT": np.ascontiguousarray(wk.T[:, gs]),
        "wvT": np.ascontiguousarray(wv.T[:, gs]),
        "wgT": np.ascontiguousarray(wg.T[:, gs]),
        "woTb": woTb,
        "wdT": np.ascontiguousarray(wd.T[:, hs]),
        "bq_f": fmaj(ins["L0_attn_bq"]),
        "bk_f": fmaj(ins["L0_attn_bk"]),
        "bg_f": fmaj(ins["L0_attn_bg"]),
        "bv_row": np.ascontiguousarray(np.asarray(ins["L0_attn_bv"], f32)[gs].reshape(1, 256)),
        "bd_col": np.ascontiguousarray(np.asarray(ins["L0_attn_bd"], f32)[hs].reshape(4, 1)),
        "gam_pk": fmaj(ins["L0_attn_gamma"]),
        "bet_pk": fmaj(ins["L0_attn_beta"]),
        "c_fmaj": np.ascontiguousarray(np.asarray(ins["L0_c"], f32)[b].reshape(8, 128).T),
        "wm_sc": np.ascontiguousarray(np.asarray(ins["L0_wmod"], f32)[:, 0:2048]),
    }
    return out


TS = 2048
E = 1024
HID = 2048
NE = 8
NSH = 2
EPS = 1e-5
KB = 1024
NTT = TS // 128


def build_l2(dbg=False, stop=99):
    nc = bass.Bass("TRN2", target_bir_lowering=False)
    CN = consts_np()

    def din(name, shape):
        return nc.dram_tensor(name, list(shape), F32, kind="ExternalInput").ap()

    zp4 = din("zp4", (4, TS, E))
    x_sl = din("x_sl", (TS, E))
    c_fmaj = din("c_fmaj", (128, 8))
    wm2 = din("wm2", (E, 4096))
    bo_row = din("bo_row", (1, E))
    routerT = din("routerT", (E, 8))
    eb_row = din("eb_row", (1, 8))
    w1T = din("w1T", (NE, E, HID))
    w3T = din("w3T", (NE, E, HID))
    w2T = din("w2T", (NE, HID, E))
    sw1T = din("sw1T", (NSH, E, HID))
    sw3T = din("sw3T", (NSH, E, HID))
    sw2T = din("sw2T", (NSH, HID, E))

    oout = nc.dram_tensor("oslice", [TS, E], F32, kind="ExternalOutput").ap()
    x1d = nc.dram_tensor("x1d", [TS, E], F32, kind="Internal").ap()
    dbg_o = {}
    if dbg:
        for nm, sh in [("d_h2", (128, 512)), ("d_cmb", (128, 16)), ("d_mod", (128, 16)),
                       ("d_x1", (128, 1024)), ("d_y", (128, 1024)), ("d_gb", (128, 1024)),
                       ("d_lg", (128, 16)), ("d_o0", (128, 1024)), ("d_xr0", (128, 1024))]:
            dbg_o[nm] = nc.dram_tensor(nm, list(sh), F32, kind="ExternalOutput").ap()

    ic = {k: nc.inline_tensor(np.ascontiguousarray(v), f"c2_{k}").ap()
          for k, v in CN.items() if k in ("ident", "ones_row")}

    A = lambda name, shape, dtype: nc.alloc_sbuf_tensor(name, list(shape), dtype, side="left")

    h2 = A("h2", (128, 8 * TS), BF16)
    yac = A("yac", (128, NTT * E), F32)
    Au = A("Au", (128, 16 * TS), BF16)
    cmb = A("cmb", (128, NTT * 8), F32)
    gbb = A("gbb", (128, E), F32)
    identt = A("identt", (128, 128), F32)
    onesr = A("onesr", (1, 128), F32)
    cfm = A("cfm", (128, 8), F32)
    silc = A("silc", (128, 8), F32)
    rt_b = A("rt_b", (128, 64), BF16)
    ebb = A("ebb", (128, 8), F32)
    lncol = A("lncol", (128, 16), F32)
    zcol = A("zcol", (128, 1), F32)
    ltile = A("ltile", (128, 8), F32)
    etile = A("etile", (128, 8), F32)
    ktile = A("ktile", (128, 8), F32)
    mcols = A("mcols", (128, 4), F32)
    dlg_t = A("dlg_t", (128, 16), F32)

    ARENA = 40 * KB
    ar0, _ = nc.bump_sbuf(ARENA, side="left")
    def AT(name, shape, dtype, off):
        return nc.alloc_sbuf_tensor_at(name, list(shape), dtype, offset=ar0 + off)
    wmt = AT("wmt", (128, 2048), F32, 0)
    modrow = AT("modrow", (1, 2048), F32, 8 * KB)
    gab = AT("gab", (128, E), F32, 16 * KB)
    bob = AT("bob", (128, E), F32, 20 * KB)
    capcb = AT("capcb", (128, E), F32, 24 * KB)
    sbb = AT("sbb", (128, E), F32, 28 * KB)
    zst = AT("zst", (128, 4096), F32, 0)
    xt1 = AT("xt1", (128, 1024), F32, 32 * KB)
    sqt = AT("sqt", (128, 1024), F32, 36 * KB)
    w2f = AT("w2f", (128, 16 * 1024), BF16, 0)
    w1r = AT("w1r", (128, 2 * 1024), BF16, 32 * KB)
    w3r = AT("w3r", (128, 2 * 1024), BF16, 36 * KB)
    outr = AT("outr", (128, 2048), F32, 0)
    x1rd = AT("x1rd", (128, 2048), F32, 8 * KB)

    psA = nc.alloc_psum_tensor("psA", [128, 512], F32)
    psB = nc.alloc_psum_tensor("psB", [128, 512], F32)
    psC = nc.alloc_psum_tensor("psC", [128, 512], F32)
    psD = nc.alloc_psum_tensor("psD", [128, 512], F32)
    psE = nc.alloc_psum_tensor("psE", [128, 8], F32)

    sems = {k: nc.alloc_semaphore(k) for k in ["ds", "dg", "dz", "t0", "v0", "s0", "vv", "sv2"]}

    def program(em: Em):
        W, D = em.wait, em.do

        def DV(fn, inc=None):
            sem = inc or "vv"
            tok = D("vector", fn, sem)
            W("vector", sem, tok)
            return tok

        def DS(fn, inc=None):
            sem = inc or "sv2"
            tok = D("scalar", fn, sem)
            W("scalar", sem, tok)
            return tok

        for nm, tile in [("ident", identt), ("ones_row", onesr)]:
            D("sync", lambda sp, nm=nm, tile=tile: sp.dma_start(out=tile[:], in_=ic[nm]), "ds", 16)
        D("sync", lambda sp: sp.dma_start(out=cfm[:], in_=c_fmaj), "ds", 16)
        c_consts = em.c["ds"]
        D("gpsimd", lambda gp: gp.dma_start(
            out=rt_b[:].rearrange("p (fc o) -> p fc o", fc=8),
            in_=routerT.rearrange("(fc p) o -> p fc o", p=128)), "dg", 16)
        c_rt = em.c["dg"]

        D("vector", lambda v: v.memset(zcol[:], 0.0))
        D("vector", lambda v: v.memset(yac[:], 0.0), "v0")
        cv_y0 = em.c["v0"]
        W("scalar", "ds", c_consts)
        cs_sil = D("scalar", lambda s: s.activation(silc[:], cfm[:], AF.Silu, bias=zcol[:]), "s0")

        mod_ps = [psA, psB, psC, psD]
        cv_h = None
        for rnd in range(2):
            for fc in range(8):
                cw = D("sync", lambda sp, rnd=rnd, fc=fc: sp.dma_start(
                    out=wmt[:], in_=wm2[128 * fc:128 * (fc + 1), 2048 * rnd:2048 * (rnd + 1)]), "ds", 16)
                W("tensor", "ds", cw)
                if rnd == 0 and fc == 0:
                    W("tensor", "s0", cs_sil)
                for half in range(4):
                    D("tensor", lambda t, fc=fc, half=half: t.matmul(
                        mod_ps[half][0:1, :], silc[:, fc:fc + 1], wmt[:, 512 * half:512 * (half + 1)],
                        start=(fc == 0), stop=(fc == 7)), "t0" if half == 3 else None)
                if fc < 7:
                    W("sync", "t0", em.c["t0"])
            cmod = em.c["t0"]
            W("vector", "t0", cmod)
            for half in range(4):
                DV(lambda v, half=half: v.tensor_copy(
                    modrow[0:1, 512 * half:512 * (half + 1)], mod_ps[half][0:1, :]))
            clamp_base = 0 if rnd == 0 else 1024
            scr1 = gab[0:1, :] if rnd == 0 else bob[0:1, :]
            scr2 = bob[0:1, :] if rnd == 0 else capcb[0:1, :]
            scr3 = capcb[0:1, :] if rnd == 0 else gbb[0:1, :]
            DV(lambda v, b=clamp_base, scr1=scr1: v.tensor_scalar(
                scr1, modrow[0:1, b:b + E], 0.25, None, ALU.max))
            DV(lambda v, b=clamp_base, scr2=scr2: v.tensor_scalar(
                scr2, modrow[0:1, b:b + E], -0.25, None, ALU.min))
            DV(lambda v, b=clamp_base, scr3=scr3: v.tensor_scalar(
                scr3, modrow[0:1, b:b + E], 0.0, None, ALU.is_ge))
            DV(lambda v, scr1=scr1, scr2=scr2: v.tensor_tensor(scr1, scr1, scr2, ALU.subtract))
            DV(lambda v, scr1=scr1, scr3=scr3: v.tensor_tensor(scr1, scr3, scr1, ALU.mult))
            DV(lambda v, b=clamp_base, scr1=scr1, scr2=scr2: v.tensor_tensor(
                modrow[0:1, b:b + E], scr2, scr1, ALU.add))
            if rnd == 1:
                DV(lambda v: v.tensor_scalar(
                    modrow[0:1, 0:E], modrow[0:1, 0:E], 1.0, 1.0, ALU.mult, ALU.add))
            cv_mod = DV(lambda v: v.tensor_copy(lncol[0:1, 1:2], modrow[0:1, 0:1]), "v0")
            W("tensor", "v0", cv_mod)
            dsts = ((gab, 0), (sbb, 1024)) if rnd == 0 else ((capcb, 0), (gbb, 1024))
            for dst, lo in dsts:
                for half in range(2):
                    cbt = D("tensor", lambda t, lo=lo, half=half: t.matmul(
                        psA[:], onesr[0:1, :], modrow[0:1, lo + 512 * half:lo + 512 * (half + 1)],
                        start=True, stop=True), "t0")
                    W("vector", "t0", cbt)
                    cvh = DV(lambda v, dst=dst, half=half: v.tensor_copy(
                        dst[:, 512 * half:512 * (half + 1)], psA[:]), "v0")
                    W("tensor", "v0", cvh)
            if rnd == 0:
                W("sync", "v0", em.c["v0"])

        W("sync", "v0", em.c["v0"])
        cbo = D("sync", lambda sp: sp.dma_start(out=modrow[0:1, 0:E], in_=bo_row), "ds", 16)
        ceb = D("sync", lambda sp: sp.dma_start(out=modrow[0:1, E:E + 8], in_=eb_row), "ds", 16)
        W("tensor", "ds", ceb)
        for dst, src, n in ((bob, modrow[0:1, 0:E], 2), (ebb, modrow[0:1, E:E + 8], 1)):
            for half in range(n):
                wsl = 512 * half
                wid = 512 if dst is bob else 8
                cbt = D("tensor", lambda t, src=src, wsl=wsl, wid=wid: t.matmul(
                    psA[:, 0:wid], onesr[0:1, :], src[0:1, wsl:wsl + wid],
                    start=True, stop=True), "t0")
                W("vector", "t0", cbt)
                cvh = DV(lambda v, dst=dst, wsl=wsl, wid=wid: v.tensor_copy(
                    dst[:, wsl:wsl + wid], psA[:, 0:wid]), "v0")
                W("tensor", "v0", cvh)
        c_s0v = em.c["v0"]
        c_s0t = em.c["t0"]

        tok_z, tok_x = {}, {}
        tok_h2 = {}
        tok_nmm = {}

        def zload(ti):
            toks = []
            for j in range(4):
                toks.append(D("sync", lambda sp, ti=ti, j=j: sp.dma_start(
                    out=zst[:, 1024 * j:1024 * (j + 1)],
                    in_=zp4[j, 128 * ti:128 * (ti + 1), :]), "ds", 16))
            toks.append(D("sync", lambda sp, ti=ti: sp.dma_start(
                out=xt1[:], in_=x_sl[128 * ti:128 * (ti + 1), :]), "ds", 16))
            return toks[-1]

        tok_h2tm = {}
        tok_lt0 = {}
        for ti in range(NTT):
            if ti > 0:
                W("sync", "v0", tok_h2tm[ti - 1])
                W("sync", "dz", 16 * ti)
            else:
                W("sync", "t0", c_s0t)
            tok_x[ti] = zload(ti)
            W("vector", "ds", tok_x[ti])
            if ti == 0:
                W("vector", "v0", c_s0v)
            DV(lambda v: v.tensor_tensor(sqt[:], zst[:, 0:1024], zst[:, 1024:2048], ALU.add))
            DV(lambda v: v.tensor_tensor(zst[:, 0:1024], zst[:, 2048:3072], zst[:, 3072:4096], ALU.add))
            DV(lambda v: v.tensor_tensor(sqt[:], sqt[:], zst[:, 0:1024], ALU.add))
            DV(lambda v: v.tensor_tensor(sqt[:], sqt[:], bob[:], ALU.add))
            DV(lambda v: v.tensor_tensor(sqt[:], sqt[:], gab[:], ALU.mult))
            cx1 = DV(lambda v: v.tensor_tensor(xt1[:], xt1[:], sqt[:], ALU.add), "v0")
            W("sync", "v0", cx1)
            D("sync", lambda sp, ti=ti: sp.dma_start(
                out=x1d[128 * ti:128 * (ti + 1), :], in_=xt1[:]), "dz", 16)
            DV(lambda v: v.reduce_sum(lncol[:, 0:1], xt1[:], axis=AXX))
            if tok_nmm.get("last"):
                W("scalar", "t0", tok_nmm["last"])
            W("scalar", "v0", cx1)
            csq = D("scalar", lambda s: s.activation(sqt[:], xt1[:], AF.Square, bias=zcol[:]), "s0")
            W("vector", "s0", csq)
            DV(lambda v: v.reduce_sum(lncol[:, 1:2], sqt[:], axis=AXX))
            DV(lambda v: v.tensor_scalar(lncol[:, 2:3], lncol[:, 0:1], 1.0 / E, None, ALU.mult))
            DV(lambda v: v.tensor_scalar(lncol[:, 3:4], lncol[:, 1:2], 1.0 / E, None, ALU.mult))
            DV(lambda v: v.tensor_tensor(lncol[:, 4:5], lncol[:, 2:3], lncol[:, 2:3], ALU.mult))
            DV(lambda v: v.tensor_scalar(lncol[:, 4:5], lncol[:, 3:4], lncol[:, 4:5], None, ALU.subtract))
            DV(lambda v: v.tensor_scalar(lncol[:, 4:5], lncol[:, 4:5], 1.0, EPS, ALU.mult, ALU.add))
            cvv = DV(lambda v: v.reciprocal(lncol[:, 5:6], lncol[:, 4:5]), "v0")
            W("scalar", "v0", cvv)
            cs2 = D("scalar", lambda s: s.activation(lncol[:, 6:7], lncol[:, 5:6], AF.Sqrt, bias=zcol[:]), "s0")
            W("vector", "s0", cs2)
            DV(lambda v: v.tensor_scalar(lncol[:, 7:8], lncol[:, 2:3], lncol[:, 6:7], None, ALU.mult))
            DV(lambda v: v.tensor_scalar(lncol[:, 7:8], lncol[:, 7:8], -1.0, None, ALU.mult))
            DV(lambda v: v.tensor_scalar(sqt[:], xt1[:], lncol[:, 6:7], lncol[:, 7:8], ALU.mult, ALU.add))
            DV(lambda v: v.tensor_tensor(sqt[:], sqt[:], capcb[:], ALU.mult))
            tok_h2tm[ti] = DV(lambda v: v.tensor_tensor(sqt[:], sqt[:], sbb[:], ALU.add), "v0")
            W("tensor", "v0", tok_h2tm[ti])
            if ti > 0:
                W("tensor", "v0", tok_lt0[ti - 1])
            for fc in range(8):
                bank = psA if fc < 4 else psB
                D("tensor", lambda t, fc=fc, bank=bank: t.transpose(
                    bank[:, 128 * (fc % 4):128 * (fc % 4 + 1)],
                    sqt[:].rearrange("p (fc q) -> p fc q", fc=8)[:, fc, :],
                    identt[:]), "t0" if fc in (3, 7) else None)
            tok_tr = em.c["t0"]
            W("vector", "t0", tok_tr)
            h23 = h2[:].rearrange("p (fc t) -> p fc t", fc=8)
            D("vector", lambda v, ti=ti, h23=h23: v.tensor_copy(
                h23[:, 0:4, 128 * ti:128 * (ti + 1)],
                psA[:].rearrange("p (fc q) -> p fc q", fc=4)))
            tok_h2[ti] = D("vector", lambda v, ti=ti, h23=h23: v.tensor_copy(
                h23[:, 4:8, 128 * ti:128 * (ti + 1)],
                psB[:].rearrange("p (fc q) -> p fc q", fc=4)), "v0")
            W("tensor", "v0", tok_h2[ti])
            if ti == 0:
                W("tensor", "dg", c_rt)
            for fc in range(8):
                D("tensor", lambda t, ti=ti, fc=fc: t.matmul(
                    psE[:], h2[:, TS * fc + 128 * ti: TS * fc + 128 * (ti + 1)],
                    rt_b[:, 8 * fc:8 * (fc + 1)],
                    start=(fc == 0), stop=(fc == 7)), "t0" if fc == 7 else None)
            tok_nmm["last"] = em.c["t0"]
            W("vector", "t0", tok_nmm["last"])
            tok_lt = DV(lambda v: v.tensor_tensor(ltile[:], psE[:], ebb[:], ALU.add), "v0")
            DV(lambda v: v.reduce_max(mcols[:, 0:1], ltile[:], axis=AXX))
            DV(lambda v: v.tensor_scalar(mcols[:, 1:2], mcols[:, 0:1], -1.0, None, ALU.mult))
            DV(lambda v: v.tensor_scalar(ktile[:], ltile[:], mcols[:, 0:1], None, ALU.is_ge))
            DV(lambda v: v.scalar_tensor_tensor(etile[:], ktile[:], -1e30, ltile[:], ALU.mult, ALU.add))
            DV(lambda v: v.reduce_max(mcols[:, 2:3], etile[:], axis=AXX))
            cvk = DV(lambda v: v.tensor_scalar(ktile[:], ltile[:], mcols[:, 2:3], None, ALU.is_ge), "v0")
            W("scalar", "v0", cvk)
            cse = DS(lambda s: s.activation(etile[:], ltile[:], AF.Exp, bias=mcols[:, 1:2]), "s0")
            W("vector", "s0", cse)
            DV(lambda v: v.tensor_tensor(etile[:], etile[:], ktile[:], ALU.mult))
            DV(lambda v: v.reduce_sum(mcols[:, 3:4], etile[:], axis=AXX))
            DV(lambda v: v.reciprocal(mcols[:, 1:2], mcols[:, 3:4]))
            tok_lt0[ti] = tok_lt
            if dbg and ti == 0:
                DV(lambda v: v.tensor_copy(dlg_t[:, 0:8], ltile[:]))
                DV(lambda v: v.tensor_copy(dlg_t[:, 8:16], ktile[:]))
            tok_cmb = DV(lambda v, ti=ti: v.tensor_scalar(
                cmb[:, 8 * ti:8 * (ti + 1)], etile[:], mcols[:, 1:2], None, ALU.mult), "v0")
        c_s1v, c_s1t = em.c["v0"], em.c["t0"]

        if dbg:
            W("gpsimd", "v0", c_s1v)
            W("gpsimd", "dz", em.c["dz"])
            D("gpsimd", lambda gp: gp.dma_start(out=dbg_o["d_h2"], in_=h2[:, 0:512]), "dg", 16)
            D("gpsimd", lambda gp: gp.dma_start(out=dbg_o["d_gb"], in_=gbb[:]), "dg", 16)
            D("gpsimd", lambda gp: gp.dma_start(out=dbg_o["d_lg"], in_=dlg_t[:]), "dg", 16)
            D("gpsimd", lambda gp: gp.dma_start(out=dbg_o["d_cmb"], in_=cmb[:, 0:16]), "dg", 16)
            D("gpsimd", lambda gp: gp.dma_start(out=dbg_o["d_x1"], in_=x1d[0:128, :]), "dg", 16)

        if stop < 2:
            W("sync", "ds", em.c["ds"]); W("sync", "dz", em.c["dz"])
            W("gpsimd", "dg", em.c["dg"])
            W("vector", "v0", em.c["v0"]); W("scalar", "s0", em.c["s0"])
            W("tensor", "t0", em.c["t0"])
            return

        def wcol_load(slot, src, e, ht, ring):
            return D("gpsimd", lambda gp, slot=slot, src=src, e=e, ht=ht, ring=ring: gp.dma_start(
                out=ring[:, 1024 * slot:1024 * (slot + 1)].rearrange("p (fc h) -> p fc h", fc=8),
                in_=src[e, :, 128 * ht:128 * (ht + 1)].rearrange("(fc p) h -> p fc h", p=128)),
                "dg", 16)

        def w2_load(src, e, hc):
            return D("gpsimd", lambda gp, src=src, e=e, hc=hc: gp.dma_start(
                out=w2f[:, 1024 * hc:1024 * (hc + 1)],
                in_=src[e, 128 * hc:128 * (hc + 1), :]), "dg", 16)

        units = [(w1T, w3T, w2T, e, "r") for e in range(NE)] + \
                [(sw1T, sw3T, sw2T, e, "s") for e in range(NSH)]

        sil_list = []
        u_list = []
        yev_list = []
        w1_toks, w3_toks, w2_toks = {}, {}, {}
        w1_mmdone, w3_mmdone = {}, {}
        W("gpsimd", "v0", c_s1v)
        w1_toks[(0, 0)] = wcol_load(0, w1T, 0, 0, w1r)
        w3_toks[(0, 0)] = wcol_load(0, w3T, 0, 0, w3r)

        for ui, (s1, s3, s2, e, kind) in enumerate(units):
            for ht in range(16):
                slot = ht % 2
                nxt = (ui, ht + 1) if ht + 1 < 16 else ((ui + 1, 0) if ui + 1 < len(units) else None)
                if nxt is not None:
                    prev_same_slot = (ui, ht - 1) if ht >= 1 else (ui - 1, 15)
                    if prev_same_slot in w1_mmdone:
                        W("gpsimd", "t0", w1_mmdone[prev_same_slot])
                    srcn = s1 if nxt[0] == ui else units[nxt[0]][0]
                    en = e if nxt[0] == ui else units[nxt[0]][3]
                    w1_toks[nxt] = wcol_load((ht + 1) % 2, srcn, en, nxt[1], w1r)
                W("tensor", "dg", w1_toks[(ui, ht)])
                if ui == 0 and ht == 0:
                    W("tensor", "v0", c_s1v)
                for tq in range(4):
                    bank = psA if tq % 2 == 0 else psB
                    if ht == 0 and tq <= 1 and ui > 0:
                        W("tensor", "v0", yev_list[-1])
                    if len(sil_list) >= 2:
                        W("tensor", "s0", sil_list[-2])
                    for fc in range(8):
                        D("tensor", lambda t, ht=ht, tq=tq, fc=fc, bank=bank, slot=slot: t.matmul(
                            bank[:], w1r[:, 1024 * slot + 128 * fc:1024 * slot + 128 * (fc + 1)],
                            h2[:, TS * fc + 512 * tq:TS * fc + 512 * (tq + 1)],
                            start=(fc == 0), stop=(fc == 7)), "t0" if fc == 7 else None)
                    cmm = em.c["t0"]
                    W("scalar", "t0", cmm)
                    sil_list.append(D("scalar", lambda s, ht=ht, tq=tq, bank=bank: s.activation(
                        Au[:, TS * ht + 512 * tq: TS * ht + 512 * (tq + 1)], bank[:], AF.Silu,
                        bias=zcol[:]), "s0"))
                w1_mmdone[(ui, ht)] = em.c["t0"]
            c_sil_done = sil_list[-1]
            for ht in range(16):
                slot = ht % 2
                nxt = (ui, ht + 1) if ht + 1 < 16 else ((ui + 1, 0) if ui + 1 < len(units) else None)
                if nxt is not None:
                    prev_same_slot = (ui, ht - 1) if ht >= 1 else (ui - 1, 15)
                    if prev_same_slot in w3_mmdone:
                        W("gpsimd", "t0", w3_mmdone[prev_same_slot])
                    srcn = s3 if nxt[0] == ui else units[nxt[0]][1]
                    en = e if nxt[0] == ui else units[nxt[0]][3]
                    w3_toks[nxt] = wcol_load((ht + 1) % 2, srcn, en, nxt[1], w3r)
                W("tensor", "dg", w3_toks[(ui, ht)])
                for tq in range(4):
                    bank = psA if tq % 2 == 0 else psB
                    if len(u_list) >= 2:
                        W("tensor", "v0", u_list[-2])
                    elif ui == 0 and ht == 0 and tq == 0:
                        W("tensor", "s0", c_sil_done)
                    for fc in range(8):
                        D("tensor", lambda t, ht=ht, tq=tq, fc=fc, bank=bank, slot=slot: t.matmul(
                            bank[:], w3r[:, 1024 * slot + 128 * fc:1024 * slot + 128 * (fc + 1)],
                            h2[:, TS * fc + 512 * tq:TS * fc + 512 * (tq + 1)],
                            start=(fc == 0), stop=(fc == 7)), "t0" if fc == 7 else None)
                    cmm = em.c["t0"]
                    W("vector", "t0", cmm)
                    sidx = 64 * ui + 4 * ht + tq
                    W("vector", "s0", sil_list[sidx])
                    u_list.append(D("vector", lambda v, ht=ht, tq=tq, bank=bank: v.tensor_tensor(
                        Au[:, TS * ht + 512 * tq: TS * ht + 512 * (tq + 1)],
                        bank[:],
                        Au[:, TS * ht + 512 * tq: TS * ht + 512 * (tq + 1)], ALU.mult), "v0"))
                w3_mmdone[(ui, ht)] = em.c["t0"]
            c_u_done = u_list[-1]
            for hc in range(16):
                if ui > 0 and hc == 0:
                    W("gpsimd", "t0", w2_mmdone)
                w2_toks[(ui, hc)] = w2_load(s2, e, hc)
            W("tensor", "dg", w2_toks[(ui, 15)])
            W("tensor", "v0", c_u_done)
            for tq in range(16):
                for dh in range(2):
                    bank = psA if (2 * tq + dh) % 2 == 0 else psB
                    if len(yev_list) >= 2:
                        W("tensor", "v0", yev_list[-2])
                    for hc in range(16):
                        D("tensor", lambda t, tq=tq, dh=dh, hc=hc, bank=bank: t.matmul(
                            bank[:], Au[:, TS * hc + 128 * tq: TS * hc + 128 * (tq + 1)],
                            w2f[:, 1024 * hc + 512 * dh: 1024 * hc + 512 * (dh + 1)],
                            start=(hc == 0), stop=(hc == 15)), "t0" if hc == 15 else None)
                    cmm = em.c["t0"]
                    W("vector", "t0", cmm)
                    if kind == "r":
                        yev_list.append(D("vector", lambda v, tq=tq, dh=dh, e=e, bank=bank:
                                          v.scalar_tensor_tensor(
                            yac[:, 1024 * tq + 512 * dh: 1024 * tq + 512 * (dh + 1)],
                            bank[:], cmb[:, 8 * tq + e:8 * tq + e + 1],
                            yac[:, 1024 * tq + 512 * dh: 1024 * tq + 512 * (dh + 1)],
                            ALU.mult, ALU.add), "v0"))
                    else:
                        yev_list.append(D("vector", lambda v, tq=tq, dh=dh, bank=bank:
                                          v.scalar_tensor_tensor(
                            yac[:, 1024 * tq + 512 * dh: 1024 * tq + 512 * (dh + 1)],
                            bank[:], 1.0,
                            yac[:, 1024 * tq + 512 * dh: 1024 * tq + 512 * (dh + 1)],
                            ALU.mult, ALU.add), "v0"))
            w2_mmdone = em.c["t0"]
        c_s2v = em.c["v0"]

        tok_out = {}
        for ti in range(NTT):
            if ti >= 2:
                W("sync", "v0", tok_out[ti - 2])
            elif ti == 0:
                W("sync", "dg", em.c["dg"])
                W("sync", "t0", em.c["t0"])
            cx = D("sync", lambda sp, ti=ti: sp.dma_start(
                out=x1rd[:, 1024 * (ti % 2):1024 * (ti % 2 + 1)],
                in_=x1d[128 * ti:128 * (ti + 1), :]), "ds", 16)
            W("vector", "ds", cx)
            if ti == 0:
                W("vector", "v0", c_s2v)
            if ti >= 2:
                W("vector", "dz", 16 * (NTT + ti - 1))
            DV(lambda v, ti=ti: v.tensor_tensor(
                outr[:, 1024 * (ti % 2):1024 * (ti % 2 + 1)],
                yac[:, 1024 * ti:1024 * (ti + 1)], gbb[:], ALU.mult))
            tok_out[ti] = DV(lambda v, ti=ti: v.tensor_tensor(
                outr[:, 1024 * (ti % 2):1024 * (ti % 2 + 1)],
                outr[:, 1024 * (ti % 2):1024 * (ti % 2 + 1)],
                x1rd[:, 1024 * (ti % 2):1024 * (ti % 2 + 1)], ALU.add), "v0")
            W("sync", "v0", tok_out[ti])
            if dbg and ti == 0:
                W("gpsimd", "v0", tok_out[0])
                D("gpsimd", lambda gp: gp.dma_start(out=dbg_o["d_o0"], in_=outr[:, 0:1024]), "dg", 16)
                D("gpsimd", lambda gp: gp.dma_start(out=dbg_o["d_xr0"], in_=x1rd[:, 0:1024]), "dg", 16)
                W("sync", "dg", em.c["dg"])
            D("sync", lambda sp, ti=ti: sp.dma_start(
                out=oout[128 * ti:128 * (ti + 1), :],
                in_=outr[:, 1024 * (ti % 2):1024 * (ti % 2 + 1)]), "dz", 16)
        if dbg:
            W("gpsimd", "v0", em.c["v0"])
            D("gpsimd", lambda gp: gp.dma_start(out=dbg_o["d_y"], in_=yac[:, 0:1024]), "dg", 16)
            cm2 = D("gpsimd", lambda gp: gp.dma_start(out=dbg_o["d_mod"][:, 0:4], in_=mcols[:, 0:4]), "dg", 16)

        W("sync", "dz", em.c["dz"])
        W("gpsimd", "dg", em.c["dg"])

    with nc.Block() as block:
        block.sync(lambda en: program(Em(nc, sems, "sync", en)))
        block.gpsimd(lambda en: program(Em(nc, sems, "gpsimd", en)))
        block.tensor(lambda en: program(Em(nc, sems, "tensor", en)))
        block.vector(lambda en: program(Em(nc, sems, "vector", en)))
        block.scalar(lambda en: program(Em(nc, sems, "scalar", en)))
    return nc


def l2_inputs(core, ins, zparts):
    b, q = core // 4, core % 4
    f32 = np.float32
    sl = slice(2048 * q, 2048 * (q + 1))
    zp4 = np.stack([np.asarray(zparts[4 * b + j][sl], f32) for j in range(4)], axis=0)
    tr = lambda w: np.ascontiguousarray(np.asarray(w, f32).transpose(0, 2, 1))
    return {
        "zp4": np.ascontiguousarray(zp4),
        "x_sl": np.ascontiguousarray(np.asarray(ins["L0_attn_x"], f32)[b, sl]),
        "c_fmaj": np.ascontiguousarray(np.asarray(ins["L0_c"], f32)[b].reshape(8, 128).T),
        "wm2": np.ascontiguousarray(np.asarray(ins["L0_wmod"], f32)[:, 2048:6144]),
        "bo_row": np.ascontiguousarray(np.asarray(ins["L0_attn_bo"], f32).reshape(1, E)),
        "routerT": np.ascontiguousarray(np.asarray(ins["L0_ffn_router_w"], f32).T),
        "eb_row": np.ascontiguousarray(np.asarray(ins["L0_ffn_expert_bias"], f32).reshape(1, 8)),
        "w1T": tr(ins["L0_ffn_w1"]),
        "w3T": tr(ins["L0_ffn_w3"]),
        "w2T": tr(ins["L0_ffn_w2"]),
        "sw1T": tr(ins["L0_ffn_sw1"]),
        "sw3T": tr(ins["L0_ffn_sw3"]),
        "sw2T": tr(ins["L0_ffn_sw2"]),
    }


from concourse.bass_utils import run_bass_kernel_spmd

_CACHE = {}

def kernel(**inputs):
    if "l1" not in _CACHE:
        _CACHE["l1"] = build_l1(dbg=False)
    nc1 = _CACHE["l1"]
    im1 = [l1_inputs(c, inputs) for c in range(8)]
    r1 = run_bass_kernel_spmd(nc1, im1, core_ids=list(range(8)))
    zparts = [r1.results[c]["zpart"] for c in range(8)]

    if "l2" not in _CACHE:
        _CACHE["l2"] = build_l2(dbg=False)
    nc2 = _CACHE["l2"]
    im2 = [l2_inputs(c, inputs, zparts) for c in range(8)]
    r2 = run_bass_kernel_spmd(nc2, im2, core_ids=list(range(8)))

    out = np.empty((2, 8192, 1024), dtype=np.float32)
    for c in range(8):
        b, q = c // 4, c % 4
        out[b, 2048 * q:2048 * (q + 1), :] = r2.results[c]["oslice"]
    return out


# version 03
# speedup vs baseline: 1.5346x; 1.0865x over previous
import numpy as np
import concourse.bass as bass
import concourse.mybir as mybir

F32 = mybir.dt.float32
BF16 = mybir.dt.bfloat16
FP16 = mybir.dt.float16
I32 = mybir.dt.int32
AF = mybir.ActivationFunctionType
ALU = mybir.AluOpType
AXX = mybir.AxisListType.X

PI = float(np.pi)
LN2 = float(np.log(2.0))
LN1E4 = float(np.log(10000.0))

T = 8192
E = 1024
NH = 4
HD = 64
NP = 16
NT = T // 128
EPS = 1e-5
KB = 1024


def consts_np():
    perm = np.zeros((128, 128), dtype=np.float32)
    for i in range(64):
        perm[2 * i, 2 * i + 1] = -1.0
        perm[2 * i + 1, 2 * i] = 1.0
    p_ = np.arange(128).reshape(128, 1)
    n_ = np.arange(512).reshape(1, 512)
    return {
        "ident": np.eye(128, dtype=np.float32),
        "permT": perm.T.copy(),
        "ones_row": np.ones((1, 128), dtype=np.float32),
        "ones_col": np.ones((128, 1), dtype=np.float32),
        "pmod32": (np.arange(128) % 32).astype(np.float32).reshape(128, 1),
        "pk2d": (p_ + 128.0 * np.arange(64).reshape(1, 64)).astype(np.float32),
        "tki": np.broadcast_to(np.arange(512, dtype=np.float32), (128, 512)).copy(),
        "drelu": np.concatenate(
            [np.maximum(p_ - n_ + c, 0).astype(np.float16) for c in (0, 128, 256, 384)],
            axis=1),
        "ospl": np.concatenate([np.concatenate([np.ones((64, 1)), np.zeros((64, 1))]),
                                np.concatenate([np.zeros((64, 1)), np.ones((64, 1))])],
                               axis=1).astype(np.float32),
        "eh4": np.concatenate([np.where(np.arange(4).reshape(4, 1) == h,
                                        np.ones((4, 128)), np.zeros((4, 128)))
                               for h in range(4)], axis=1).astype(np.float32),
    }


class Em:
    def __init__(self, nc, sems, target, engobj):
        self.nc, self.sems, self.target, self.engobj = nc, sems, target, engobj
        self.c = {k: 0 for k in sems}

    def do(self, eng, fn, inc=None, n=1):
        if self.target == eng:
            inst = fn(self.engobj)
            if inc is not None:
                inst.then_inc(self.sems[inc], n)
        if inc is not None:
            self.c[inc] += n
            return self.c[inc]
        return None

    def wait(self, eng, sem, val):
        assert val <= self.c[sem], (sem, val, self.c[sem])
        if self.target == eng:
            self.engobj.wait_ge(self.sems[sem], val)

    def _bump(self, sem):
        self.c[sem] += 1
        return self.c[sem]


def build_l1(dbg=False, stop=99):
    nc = bass.Bass("TRN2", target_bir_lowering=False)
    CN = consts_np()

    def din(name, shape):
        return nc.dram_tensor(name, list(shape), F32, kind="ExternalInput").ap()

    xb = din("xb", (T, E))
    wqT = din("wqT", (E, 256)); wkT = din("wkT", (E, 256))
    wvT = din("wvT", (E, 256)); wgT = din("wgT", (E, 256))
    woTb = din("woTb", (128, 2048))
    wdT = din("wdT", (E, 4))
    bq_f = din("bq_f", (128, 2)); bk_f = din("bk_f", (128, 2)); bg_f = din("bg_f", (128, 2))
    bv_row = din("bv_row", (1, 256))
    bd_col = din("bd_col", (4, 1))
    gam_pk = din("gam_pk", (128, 2)); bet_pk = din("bet_pk", (128, 2))
    c_fmaj = din("c_fmaj", (128, 8))
    wm_sc = din("wm_sc", (E, 2048))

    zout = nc.dram_tensor("zpart", [T, E], F32, kind="ExternalOutput").ap()
    dbg_o = {}
    if dbg:
        for nm, sh in [("t_q", (128, 512)), ("t_k", (128, 512)), ("t_v", (128, 260)),
                       ("t_g", (128, 512)), ("t_h", (128, 512)), ("t_og", (128, 512)),
                       ("t_m", (128, 16))]:
            dbg_o[nm] = nc.dram_tensor(nm, list(sh), F32, kind="ExternalOutput").ap()

    ic = {k: nc.inline_tensor(np.ascontiguousarray(v), f"c_{k}").ap() for k, v in CN.items()}

    A = lambda name, shape, dtype: nc.alloc_sbuf_tensor(name, list(shape), dtype, side="left")

    Vp = A("Vp", (128, NT * 260), BF16)
    gf = A("gf", (128, 2 * T), BF16)
    wq_b = A("wq_b", (128, 2048), BF16); wk_b = A("wk_b", (128, 2048), BF16)
    wv_b = A("wv_b", (128, 2048), BF16); wg_b = A("wg_b", (128, 2048), BF16)
    wo_b = A("wo_b", (128, 2048), BF16); wd_b = A("wd_b", (128, 32), BF16)
    identt = A("identt", (128, 128), F32)
    permt = A("permt", (128, 128), F32)
    onesr = A("onesr", (1, 128), F32)
    onesc = A("onesc", (128, 1), F32)
    pmod32t = A("pmod32t", (128, 1), F32)
    pk2dt = A("pk2dt", (128, 64), F32)
    tkit = A("tkit", (128, 512), F32)
    osplt = A("osplt", (128, 2), F32)
    eh4t = A("eh4t", (4, 512), F32)
    bqf = A("bqf", (128, 2), F32); bkf = A("bkf", (128, 2), F32); bgf = A("bgf", (128, 2), F32)
    gampk = A("gampk", (128, 2), F32); betpk = A("betpk", (128, 2), F32)
    bdc = A("bdc", (4, 1), F32)
    cfm = A("cfm", (128, 8), F32)
    fcol = A("fcol", (128, 1), F32)
    silc = A("silc", (128, 8), F32)
    hsum = A("hsum", (128, 8), F32)
    hmean_b = A("hmean_b", (128, 8), BF16)
    bv_bc = A("bv_bc", (128, 256), F32)
    pdld = A("pdld", (4, 4), F32)
    ldbc = A("ldbc", (128, 4), F32)
    nbbc = A("nbbc", (128, 4), F32)
    ldkb = A("ldkb", (128, 4 * 64), F32)
    n2q = A("n2q", (2, 8), F32)
    n2k = A("n2k", (2, 8), F32)
    n2t = A("n2t", (2, 2), F32)
    s1a = A("s1a", (128, 4), F32)
    s2a = A("s2a", (128, 4), F32)
    abcol = A("abcol", (128, 8), F32)
    ogscr = A("ogscr", (64, 512), BF16)
    sqt = A("sqt", (128, 1024), F32)
    lncol = A("lncol", (128, 16), F32)
    zcol = A("zcol", (128, 1), F32)
    zc4 = A("zc4", (4, 1), F32)
    c5l2 = A("c5l2", (4, 1), F32)
    b2c = A("b2c", (2, 4), F32)
    bnd2 = A("bnd2", (2, 2), F32)
    tofff = A("tofff", (128, 1), F32)
    negld = A("negld", (128, 1), F32)
    nlq = A("nlq", (128, 1), F32)
    Qf = A("Qf", (128, 2 * T), BF16)
    kf0, _ = nc.bump_sbuf(32 * KB, side="left")
    Kf = nc.alloc_sbuf_tensor_at("Kf", [128, 2 * T], BF16, offset=kf0)
    if dbg:
        dbm = A("dbm", (128, 16), F32)

    ARENA = 45 * KB
    ar0, _ = nc.bump_sbuf(ARENA, side="left")
    def AT(name, shape, dtype, off):
        return nc.alloc_sbuf_tensor_at(name, list(shape), dtype, offset=ar0 + off)
    wmt = AT("wmt", (128, 2048), F32, 0)
    modrow = AT("modrow", (1, 2048), F32, 8 * KB)
    bvr = AT("bvr", (1, 256), F32, 16 * KB)
    sa_bc = AT("sa_bc", (128, 1024), F32, 18 * KB)
    capc_bc = AT("capc_bc", (128, 1024), F32, 22 * KB)
    tmp1k = AT("tmp1k", (128, 1024), F32, 0)
    q0r = AT("q0r", (128, 1024), F32, 4 * KB)
    rtmp = AT("rtmp", (128, 512), F32, 8 * KB)
    sint = AT("sint", (128, 512), F32, 10 * KB)
    cost = AT("cost", (128, 512), F32, 12 * KB)
    angt = AT("angt", (128, 512), F32, 14 * KB)
    rrt = AT("rrt", (128, 512), F32, 16 * KB)
    xr = AT("xr", (128, 2048), F32, 26 * KB)
    hp = AT("hp", (128, 4096), BF16, 34 * KB)
    ki32t = AT("ki32t", (128, 512), I32, 42 * KB)
    og = AT("og", (128, 2 * T), BF16, 0)
    drt = AT("drt", (128, 2048), FP16, 32 * KB)
    Pr = AT("Pr", (128, 2048), BF16, 36 * KB)
    ldqt = AT("ldqt", (128, 512), F32, 40 * KB)
    rbct = AT("rbct", (64, 512), F32, 40 * KB)
    rrow = AT("rrow", (1, 512), F32, 42 * KB)
    stcol = AT("stcol", (64, 4), F32, 44 * KB)
    fincol = AT("fincol", (1, 16), F32, 44 * KB + 256)
    zev = nc.alloc_sbuf_tensor_at("zev", [128, 2048], F32, offset=kf0)
    ont = nc.alloc_sbuf_tensor_at("ont", [128, 512], F32, offset=kf0 + 8 * KB)

    psAB_full = nc.alloc_psum_tensor("psAB", [128, 1024], F32)
    psCD_full = nc.alloc_psum_tensor("psCD", [128, 1024], F32)
    psAB = psAB_full
    psCD = psCD_full
    psA = psAB[:, 0:512]
    psB = psAB[:, 512:1024]
    psC = psCD[:, 0:512]
    psD = psCD[:, 512:1024]
    psE = nc.alloc_psum_tensor("psE", [128, 512], F32)
    psF = nc.alloc_psum_tensor("psF", [65, 512], F32)
    psG = nc.alloc_psum_tensor("psG", [65, 512], F32)
    psH = nc.alloc_psum_tensor("psH", [64, 512], F32)

    sems = {k: nc.alloc_semaphore(k) for k in ["ds", "dg", "dz", "t0", "v0", "s0", "vv", "sv2"]}

    def program(em: Em):
        W, D = em.wait, em.do

        def DV(fn, inc=None):
            sem = inc or "vv"
            tok = D("vector", fn, sem)
            W("vector", sem, tok)
            return tok

        def DS(fn, inc=None):
            sem = inc or "sv2"
            tok = D("scalar", fn, sem)
            W("scalar", sem, tok)
            return tok

        for nm, tile in [("ident", identt), ("permT", permt), ("ones_row", onesr),
                         ("ones_col", onesc), ("pmod32", pmod32t), ("pk2d", pk2dt),
                         ("tki", tkit), ("ospl", osplt), ("eh4", eh4t)]:
            D("sync", lambda sp, nm=nm, tile=tile: sp.dma_start(out=tile[:], in_=ic[nm]), "ds", 16)
        for src, tile in [(bq_f, bqf), (bk_f, bkf), (bg_f, bgf), (gam_pk, gampk),
                          (bet_pk, betpk), (bd_col, bdc), (c_fmaj, cfm)]:
            D("sync", lambda sp, src=src, tile=tile: sp.dma_start(out=tile[:], in_=src), "ds", 16)
        D("sync", lambda sp: sp.dma_start(out=bvr[:], in_=bv_row), "ds", 16)
        c_consts = em.c["ds"]

        for src, tile in [(wqT, wq_b), (wkT, wk_b), (wvT, wv_b), (wgT, wg_b)]:
            D("gpsimd", lambda gp, src=src, tile=tile: gp.dma_start(
                out=tile[:].rearrange("p (fc o) -> p fc o", fc=8),
                in_=src.rearrange("(fc p) o -> p fc o", p=128)), "dg", 16)
        D("gpsimd", lambda gp: gp.dma_start(
            out=wd_b[:].rearrange("p (fc o) -> p fc o", fc=8),
            in_=wdT.rearrange("(fc p) o -> p fc o", p=128)), "dg", 16)
        D("gpsimd", lambda gp: gp.dma_start(out=wo_b[:], in_=woTb), "dg", 16)
        c_wg = em.c["dg"]

        D("vector", lambda v: v.memset(zcol[:], 0.0))
        D("vector", lambda v: v.memset(zc4[:], 0.0))
        D("vector", lambda v: v.memset(c5l2[:], -5.0 * LN2))
        D("vector", lambda v: v.memset(hsum[:], 0.0))
        D("vector", lambda v: v.memset(n2q[:], 0.0))
        D("vector", lambda v: v.memset(n2k[:], 0.0))
        D("vector", lambda v: v.memset(s1a[:], 0.0))
        D("vector", lambda v: v.memset(s2a[:], 0.0))
        cv_ones = D("vector", lambda v: v.memset(
            Vp[:].rearrange("p (kth c) -> p kth c", c=65)[:, :, 64:65], 1.0), "v0")

        W("scalar", "ds", c_consts)
        D("scalar", lambda s: s.activation(fcol[:], pmod32t[:], AF.Exp, bias=zcol[:],
                                           scale=-LN1E4 / 32.0), "s0")
        cs_sil = D("scalar", lambda s: s.activation(silc[:], cfm[:], AF.Silu, bias=zcol[:]), "s0")

        mod_ps = [psA, psB, psC, psD]
        for fc in range(8):
            cw = D("sync", lambda sp, fc=fc: sp.dma_start(
                out=wmt[:], in_=wm_sc[128 * fc:128 * (fc + 1), :]), "ds", 16)
            W("tensor", "ds", cw)
            if fc == 0:
                W("tensor", "s0", cs_sil)
            for half in range(4):
                D("tensor", lambda t, fc=fc, half=half: t.matmul(
                    mod_ps[half][0:1, :], silc[:, fc:fc + 1], wmt[:, 512 * half:512 * (half + 1)],
                    start=(fc == 0), stop=(fc == 7)), "t0" if half == 3 else None)
            if fc < 7:
                W("sync", "t0", em.c["t0"])
        c_modmm = em.c["t0"]
        W("vector", "t0", c_modmm)
        for half in range(4):
            DV(lambda v, half=half: v.tensor_copy(
                modrow[0:1, 512 * half:512 * (half + 1)], mod_ps[half][0:1, :]))
        cv1 = DV(lambda v: v.tensor_scalar(
            modrow[0:1, 1024:2048], modrow[0:1, 1024:2048], 1.0, 1.0, ALU.mult, ALU.add), "v0")
        W("tensor", "v0", cv1)
        for half in range(4):
            dst = sa_bc if half < 2 else capc_bc
            cbt = D("tensor", lambda t, half=half: t.matmul(
                psE[:], onesr[0:1, :], modrow[0:1, 512 * half:512 * (half + 1)],
                start=True, stop=True), "t0")
            W("vector", "t0", cbt)
            cvh = D("vector", lambda v, dst=dst, half=half: v.tensor_copy(
                dst[:, 512 * (half % 2):512 * (half % 2 + 1)], psE[:]), "v0")
            W("tensor", "v0", cvh)
        cbt = D("tensor", lambda t: t.matmul(psE[:, 0:256], onesr[0:1, :], bvr[0:1, :],
                                             start=True, stop=True), "t0")
        W("vector", "t0", cbt)
        cv_bv = D("vector", lambda v: v.tensor_copy(bv_bc[:], psE[:, 0:256]), "v0")
        W("vector", "ds", c_consts)
        cv_bq8 = DV(lambda v: v.tensor_scalar(bqf[:], bqf[:], 0.125, None, ALU.mult), "v0")

        if stop < 1:
            W("sync", "ds", em.c["ds"]); W("gpsimd", "dg", em.c["dg"])
            W("vector", "v0", em.c["v0"]); W("scalar", "s0", em.c["s0"])
            W("tensor", "t0", em.c["t0"])
            return
        tok_x, tok_hTM, tok_hT, tok_hfm = {}, {}, {}, {}
        tok_qk, tok_gev, tok_nrm, tok_nmm = {}, {}, {}, {}

        def xload(ti):
            return D("sync", lambda sp, ti=ti: sp.dma_start(
                out=xr[:, 1024 * (ti % 2):1024 * (ti % 2 + 1)],
                in_=xb[128 * ti:128 * (ti + 1), :]), "ds", 16)

        tok_x[0] = xload(0)
        tok_x[1] = xload(1)

        for tt in range(NP):
            for st in range(4):
                ti = 4 * tt + st
                xsl = xr[:, 1024 * (ti % 2):1024 * (ti % 2 + 1)]
                W("vector", "ds", tok_x[ti])
                if tt == 0 and st == 0:
                    W("vector", "v0", cv_bv)
                DV(lambda v, xsl=xsl: v.reduce_sum(lncol[:, 0:1], xsl, axis=AXX))
                W("scalar", "ds", tok_x[ti])
                if tok_nmm.get("last"):
                    W("scalar", "t0", tok_nmm["last"])
                csq = D("scalar", lambda s, xsl=xsl: s.activation(sqt[:], xsl, AF.Square, bias=zcol[:]), "s0")
                W("vector", "s0", csq)
                DV(lambda v: v.reduce_sum(lncol[:, 1:2], sqt[:], axis=AXX))
                DV(lambda v: v.tensor_scalar(lncol[:, 2:3], lncol[:, 0:1], 1.0 / E, None, ALU.mult))
                DV(lambda v: v.tensor_scalar(lncol[:, 3:4], lncol[:, 1:2], 1.0 / E, None, ALU.mult))
                DV(lambda v: v.tensor_tensor(lncol[:, 4:5], lncol[:, 2:3], lncol[:, 2:3], ALU.mult))
                DV(lambda v: v.tensor_scalar(lncol[:, 4:5], lncol[:, 3:4], lncol[:, 4:5], None, ALU.subtract))
                DV(lambda v: v.tensor_scalar(lncol[:, 4:5], lncol[:, 4:5], 1.0, EPS, ALU.mult, ALU.add))
                cvv = DV(lambda v: v.reciprocal(lncol[:, 5:6], lncol[:, 4:5]), "v0")
                W("scalar", "v0", cvv)
                cs2 = D("scalar", lambda s: s.activation(lncol[:, 6:7], lncol[:, 5:6], AF.Sqrt, bias=zcol[:]), "s0")
                W("vector", "s0", cs2)
                DV(lambda v: v.tensor_scalar(lncol[:, 7:8], lncol[:, 2:3], lncol[:, 6:7], None, ALU.mult))
                DV(lambda v: v.tensor_scalar(lncol[:, 7:8], lncol[:, 7:8], -1.0, None, ALU.mult))
                DV(lambda v, xsl=xsl: v.tensor_scalar(
                    tmp1k[:], xsl, lncol[:, 6:7], lncol[:, 7:8], ALU.mult, ALU.add))
                DV(lambda v: v.tensor_tensor(tmp1k[:], tmp1k[:], capc_bc[:], ALU.mult))
                tok_hTM[ti] = DV(lambda v, xsl=xsl: v.tensor_tensor(
                    xsl, tmp1k[:], sa_bc[:], ALU.add), "v0")

                W("tensor", "v0", tok_hTM[ti])
                for fc in range(8):
                    bank = psA if fc < 4 else psB
                    D("tensor", lambda t, xsl=xsl, fc=fc, bank=bank: t.transpose(
                        bank[:, 128 * (fc % 4):128 * (fc % 4 + 1)],
                        xsl.rearrange("p (fc q) -> p fc q", fc=8)[:, fc, :],
                        identt[:]), "t0" if fc in (3, 7) else None)
                tok_hT[ti] = em.c["t0"]
                W("vector", "t0", tok_hT[ti])
                hp3 = hp[:].rearrange("p (fc q) -> p fc q", fc=8)
                D("vector", lambda v, st=st, hp3=hp3: v.tensor_copy(
                    hp3[:, 0:4, 128 * st:128 * (st + 1)],
                    psA.rearrange("p (fc q) -> p fc q", fc=4)))
                tok_hfm[ti] = D("vector", lambda v, st=st, hp3=hp3: v.tensor_copy(
                    hp3[:, 4:8, 128 * st:128 * (st + 1)],
                    psB.rearrange("p (fc q) -> p fc q", fc=4)), "v0")
                if ti + 2 < 64:
                    W("sync", "t0", tok_hT[ti])
                    tok_x[ti + 2] = xload(ti + 2)

            W("vector", "v0", tok_hfm[4 * tt + 3])
            for fc in range(8):
                DV(lambda v, fc=fc: v.reduce_sum(
                    lncol[:, 8:9], hp[:, 512 * fc:512 * (fc + 1)], axis=AXX))
                DV(lambda v, fc=fc: v.tensor_scalar(
                    hsum[:, fc:fc + 1], hsum[:, fc:fc + 1], lncol[:, 8:9], None, ALU.add),
                   "v0" if fc == 7 else None)

            DV(lambda v, tt=tt: v.tensor_scalar(tofff[:], fcol[:], float(512 * tt), None, ALU.mult))
            for which in ("sin", "cos"):
                if which == "sin":
                    DV(lambda v: v.tensor_scalar(angt[:], tkit[:], fcol[:], tofff[:], ALU.mult, ALU.add))
                else:
                    W("vector", "s0", em.c["s0"])
                    DV(lambda v: v.tensor_scalar(angt[:], angt[:], PI / 2.0, None, ALU.add))
                DV(lambda v: v.tensor_scalar(rrt[:], angt[:], 1.0 / (2 * PI), None, ALU.mult))
                DV(lambda v: v.tensor_copy(ki32t[:], rrt[:]))
                DV(lambda v: v.tensor_copy(rrt[:], ki32t[:]))
                cvx = DV(lambda v: v.scalar_tensor_tensor(
                    rrt[:], rrt[:], -2 * PI, angt[:], ALU.mult, ALU.add), "v0")
                W("scalar", "v0", cvx)
                dstv = sint if which == "sin" else cost
                D("scalar", lambda s, dstv=dstv: s.activation(dstv[:], rrt[:], AF.Sin, bias=zcol[:]), "s0")
            tok_sc = em.c["s0"]

            for wnm, wtile, dsttile, bias_t, scale in (("q", wq_b, Qf, bqf, 0.125),
                                                       ("k", wk_b, Kf, bkf, 1.0)):
                for ch in range(2):
                    bank = psC if ch == 0 else psD
                    if wnm == "q" and ch == 0:
                        W("tensor", "v0", tok_hfm[4 * tt + 3])
                        if tt > 0:
                            W("tensor", "s0", tok_gev[(tt - 1, 1)])
                    for fc in range(8):
                        D("tensor", lambda t, wtile=wtile, fc=fc, ch=ch, bank=bank: t.matmul(
                            bank[:], wtile[:, 256 * fc + 128 * ch:256 * fc + 128 * (ch + 1)],
                            hp[:, 512 * fc:512 * (fc + 1)],
                            start=(fc == 0), stop=(fc == 7)), "t0" if fc == 7 else None)
                    tok_mm = em.c["t0"]
                    W("scalar", "t0", tok_mm)
                    if tt == 0 and wnm == "q" and ch == 0:
                        W("scalar", "v0", cv_bq8)
                    q0sl = q0r[:, 512 * ch:512 * (ch + 1)]
                    cse = D("scalar", lambda s, bank=bank, bias_t=bias_t, ch=ch, q0sl=q0sl, scale=scale:
                            s.activation(q0sl, bank[:], AF.Identity, bias=bias_t[:, ch:ch + 1], scale=scale), "s0")
                    prevn = tok_nrm.get(("last",))
                    if prevn:
                        W("tensor", "v0", prevn)
                    W("tensor", "s0", cse)
                    W("tensor", "s0", tok_sc)
                    cpm = D("tensor", lambda t, q0sl=q0sl: t.matmul(
                        psE[:], permt[:], q0sl, start=True, stop=True), "t0")
                    W("vector", "t0", cpm)
                    D("vector", lambda v: v.tensor_tensor(rtmp[:], psE[:], sint[:], ALU.mult))
                    DV(lambda v, q0sl=q0sl: v.tensor_tensor(q0sl, q0sl, cost[:], ALU.mult))
                    tok_qk[(tt, wnm, ch)] = DV(
                        lambda v, dsttile=dsttile, ch=ch, tt=tt, q0sl=q0sl:
                        v.tensor_tensor(dsttile[:, T * ch + 512 * tt: T * ch + 512 * (tt + 1)],
                                        rtmp[:], q0sl, ALU.add), "v0")
                    W("scalar", "v0", tok_qk[(tt, wnm, ch)])
                    csq = D("scalar", lambda s, dsttile=dsttile, ch=ch, tt=tt: s.activation(
                        sqt[:, 0:512], dsttile[:, T * ch + 512 * tt: T * ch + 512 * (tt + 1)],
                        AF.Square, bias=zcol[:]), "s0")
                    W("tensor", "s0", csq)
                    cmm = D("tensor", lambda t: t.matmul(psE[0:2, :], osplt[:], sqt[:, 0:512],
                                                         start=True, stop=True), "t0")
                    tok_nmm["last"] = cmm
                    W("vector", "t0", cmm)
                    DV(lambda v: v.reduce_max(n2t[0:2, 0:1], psE[0:2, :], axis=AXX))
                    dstn = n2q if wnm == "q" else n2k
                    tok_nrm[("last",)] = DV(lambda v, dstn=dstn, ch=ch: v.tensor_scalar(
                        dstn[0:2, ch:ch + 1], dstn[0:2, ch:ch + 1], n2t[0:2, 0:1], None, ALU.max), "v0")

            tok_vev = {}
            for st in range(4):
                bank = psC if st % 2 == 0 else psD
                if st >= 2:
                    W("tensor", "v0", tok_vev[st - 2])
                for fc in range(8):
                    D("tensor", lambda t, st=st, fc=fc, bank=bank: t.matmul(
                        bank[:, 0:256],
                        hp[:, 512 * fc + 128 * st:512 * fc + 128 * (st + 1)],
                        wv_b[:, 256 * fc:256 * (fc + 1)],
                        start=(fc == 0), stop=(fc == 7)), "t0" if fc == 7 else None)
                tok_mm = em.c["t0"]
                W("vector", "t0", tok_mm)
                kt = 4 * tt + st
                tok_vev[st] = D("vector", lambda v, bank=bank, kt=kt: v.tensor_tensor(
                    Vp[:].rearrange("p (kt2 h c) -> p kt2 h c", h=4, c=65)[:, kt, :, 0:64],
                    bank[:, 0:256].rearrange("p (h c) -> p h c", h=4),
                    bv_bc[:].rearrange("p (h c) -> p h c", h=4), ALU.add), "v0")

            for ch in range(2):
                bank = psC if ch == 0 else psD
                W("tensor", "v0", tok_vev[2 + ch])
                for fc in range(8):
                    D("tensor", lambda t, fc=fc, ch=ch, bank=bank: t.matmul(
                        bank[:], wg_b[:, 256 * fc + 128 * ch:256 * fc + 128 * (ch + 1)],
                        hp[:, 512 * fc:512 * (fc + 1)],
                        start=(fc == 0), stop=(fc == 7)), "t0" if fc == 7 else None)
                tok_mm = em.c["t0"]
                W("scalar", "t0", tok_mm)
                tok_gev[(tt, ch)] = D("scalar", lambda s, bank=bank, ch=ch, tt=tt: s.activation(
                    gf[:, T * ch + 512 * tt: T * ch + 512 * (tt + 1)], bank[:], AF.Silu,
                    bias=bgf[:, ch:ch + 1]), "s0")

        c_ph1_v, c_ph1_s, c_ph1_t = em.c["v0"], em.c["s0"], em.c["t0"]
        if dbg:
            W("gpsimd", "v0", c_ph1_v)
            D("gpsimd", lambda gp: gp.dma_start(out=dbg_o["t_h"], in_=hp[:, 0:512]), "dg", 16)

        if stop < 2:
            W("sync", "ds", em.c["ds"]); W("gpsimd", "dg", em.c["dg"])
            W("vector", "v0", em.c["v0"]); W("scalar", "s0", em.c["s0"])
            W("tensor", "t0", em.c["t0"])
            return
        DV(lambda v: v.tensor_scalar(hmean_b[:], hsum[:], 1.0 / T, None, ALU.mult), "v0")
        cv_hm = em.c["v0"]
        W("tensor", "v0", cv_hm)
        for fc in range(8):
            D("tensor", lambda t, fc=fc: t.matmul(
                psA[0:4, 0:1], wd_b[:, 4 * fc:4 * (fc + 1)], hmean_b[:, fc:fc + 1],
                start=(fc == 0), stop=(fc == 7)), "t0" if fc == 7 else None)
        c_pdmm = em.c["t0"]
        W("scalar", "t0", c_pdmm)
        cs_pd = DS(lambda s: s.activation(pdld[:, 0:1], psA[0:4, 0:1], AF.Sigmoid, bias=bdc[:]), "s0")
        cs_e2 = DS(lambda s: s.activation(pdld[:, 1:2], pdld[:, 0:1], AF.Exp,
                                          bias=c5l2[:], scale=-8.0 * LN2), "s0")
        W("vector", "s0", cs_e2)
        cv3 = DV(lambda v: v.tensor_scalar(pdld[:, 1:2], pdld[:, 1:2], -1.0, 1.0, ALU.mult, ALU.add), "v0")
        W("scalar", "v0", cv3)
        cs_ld = D("scalar", lambda s: s.activation(pdld[:, 2:3], pdld[:, 1:2], AF.Ln, bias=zc4[:]), "s0")
        W("vector", "s0", cs_ld)
        for ch in range(2):
            DV(lambda v, ch=ch: v.reduce_max(b2c[0:2, 0:1], n2q[0:2, ch:ch + 1], axis=AXX))
            DV(lambda v, ch=ch: v.reduce_max(b2c[0:2, 1:2], n2k[0:2, ch:ch + 1], axis=AXX))
            cvb = DV(lambda v: v.tensor_scalar(b2c[0:2, 2:3], b2c[0:2, 0:1], b2c[0:2, 1:2], None, ALU.mult), "v0")
            W("scalar", "v0", cvb)
            D("scalar", lambda s, ch=ch: s.activation(bnd2[0:2, ch:ch + 1], b2c[0:2, 2:3], AF.Sqrt,
                                                      bias=zcol[0:2, :]), "s0")
        cs_b = em.c["s0"]
        W("tensor", "s0", cs_b)
        prev_tok = None
        for h in range(NH):
            if prev_tok:
                W("tensor", "v0", prev_tok)
            cm1 = D("tensor", lambda t, h=h: t.matmul(psA[:, 0:1], eh4t[:, 128 * h:128 * (h + 1)],
                                                      pdld[:, 2:3], start=True, stop=True), "t0")
            cm2 = D("tensor", lambda t, h=h: t.matmul(psB[:, 0:1], eh4t[0:2, 128 * (h % 2):128 * (h % 2) + 128],
                                                      bnd2[0:2, h // 2:h // 2 + 1], start=True, stop=True), "t0")
            W("vector", "t0", cm2)
            DV(lambda v, h=h: v.tensor_copy(ldbc[:, h:h + 1], psA[:, 0:1]))
            DV(lambda v, h=h: v.tensor_scalar(nbbc[:, h:h + 1], psB[:, 0:1], -1.0, None, ALU.mult))
            prev_tok = DV(lambda v, h=h: v.tensor_scalar(
                ldkb[:, 64 * h:64 * (h + 1)], pk2dt[:], ldbc[:, h:h + 1], None,
                ALU.mult), "v0")
        c_ph2_v = em.c["v0"]

        if stop < 3:
            W("sync", "ds", em.c["ds"]); W("gpsimd", "dg", em.c["dg"])
            W("vector", "v0", em.c["v0"]); W("scalar", "s0", em.c["s0"])
            W("tensor", "t0", em.c["t0"])
            return
        W("sync", "t0", c_ph1_t)
        if dbg:
            W("sync", "dg", em.c["dg"])
        c_dr = D("sync", lambda sp: sp.dma_start(out=drt[:], in_=ic["drelu"]), "ds", 16)
        tok_P, tok_avt, tok_oge = {}, {}, {}
        NPAIR = NT // 2
        it = -1
        for h in range(NH):
            ch, rw = h // 2, 64 * (h % 2)
            for qt in range(NP):
                it += 1
                O2 = psF if it % 2 == 0 else psG
                if NT - (4 * qt + 4) > 0:
                    W("vector", "v0", c_ph2_v)
                    DV(lambda v, h=h: v.tensor_scalar(negld[:], ldbc[:, h:h + 1], -1.0, None, ALU.mult))
                    DV(lambda v, qt=qt: v.tensor_scalar(nlq[:], negld[:], float(512 * qt), None, ALU.mult))
                    DV(lambda v: v.tensor_scalar(
                        ldqt[:], tkit[:], negld[:], nlq[:], ALU.mult, ALU.add), "v0")

                def emit_pair(pb):
                    pbank = psAB_full if pb % 2 == 0 else psCD_full
                    if pb >= 2:
                        W("tensor", "s0", tok_P[(h, qt, pb - 2)])
                    elif it > 0:
                        W("tensor", "s0", tok_P[divmod_prev(h, qt, NPAIR - 2 + pb)])
                    else:
                        W("tensor", "v0", c_ph2_v)
                        W("tensor", "ds", c_dr)
                    cvf = None
                    for j in range(2):
                        kb = 2 * pb + j
                        half = pbank[:, 512 * j:512 * (j + 1)]
                        tok_mm = D("tensor", lambda t, ch=ch, rw=rw, kb=kb, qt=qt, half=half: t.matmul(
                            half, Kf[rw:rw + 64, T * ch + 128 * kb: T * ch + 128 * (kb + 1)],
                            Qf[rw:rw + 64, T * ch + 512 * qt: T * ch + 512 * (qt + 1)],
                            start=True, stop=True), "t0")
                        kind = "past" if kb < 4 * qt else ("diag" if kb < 4 * qt + 4 else "fut")
                        if kind == "fut":
                            W("vector", "t0", tok_mm)
                            cvf = D("vector", lambda v, half=half, h=h, kb=kb: v.scalar_tensor_tensor(
                                half, half, ldkb[:, 64 * h + kb:64 * h + kb + 1], ldqt[:],
                                ALU.add, ALU.add), "v0")
                        elif kind == "diag":
                            W("vector", "t0", tok_mm)
                            ci = kb - 4 * qt
                            cvf = D("vector", lambda v, half=half, ci=ci, h=h: v.scalar_tensor_tensor(
                                half, drt[:, 512 * ci:512 * (ci + 1)], ldbc[:, h:h + 1], half,
                                ALU.mult, ALU.add), "v0")
                    if cvf is not None:
                        W("scalar", "v0", cvf)
                    else:
                        W("scalar", "t0", em.c["t0"])
                    if pb >= 2:
                        W("scalar", "t0", tok_avt[(h, qt, 2 * (pb - 2) + 1)])
                    elif it > 0:
                        W("scalar", "t0", tok_avt[divmod_prev(h, qt, 2 * (NPAIR - 2 + pb) + 1)])
                    psl = Pr[:, 1024 * (pb % 2):1024 * (pb % 2 + 1)]
                    tok_P[(h, qt, pb)] = D("scalar", lambda s, pbank=pbank, psl=psl, h=h:
                                           s.activation(psl, pbank[:, :], AF.Exp,
                                                        bias=nbbc[:, h:h + 1]), "s0")
                    return psl

                psl_map = {}
                psl_map[0] = emit_pair(0)
                psl_map[1] = emit_pair(1)
                for pb in range(NPAIR):
                    psl = psl_map.pop(pb)
                    W("tensor", "s0", tok_P[(h, qt, pb)])
                    if pb == 0 and it >= 2:
                        W("tensor", "v0", tok_oge[it - 2])
                    for j in range(2):
                        kb = 2 * pb + j
                        tok_avt[(h, qt, kb)] = D("tensor", lambda t, kb=kb, h=h, psl=psl, j=j, O2=O2: t.matmul(
                            O2[:], Vp[:, 260 * kb + 65 * h: 260 * kb + 65 * (h + 1)],
                            psl[:, 512 * j:512 * (j + 1)],
                            start=(kb == 0), stop=(kb == NT - 1)), "t0")
                    if pb + 2 < NPAIR:
                        psl_map[pb + 2] = emit_pair(pb + 2)
                W("vector", "t0", tok_avt[(h, qt, NT - 1)])
                cvr = D("vector", lambda v, O2=O2: v.reciprocal(rrow[0:1, :], O2[64:65, :]), "v0")
                W("tensor", "v0", cvr)
                cmm = D("tensor", lambda t: t.matmul(psH[:], onesr[0:1, 0:64], rrow[0:1, :],
                                                     start=True, stop=True), "t0")
                W("vector", "t0", cmm)
                DV(lambda v: v.tensor_copy(rbct[:, :], psH[0:64, :]))
                ogdst = (og[0:64, T * ch + 512 * qt: T * ch + 512 * (qt + 1)] if rw == 0
                         else ogscr[:, :])
                cvo = DV(lambda v, O2=O2, ogdst=ogdst: v.tensor_tensor(
                    ogdst, O2[0:64, :], rbct[:, :], ALU.mult), "v0")
                if rw != 0:
                    W("gpsimd", "v0", cvo)
                    D("gpsimd", lambda gp, ch=ch, qt=qt: gp.dma_start(
                        out=og[64:128, T * ch + 512 * qt: T * ch + 512 * (qt + 1)],
                        in_=ogscr[:, :]), "dg", 16)
                DV(lambda v, ogdst=ogdst: v.reduce_sum(stcol[:, 0:1], ogdst, axis=AXX))
                DV(lambda v, h=h: v.tensor_scalar(s1a[0:64, h:h + 1], s1a[0:64, h:h + 1],
                                                  stcol[:, 0:1], None, ALU.add))
                W("scalar", "v0", cvo)
                csq = D("scalar", lambda s, ogdst=ogdst: s.activation(
                    sqt[0:64, 0:512], ogdst, AF.Square, bias=zcol[0:64, :]), "s0")
                W("vector", "s0", csq)
                DV(lambda v: v.reduce_sum(stcol[:, 1:2], sqt[0:64, 0:512], axis=AXX))
                tok_oge[it] = DV(lambda v, h=h: v.tensor_scalar(
                    s2a[0:64, h:h + 1], s2a[0:64, h:h + 1], stcol[:, 1:2], None, ALU.add), "v0")

        c_ph3_v, c_ph3_t, c_ph3_dg, c_ph3_s = em.c["v0"], em.c["t0"], em.c["dg"], em.c["s0"]

        if dbg:
            W("gpsimd", "v0", c_ph3_v)
            c_tap = D("gpsimd", lambda gp: gp.dma_start(out=dbg_o["t_og"], in_=og[:, 0:512]), "dg", 16)
            D("gpsimd", lambda gp: gp.dma_start(out=dbg_o["t_q"], in_=Qf[:, 0:512]), "dg", 16)
            D("gpsimd", lambda gp: gp.dma_start(out=dbg_o["t_k"], in_=Kf[:, 0:512]), "dg", 16)
            D("gpsimd", lambda gp: gp.dma_start(out=dbg_o["t_v"], in_=Vp[:, 0:260]), "dg", 16)
            D("gpsimd", lambda gp: gp.dma_start(out=dbg_o["t_g"], in_=gf[:, 0:512]), "dg", 16)
        else:
            c_tap = None

        if stop < 4:
            W("sync", "ds", em.c["ds"]); W("gpsimd", "dg", em.c["dg"])
            W("vector", "v0", em.c["v0"]); W("scalar", "s0", em.c["s0"])
            W("tensor", "t0", em.c["t0"])
            return
        prev_tok = None
        tok_gn = {}
        for h in range(NH):
            ch, rw = h // 2, 64 * (h % 2)
            W("tensor", "v0", c_ph3_v if prev_tok is None else prev_tok)
            cm1 = D("tensor", lambda t, h=h: t.matmul(psA[0:1, 0:1], onesc[0:64, :], s1a[0:64, h:h + 1],
                                                      start=True, stop=True), "t0")
            cm2 = D("tensor", lambda t, h=h: t.matmul(psA[0:1, 1:2], onesc[0:64, :], s2a[0:64, h:h + 1],
                                                      start=True, stop=True), "t0")
            W("vector", "t0", cm2)
            NN = float(T * HD)
            DV(lambda v: v.tensor_scalar(fincol[0:1, 0:1], psA[0:1, 0:1], 1.0 / NN, None, ALU.mult))
            DV(lambda v: v.tensor_scalar(fincol[0:1, 1:2], psA[0:1, 1:2], 1.0 / NN, None, ALU.mult))
            DV(lambda v: v.tensor_tensor(fincol[0:1, 2:3], fincol[0:1, 0:1], fincol[0:1, 0:1], ALU.mult))
            DV(lambda v: v.tensor_scalar(fincol[0:1, 2:3], fincol[0:1, 1:2], fincol[0:1, 2:3], None, ALU.subtract))
            DV(lambda v: v.tensor_scalar(fincol[0:1, 2:3], fincol[0:1, 2:3], 1.0, EPS, ALU.mult, ALU.add))
            cvf = DV(lambda v: v.reciprocal(fincol[0:1, 3:4], fincol[0:1, 2:3]), "v0")
            W("scalar", "v0", cvf)
            csA = D("scalar", lambda s: s.activation(fincol[0:1, 4:5], fincol[0:1, 3:4], AF.Sqrt,
                                                     bias=zcol[0:1, :]), "s0")
            W("vector", "s0", csA)
            DV(lambda v: v.tensor_scalar(fincol[0:1, 5:6], fincol[0:1, 0:1], fincol[0:1, 4:5], None, ALU.mult))
            cvB = DV(lambda v: v.tensor_scalar(fincol[0:1, 5:6], fincol[0:1, 5:6], -1.0, None, ALU.mult), "v0")
            W("tensor", "v0", cvB)
            cb1 = D("tensor", lambda t: t.matmul(psA[:, 2:3], onesr[0:1, :], fincol[0:1, 4:5],
                                                 start=True, stop=True), "t0")
            cb2 = D("tensor", lambda t: t.matmul(psB[:, 2:3], onesr[0:1, :], fincol[0:1, 5:6],
                                                 start=True, stop=True), "t0")
            W("vector", "t0", cb2)
            DV(lambda v, h=h: v.tensor_copy(abcol[:, h:h + 1], psA[:, 2:3]))
            prev_tok = DV(lambda v, h=h: v.tensor_copy(abcol[:, 4 + h:5 + h], psB[:, 2:3]), "v0")
            if rw != 0:
                W("scalar", "dg", c_ph3_dg)
            if dbg and h == 0:
                W("scalar", "dg", c_tap)
            W("scalar", "v0", prev_tok)
            for tq in range(NP):
                if tq > 0:
                    W("scalar", "v0", tok_gn[(h, tq - 1)])
                elif h > 0:
                    W("scalar", "v0", tok_gn[(h - 1, NP - 1)])
                sl = (slice(rw, rw + 64), slice(T * ch + 512 * tq, T * ch + 512 * (tq + 1)))
                csn = D("scalar", lambda s, sl=sl, h=h, rw=rw: s.activation(
                    ont[rw:rw + 64, :], og[sl], AF.Identity,
                    bias=abcol[rw:rw + 64, 4 + h:5 + h], scale=abcol[rw:rw + 64, h:h + 1]), "s0")
                W("vector", "s0", csn)
                DV(lambda v, h=h, rw=rw: v.tensor_scalar(
                    ont[rw:rw + 64, :], ont[rw:rw + 64, :], gampk[rw:rw + 64, h // 2:h // 2 + 1],
                    betpk[rw:rw + 64, h // 2:h // 2 + 1], ALU.mult, ALU.add))
                tok_gn[(h, tq)] = DV(lambda v, sl=sl, h=h, rw=rw, ch=ch, tq=tq: v.tensor_tensor(
                    og[sl], ont[rw:rw + 64, :], gf[sl], ALU.mult), "v0")
        c_gn = em.c["v0"]
        if stop < 5:
            W("sync", "ds", em.c["ds"]); W("gpsimd", "dg", em.c["dg"])
            W("vector", "v0", em.c["v0"]); W("scalar", "s0", em.c["s0"])
            W("tensor", "t0", em.c["t0"])
            return

        tok_zev = {}
        for ti in range(NT):
            for oh in range(2):
                bank = psA if oh == 0 else psB
                if ti >= 1:
                    if oh == 0:
                        W("tensor", "v0", tok_zev[ti - 1])
                    if ti >= 2 and oh == 0:
                        W("vector", "dz", 16 * (ti - 1))
                elif ti == 0 and oh == 0:
                    W("tensor", "v0", c_gn)
                for ch in range(2):
                    D("tensor", lambda t, ti=ti, oh=oh, ch=ch, bank=bank: t.matmul(
                        bank[:],
                        og[:, T * ch + 128 * ti: T * ch + 128 * (ti + 1)],
                        wo_b[:, 1024 * ch + 512 * oh: 1024 * ch + 512 * (oh + 1)],
                        start=(ch == 0), stop=(ch == 1)), "t0" if ch == 1 else None)
                cmm = em.c["t0"]
                W("vector", "t0", cmm)
                D("vector", lambda v, ti=ti, oh=oh, bank=bank: v.tensor_copy(
                    zev[:, 1024 * (ti % 2) + 512 * oh: 1024 * (ti % 2) + 512 * (oh + 1)], bank[:]),
                  "v0" if oh == 1 else None)
            tok_zev[ti] = em.c["v0"]
            W("sync", "v0", tok_zev[ti])
            D("sync", lambda sp, ti=ti: sp.dma_start(
                out=zout[128 * ti:128 * (ti + 1), :],
                in_=zev[:, 1024 * (ti % 2):1024 * (ti % 2 + 1)]), "dz", 16)

        if stop < 6:
            W("sync", "ds", em.c["ds"]); W("sync", "dz", em.c["dz"]); W("gpsimd", "dg", em.c["dg"])
            W("vector", "v0", em.c["v0"]); W("scalar", "s0", em.c["s0"])
            W("tensor", "t0", em.c["t0"])
            return
        if dbg:
            W("vector", "v0", c_gn)
            D("vector", lambda v: v.tensor_copy(dbm[:, 0:4], ldbc[:]))
            D("vector", lambda v: v.tensor_copy(dbm[:, 4:8], nbbc[:]))
            cvm = D("vector", lambda v: v.tensor_copy(dbm[:, 8:16], abcol[:]), "v0")
            W("sync", "v0", cvm)
            D("sync", lambda sp: sp.dma_start(out=dbg_o["t_m"], in_=dbm[:]), "dz", 16)

        W("sync", "dz", em.c["dz"])
        W("gpsimd", "dg", em.c["dg"])

    def divmod_prev(h, qt, kb):
        if qt > 0:
            return (h, qt - 1, kb)
        return (h - 1, NP - 1, kb)

    with nc.Block() as block:
        block.sync(lambda e: program(Em(nc, sems, "sync", e)))
        block.gpsimd(lambda e: program(Em(nc, sems, "gpsimd", e)))
        block.tensor(lambda e: program(Em(nc, sems, "tensor", e)))
        block.vector(lambda e: program(Em(nc, sems, "vector", e)))
        block.scalar(lambda e: program(Em(nc, sems, "scalar", e)))

    return nc


def l1_inputs(core, ins):
    b, g = core // 4, core % 4
    gs = slice(256 * g, 256 * (g + 1))
    hs = slice(4 * g, 4 * (g + 1))
    f32 = np.float32
    wq = np.asarray(ins["L0_attn_wq"], f32); wk = np.asarray(ins["L0_attn_wk"], f32)
    wv = np.asarray(ins["L0_attn_wv"], f32); wg = np.asarray(ins["L0_attn_wg"], f32)
    wo = np.asarray(ins["L0_attn_wo"], f32); wd = np.asarray(ins["L0_attn_wd"], f32)
    woT = np.ascontiguousarray(wo.T[gs, :])
    woTb = np.zeros((128, 2048), f32)
    for hh in range(4):
        woTb[64 * (hh % 2):64 * (hh % 2) + 64, 1024 * (hh // 2):1024 * (hh // 2) + 1024] = \
            woT[64 * hh:64 * (hh + 1), :]
    def fmaj(vec):
        return np.ascontiguousarray(np.asarray(vec, f32)[gs].reshape(2, 128).T)
    out = {
        "xb": np.ascontiguousarray(np.asarray(ins["L0_attn_x"], f32)[b]),
        "wqT": np.ascontiguousarray(wq.T[:, gs]),
        "wkT": np.ascontiguousarray(wk.T[:, gs]),
        "wvT": np.ascontiguousarray(wv.T[:, gs]),
        "wgT": np.ascontiguousarray(wg.T[:, gs]),
        "woTb": woTb,
        "wdT": np.ascontiguousarray(wd.T[:, hs]),
        "bq_f": fmaj(ins["L0_attn_bq"]),
        "bk_f": fmaj(ins["L0_attn_bk"]),
        "bg_f": fmaj(ins["L0_attn_bg"]),
        "bv_row": np.ascontiguousarray(np.asarray(ins["L0_attn_bv"], f32)[gs].reshape(1, 256)),
        "bd_col": np.ascontiguousarray(np.asarray(ins["L0_attn_bd"], f32)[hs].reshape(4, 1)),
        "gam_pk": fmaj(ins["L0_attn_gamma"]),
        "bet_pk": fmaj(ins["L0_attn_beta"]),
        "c_fmaj": np.ascontiguousarray(np.asarray(ins["L0_c"], f32)[b].reshape(8, 128).T),
        "wm_sc": np.ascontiguousarray(np.asarray(ins["L0_wmod"], f32)[:, 0:2048]),
    }
    return out


TS = 2048
E = 1024
HID = 2048
NE = 8
NSH = 2
EPS = 1e-5
KB = 1024
NTT = TS // 128


def build_l2(dbg=False, stop=99):
    nc = bass.Bass("TRN2", target_bir_lowering=False)
    CN = consts_np()

    def din(name, shape):
        return nc.dram_tensor(name, list(shape), F32, kind="ExternalInput").ap()

    zp4 = din("zp4", (4, TS, E))
    x_sl = din("x_sl", (TS, E))
    c_fmaj = din("c_fmaj", (128, 8))
    wm2 = din("wm2", (E, 4096))
    bo_row = din("bo_row", (1, E))
    routerT = din("routerT", (E, 8))
    eb_row = din("eb_row", (1, 8))
    w1T = din("w1T", (NE, E, HID))
    w3T = din("w3T", (NE, E, HID))
    w2T = din("w2T", (NE, HID, E))
    sw1T = din("sw1T", (NSH, E, HID))
    sw3T = din("sw3T", (NSH, E, HID))
    sw2T = din("sw2T", (NSH, HID, E))

    oout = nc.dram_tensor("oslice", [TS, E], F32, kind="ExternalOutput").ap()
    x1d = nc.dram_tensor("x1d", [TS, E], F32, kind="Internal").ap()
    dbg_o = {}
    if dbg:
        for nm, sh in [("d_h2", (128, 512)), ("d_cmb", (128, 16)), ("d_mod", (128, 16)),
                       ("d_x1", (128, 1024)), ("d_y", (128, 1024)), ("d_gb", (128, 1024)),
                       ("d_lg", (128, 16)), ("d_o0", (128, 1024)), ("d_xr0", (128, 1024))]:
            dbg_o[nm] = nc.dram_tensor(nm, list(sh), F32, kind="ExternalOutput").ap()

    ic = {k: nc.inline_tensor(np.ascontiguousarray(v), f"c2_{k}").ap()
          for k, v in CN.items() if k in ("ident", "ones_row")}

    A = lambda name, shape, dtype: nc.alloc_sbuf_tensor(name, list(shape), dtype, side="left")

    h2 = A("h2", (128, 8 * TS), BF16)
    yac = A("yac", (128, NTT * E), F32)
    Au = A("Au", (128, 16 * TS), BF16)
    cmb = A("cmb", (128, NTT * 8), F32)
    gbb = A("gbb", (128, E), F32)
    identt = A("identt", (128, 128), F32)
    onesr = A("onesr", (1, 128), F32)
    cfm = A("cfm", (128, 8), F32)
    silc = A("silc", (128, 8), F32)
    rt_b = A("rt_b", (128, 64), BF16)
    ebb = A("ebb", (128, 8), F32)
    lncol = A("lncol", (128, 16), F32)
    zcol = A("zcol", (128, 1), F32)
    ltile = A("ltile", (128, 8), F32)
    etile = A("etile", (128, 8), F32)
    ktile = A("ktile", (128, 8), F32)
    mcols = A("mcols", (128, 4), F32)
    dlg_t = A("dlg_t", (128, 16), F32)

    ARENA = 40 * KB
    ar0, _ = nc.bump_sbuf(ARENA, side="left")
    def AT(name, shape, dtype, off):
        return nc.alloc_sbuf_tensor_at(name, list(shape), dtype, offset=ar0 + off)
    wmt = AT("wmt", (128, 2048), F32, 0)
    modrow = AT("modrow", (1, 2048), F32, 8 * KB)
    gab = AT("gab", (128, E), F32, 16 * KB)
    bob = AT("bob", (128, E), F32, 20 * KB)
    capcb = AT("capcb", (128, E), F32, 24 * KB)
    sbb = AT("sbb", (128, E), F32, 28 * KB)
    zst = AT("zst", (128, 4096), F32, 0)
    xt1 = AT("xt1", (128, 1024), F32, 32 * KB)
    sqt = AT("sqt", (128, 1024), F32, 36 * KB)
    w2f = AT("w2f", (128, 16 * 1024), BF16, 0)
    w1r = AT("w1r", (128, 2 * 1024), BF16, 32 * KB)
    w3r = AT("w3r", (128, 2 * 1024), BF16, 36 * KB)
    outr = AT("outr", (128, 2048), F32, 0)
    x1rd = AT("x1rd", (128, 2048), F32, 8 * KB)

    psAB = nc.alloc_psum_tensor("psAB", [128, 1024], F32)
    psCD = nc.alloc_psum_tensor("psCD", [128, 1024], F32)
    psA = psAB[:, 0:512]
    psB = psAB[:, 512:1024]
    psC = psCD[:, 0:512]
    psD = psCD[:, 512:1024]
    psE = nc.alloc_psum_tensor("psE", [128, 8], F32)

    sems = {k: nc.alloc_semaphore(k) for k in ["ds", "dg", "dz", "t0", "v0", "s0", "vv", "sv2"]}

    def program(em: Em):
        W, D = em.wait, em.do

        def DV(fn, inc=None):
            sem = inc or "vv"
            tok = D("vector", fn, sem)
            W("vector", sem, tok)
            return tok

        def DS(fn, inc=None):
            sem = inc or "sv2"
            tok = D("scalar", fn, sem)
            W("scalar", sem, tok)
            return tok

        for nm, tile in [("ident", identt), ("ones_row", onesr)]:
            D("sync", lambda sp, nm=nm, tile=tile: sp.dma_start(out=tile[:], in_=ic[nm]), "ds", 16)
        D("sync", lambda sp: sp.dma_start(out=cfm[:], in_=c_fmaj), "ds", 16)
        c_consts = em.c["ds"]
        D("gpsimd", lambda gp: gp.dma_start(
            out=rt_b[:].rearrange("p (fc o) -> p fc o", fc=8),
            in_=routerT.rearrange("(fc p) o -> p fc o", p=128)), "dg", 16)
        c_rt = em.c["dg"]

        D("vector", lambda v: v.memset(zcol[:], 0.0))
        D("vector", lambda v: v.memset(yac[:], 0.0), "v0")
        cv_y0 = em.c["v0"]
        W("scalar", "ds", c_consts)
        cs_sil = D("scalar", lambda s: s.activation(silc[:], cfm[:], AF.Silu, bias=zcol[:]), "s0")

        mod_ps = [psA, psB, psC, psD]
        cv_h = None
        for rnd in range(2):
            for fc in range(8):
                cw = D("sync", lambda sp, rnd=rnd, fc=fc: sp.dma_start(
                    out=wmt[:], in_=wm2[128 * fc:128 * (fc + 1), 2048 * rnd:2048 * (rnd + 1)]), "ds", 16)
                W("tensor", "ds", cw)
                if rnd == 0 and fc == 0:
                    W("tensor", "s0", cs_sil)
                for half in range(4):
                    D("tensor", lambda t, fc=fc, half=half: t.matmul(
                        mod_ps[half][0:1, :], silc[:, fc:fc + 1], wmt[:, 512 * half:512 * (half + 1)],
                        start=(fc == 0), stop=(fc == 7)), "t0" if half == 3 else None)
                if fc < 7:
                    W("sync", "t0", em.c["t0"])
            cmod = em.c["t0"]
            W("vector", "t0", cmod)
            for half in range(4):
                DV(lambda v, half=half: v.tensor_copy(
                    modrow[0:1, 512 * half:512 * (half + 1)], mod_ps[half][0:1, :]))
            clamp_base = 0 if rnd == 0 else 1024
            scr1 = gab[0:1, :] if rnd == 0 else bob[0:1, :]
            scr2 = bob[0:1, :] if rnd == 0 else capcb[0:1, :]
            scr3 = capcb[0:1, :] if rnd == 0 else gbb[0:1, :]
            DV(lambda v, b=clamp_base, scr1=scr1: v.tensor_scalar(
                scr1, modrow[0:1, b:b + E], 0.25, None, ALU.max))
            DV(lambda v, b=clamp_base, scr2=scr2: v.tensor_scalar(
                scr2, modrow[0:1, b:b + E], -0.25, None, ALU.min))
            DV(lambda v, b=clamp_base, scr3=scr3: v.tensor_scalar(
                scr3, modrow[0:1, b:b + E], 0.0, None, ALU.is_ge))
            DV(lambda v, scr1=scr1, scr2=scr2: v.tensor_tensor(scr1, scr1, scr2, ALU.subtract))
            DV(lambda v, scr1=scr1, scr3=scr3: v.tensor_tensor(scr1, scr3, scr1, ALU.mult))
            DV(lambda v, b=clamp_base, scr1=scr1, scr2=scr2: v.tensor_tensor(
                modrow[0:1, b:b + E], scr2, scr1, ALU.add))
            if rnd == 1:
                DV(lambda v: v.tensor_scalar(
                    modrow[0:1, 0:E], modrow[0:1, 0:E], 1.0, 1.0, ALU.mult, ALU.add))
            cv_mod = DV(lambda v: v.tensor_copy(lncol[0:1, 1:2], modrow[0:1, 0:1]), "v0")
            W("tensor", "v0", cv_mod)
            dsts = ((gab, 0), (sbb, 1024)) if rnd == 0 else ((capcb, 0), (gbb, 1024))
            for dst, lo in dsts:
                for half in range(2):
                    cbt = D("tensor", lambda t, lo=lo, half=half: t.matmul(
                        psA, onesr[0:1, :], modrow[0:1, lo + 512 * half:lo + 512 * (half + 1)],
                        start=True, stop=True), "t0")
                    W("vector", "t0", cbt)
                    cvh = DV(lambda v, dst=dst, half=half: v.tensor_copy(
                        dst[:, 512 * half:512 * (half + 1)], psA), "v0")
                    W("tensor", "v0", cvh)
            if rnd == 0:
                W("sync", "v0", em.c["v0"])

        W("sync", "v0", em.c["v0"])
        cbo = D("sync", lambda sp: sp.dma_start(out=modrow[0:1, 0:E], in_=bo_row), "ds", 16)
        ceb = D("sync", lambda sp: sp.dma_start(out=modrow[0:1, E:E + 8], in_=eb_row), "ds", 16)
        W("tensor", "ds", ceb)
        for dst, src, n in ((bob, modrow[0:1, 0:E], 2), (ebb, modrow[0:1, E:E + 8], 1)):
            for half in range(n):
                wsl = 512 * half
                wid = 512 if dst is bob else 8
                cbt = D("tensor", lambda t, src=src, wsl=wsl, wid=wid: t.matmul(
                    psA[:, 0:wid], onesr[0:1, :], src[0:1, wsl:wsl + wid],
                    start=True, stop=True), "t0")
                W("vector", "t0", cbt)
                cvh = DV(lambda v, dst=dst, wsl=wsl, wid=wid: v.tensor_copy(
                    dst[:, wsl:wsl + wid], psA[:, 0:wid]), "v0")
                W("tensor", "v0", cvh)
        c_s0v = em.c["v0"]
        c_s0t = em.c["t0"]

        tok_z, tok_x = {}, {}
        tok_h2 = {}
        tok_nmm = {}

        def zload(ti):
            toks = []
            for j in range(4):
                toks.append(D("sync", lambda sp, ti=ti, j=j: sp.dma_start(
                    out=zst[:, 1024 * j:1024 * (j + 1)],
                    in_=zp4[j, 128 * ti:128 * (ti + 1), :]), "ds", 16))
            toks.append(D("sync", lambda sp, ti=ti: sp.dma_start(
                out=xt1[:], in_=x_sl[128 * ti:128 * (ti + 1), :]), "ds", 16))
            return toks[-1]

        tok_h2tm = {}
        tok_lt0 = {}
        for ti in range(NTT):
            if ti > 0:
                W("sync", "v0", tok_h2tm[ti - 1])
                W("sync", "dz", 16 * ti)
            else:
                W("sync", "t0", c_s0t)
            tok_x[ti] = zload(ti)
            W("vector", "ds", tok_x[ti])
            if ti == 0:
                W("vector", "v0", c_s0v)
            DV(lambda v: v.tensor_tensor(sqt[:], zst[:, 0:1024], zst[:, 1024:2048], ALU.add))
            DV(lambda v: v.tensor_tensor(zst[:, 0:1024], zst[:, 2048:3072], zst[:, 3072:4096], ALU.add))
            DV(lambda v: v.tensor_tensor(sqt[:], sqt[:], zst[:, 0:1024], ALU.add))
            DV(lambda v: v.tensor_tensor(sqt[:], sqt[:], bob[:], ALU.add))
            DV(lambda v: v.tensor_tensor(sqt[:], sqt[:], gab[:], ALU.mult))
            cx1 = DV(lambda v: v.tensor_tensor(xt1[:], xt1[:], sqt[:], ALU.add), "v0")
            W("sync", "v0", cx1)
            D("sync", lambda sp, ti=ti: sp.dma_start(
                out=x1d[128 * ti:128 * (ti + 1), :], in_=xt1[:]), "dz", 16)
            DV(lambda v: v.reduce_sum(lncol[:, 0:1], xt1[:], axis=AXX))
            if tok_nmm.get("last"):
                W("scalar", "t0", tok_nmm["last"])
            W("scalar", "v0", cx1)
            csq = D("scalar", lambda s: s.activation(sqt[:], xt1[:], AF.Square, bias=zcol[:]), "s0")
            W("vector", "s0", csq)
            DV(lambda v: v.reduce_sum(lncol[:, 1:2], sqt[:], axis=AXX))
            DV(lambda v: v.tensor_scalar(lncol[:, 2:3], lncol[:, 0:1], 1.0 / E, None, ALU.mult))
            DV(lambda v: v.tensor_scalar(lncol[:, 3:4], lncol[:, 1:2], 1.0 / E, None, ALU.mult))
            DV(lambda v: v.tensor_tensor(lncol[:, 4:5], lncol[:, 2:3], lncol[:, 2:3], ALU.mult))
            DV(lambda v: v.tensor_scalar(lncol[:, 4:5], lncol[:, 3:4], lncol[:, 4:5], None, ALU.subtract))
            DV(lambda v: v.tensor_scalar(lncol[:, 4:5], lncol[:, 4:5], 1.0, EPS, ALU.mult, ALU.add))
            cvv = DV(lambda v: v.reciprocal(lncol[:, 5:6], lncol[:, 4:5]), "v0")
            W("scalar", "v0", cvv)
            cs2 = D("scalar", lambda s: s.activation(lncol[:, 6:7], lncol[:, 5:6], AF.Sqrt, bias=zcol[:]), "s0")
            W("vector", "s0", cs2)
            DV(lambda v: v.tensor_scalar(lncol[:, 7:8], lncol[:, 2:3], lncol[:, 6:7], None, ALU.mult))
            DV(lambda v: v.tensor_scalar(lncol[:, 7:8], lncol[:, 7:8], -1.0, None, ALU.mult))
            DV(lambda v: v.tensor_scalar(sqt[:], xt1[:], lncol[:, 6:7], lncol[:, 7:8], ALU.mult, ALU.add))
            DV(lambda v: v.tensor_tensor(sqt[:], sqt[:], capcb[:], ALU.mult))
            tok_h2tm[ti] = DV(lambda v: v.tensor_tensor(sqt[:], sqt[:], sbb[:], ALU.add), "v0")
            W("tensor", "v0", tok_h2tm[ti])
            if ti > 0:
                W("tensor", "v0", tok_lt0[ti - 1])
            for fc in range(8):
                bank = psA if fc < 4 else psB
                D("tensor", lambda t, fc=fc, bank=bank: t.transpose(
                    bank[:, 128 * (fc % 4):128 * (fc % 4 + 1)],
                    sqt[:].rearrange("p (fc q) -> p fc q", fc=8)[:, fc, :],
                    identt[:]), "t0" if fc in (3, 7) else None)
            tok_tr = em.c["t0"]
            W("vector", "t0", tok_tr)
            h23 = h2[:].rearrange("p (fc t) -> p fc t", fc=8)
            D("vector", lambda v, ti=ti, h23=h23: v.tensor_copy(
                h23[:, 0:4, 128 * ti:128 * (ti + 1)],
                psA.rearrange("p (fc q) -> p fc q", fc=4)))
            tok_h2[ti] = D("vector", lambda v, ti=ti, h23=h23: v.tensor_copy(
                h23[:, 4:8, 128 * ti:128 * (ti + 1)],
                psB.rearrange("p (fc q) -> p fc q", fc=4)), "v0")
            W("tensor", "v0", tok_h2[ti])
            if ti == 0:
                W("tensor", "dg", c_rt)
            for fc in range(8):
                D("tensor", lambda t, ti=ti, fc=fc: t.matmul(
                    psE[:], h2[:, TS * fc + 128 * ti: TS * fc + 128 * (ti + 1)],
                    rt_b[:, 8 * fc:8 * (fc + 1)],
                    start=(fc == 0), stop=(fc == 7)), "t0" if fc == 7 else None)
            tok_nmm["last"] = em.c["t0"]
            W("vector", "t0", tok_nmm["last"])
            tok_lt = DV(lambda v: v.tensor_tensor(ltile[:], psE[:], ebb[:], ALU.add), "v0")
            DV(lambda v: v.reduce_max(mcols[:, 0:1], ltile[:], axis=AXX))
            DV(lambda v: v.tensor_scalar(mcols[:, 1:2], mcols[:, 0:1], -1.0, None, ALU.mult))
            DV(lambda v: v.tensor_scalar(ktile[:], ltile[:], mcols[:, 0:1], None, ALU.is_ge))
            DV(lambda v: v.scalar_tensor_tensor(etile[:], ktile[:], -1e30, ltile[:], ALU.mult, ALU.add))
            DV(lambda v: v.reduce_max(mcols[:, 2:3], etile[:], axis=AXX))
            cvk = DV(lambda v: v.tensor_scalar(ktile[:], ltile[:], mcols[:, 2:3], None, ALU.is_ge), "v0")
            W("scalar", "v0", cvk)
            cse = DS(lambda s: s.activation(etile[:], ltile[:], AF.Exp, bias=mcols[:, 1:2]), "s0")
            W("vector", "s0", cse)
            DV(lambda v: v.tensor_tensor(etile[:], etile[:], ktile[:], ALU.mult))
            DV(lambda v: v.reduce_sum(mcols[:, 3:4], etile[:], axis=AXX))
            DV(lambda v: v.reciprocal(mcols[:, 1:2], mcols[:, 3:4]))
            tok_lt0[ti] = tok_lt
            if dbg and ti == 0:
                DV(lambda v: v.tensor_copy(dlg_t[:, 0:8], ltile[:]))
                DV(lambda v: v.tensor_copy(dlg_t[:, 8:16], ktile[:]))
            tok_cmb = DV(lambda v, ti=ti: v.tensor_scalar(
                cmb[:, 8 * ti:8 * (ti + 1)], etile[:], mcols[:, 1:2], None, ALU.mult), "v0")
        c_s1v, c_s1t = em.c["v0"], em.c["t0"]

        if dbg:
            W("gpsimd", "v0", c_s1v)
            W("gpsimd", "dz", em.c["dz"])
            D("gpsimd", lambda gp: gp.dma_start(out=dbg_o["d_h2"], in_=h2[:, 0:512]), "dg", 16)
            D("gpsimd", lambda gp: gp.dma_start(out=dbg_o["d_gb"], in_=gbb[:]), "dg", 16)
            D("gpsimd", lambda gp: gp.dma_start(out=dbg_o["d_lg"], in_=dlg_t[:]), "dg", 16)
            D("gpsimd", lambda gp: gp.dma_start(out=dbg_o["d_cmb"], in_=cmb[:, 0:16]), "dg", 16)
            D("gpsimd", lambda gp: gp.dma_start(out=dbg_o["d_x1"], in_=x1d[0:128, :]), "dg", 16)

        if stop < 2:
            W("sync", "ds", em.c["ds"]); W("sync", "dz", em.c["dz"])
            W("gpsimd", "dg", em.c["dg"])
            W("vector", "v0", em.c["v0"]); W("scalar", "s0", em.c["s0"])
            W("tensor", "t0", em.c["t0"])
            return

        def wcol_load(slot, src, e, ht, ring):
            return D("gpsimd", lambda gp, slot=slot, src=src, e=e, ht=ht, ring=ring: gp.dma_start(
                out=ring[:, 1024 * slot:1024 * (slot + 1)].rearrange("p (fc h) -> p fc h", fc=8),
                in_=src[e, :, 128 * ht:128 * (ht + 1)].rearrange("(fc p) h -> p fc h", p=128)),
                "dg", 16)

        def w2_load(src, e, hc):
            return D("gpsimd", lambda gp, src=src, e=e, hc=hc: gp.dma_start(
                out=w2f[:, 1024 * hc:1024 * (hc + 1)],
                in_=src[e, 128 * hc:128 * (hc + 1), :]), "dg", 16)

        units = [(w1T, w3T, w2T, e, "r") for e in range(NE)] + \
                [(sw1T, sw3T, sw2T, e, "s") for e in range(NSH)]

        sil_list = []
        u_list = []
        yev_list = []
        w1_toks, w3_toks, w2_toks = {}, {}, {}
        w1_mmdone, w3_mmdone = {}, {}
        W("gpsimd", "v0", c_s1v)
        w1_toks[(0, 0)] = wcol_load(0, w1T, 0, 0, w1r)
        w3_toks[(0, 0)] = wcol_load(0, w3T, 0, 0, w3r)

        for ui, (s1, s3, s2, e, kind) in enumerate(units):
            for ht in range(16):
                slot = ht % 2
                nxt = (ui, ht + 1) if ht + 1 < 16 else ((ui + 1, 0) if ui + 1 < len(units) else None)
                if nxt is not None:
                    prev_same_slot = (ui, ht - 1) if ht >= 1 else (ui - 1, 15)
                    if prev_same_slot in w1_mmdone:
                        W("gpsimd", "t0", w1_mmdone[prev_same_slot])
                    srcn = s1 if nxt[0] == ui else units[nxt[0]][0]
                    en = e if nxt[0] == ui else units[nxt[0]][3]
                    w1_toks[nxt] = wcol_load((ht + 1) % 2, srcn, en, nxt[1], w1r)
                W("tensor", "dg", w1_toks[(ui, ht)])
                if ui == 0 and ht == 0:
                    W("tensor", "v0", c_s1v)
                for tq in range(4):
                    bank = psA if tq % 2 == 0 else psB
                    if ht == 0 and tq <= 1 and ui > 0:
                        W("tensor", "v0", yev_list[-1])
                    if len(sil_list) >= 2:
                        W("tensor", "s0", sil_list[-2])
                    for fc in range(8):
                        D("tensor", lambda t, ht=ht, tq=tq, fc=fc, bank=bank, slot=slot: t.matmul(
                            bank[:], w1r[:, 1024 * slot + 128 * fc:1024 * slot + 128 * (fc + 1)],
                            h2[:, TS * fc + 512 * tq:TS * fc + 512 * (tq + 1)],
                            start=(fc == 0), stop=(fc == 7)), "t0" if fc == 7 else None)
                    cmm = em.c["t0"]
                    W("scalar", "t0", cmm)
                    sil_list.append(D("scalar", lambda s, ht=ht, tq=tq, bank=bank: s.activation(
                        Au[:, TS * ht + 512 * tq: TS * ht + 512 * (tq + 1)], bank[:], AF.Silu,
                        bias=zcol[:]), "s0"))
                w1_mmdone[(ui, ht)] = em.c["t0"]
            c_sil_done = sil_list[-1]
            for ht in range(16):
                slot = ht % 2
                nxt = (ui, ht + 1) if ht + 1 < 16 else ((ui + 1, 0) if ui + 1 < len(units) else None)
                if nxt is not None:
                    prev_same_slot = (ui, ht - 1) if ht >= 1 else (ui - 1, 15)
                    if prev_same_slot in w3_mmdone:
                        W("gpsimd", "t0", w3_mmdone[prev_same_slot])
                    srcn = s3 if nxt[0] == ui else units[nxt[0]][1]
                    en = e if nxt[0] == ui else units[nxt[0]][3]
                    w3_toks[nxt] = wcol_load((ht + 1) % 2, srcn, en, nxt[1], w3r)
                W("tensor", "dg", w3_toks[(ui, ht)])
                for tq in range(4):
                    bank = psA if tq % 2 == 0 else psB
                    if len(u_list) >= 2:
                        W("tensor", "v0", u_list[-2])
                    elif ui == 0 and ht == 0 and tq == 0:
                        W("tensor", "s0", c_sil_done)
                    for fc in range(8):
                        D("tensor", lambda t, ht=ht, tq=tq, fc=fc, bank=bank, slot=slot: t.matmul(
                            bank[:], w3r[:, 1024 * slot + 128 * fc:1024 * slot + 128 * (fc + 1)],
                            h2[:, TS * fc + 512 * tq:TS * fc + 512 * (tq + 1)],
                            start=(fc == 0), stop=(fc == 7)), "t0" if fc == 7 else None)
                    cmm = em.c["t0"]
                    W("vector", "t0", cmm)
                    sidx = 64 * ui + 4 * ht + tq
                    W("vector", "s0", sil_list[sidx])
                    u_list.append(D("vector", lambda v, ht=ht, tq=tq, bank=bank: v.tensor_tensor(
                        Au[:, TS * ht + 512 * tq: TS * ht + 512 * (tq + 1)],
                        bank[:],
                        Au[:, TS * ht + 512 * tq: TS * ht + 512 * (tq + 1)], ALU.mult), "v0"))
                w3_mmdone[(ui, ht)] = em.c["t0"]
            c_u_done = u_list[-1]
            for hc in range(16):
                if ui > 0 and hc == 0:
                    W("gpsimd", "t0", w2_mmdone)
                w2_toks[(ui, hc)] = w2_load(s2, e, hc)
            W("tensor", "dg", w2_toks[(ui, 15)])
            W("tensor", "v0", c_u_done)
            for tq in range(16):
                for dh in range(2):
                    bank = psA if (2 * tq + dh) % 2 == 0 else psB
                    if len(yev_list) >= 2:
                        W("tensor", "v0", yev_list[-2])
                    for hc in range(16):
                        D("tensor", lambda t, tq=tq, dh=dh, hc=hc, bank=bank: t.matmul(
                            bank[:], Au[:, TS * hc + 128 * tq: TS * hc + 128 * (tq + 1)],
                            w2f[:, 1024 * hc + 512 * dh: 1024 * hc + 512 * (dh + 1)],
                            start=(hc == 0), stop=(hc == 15)), "t0" if hc == 15 else None)
                    cmm = em.c["t0"]
                    W("vector", "t0", cmm)
                    if kind == "r":
                        yev_list.append(D("vector", lambda v, tq=tq, dh=dh, e=e, bank=bank:
                                          v.scalar_tensor_tensor(
                            yac[:, 1024 * tq + 512 * dh: 1024 * tq + 512 * (dh + 1)],
                            bank[:], cmb[:, 8 * tq + e:8 * tq + e + 1],
                            yac[:, 1024 * tq + 512 * dh: 1024 * tq + 512 * (dh + 1)],
                            ALU.mult, ALU.add), "v0"))
                    else:
                        yev_list.append(D("vector", lambda v, tq=tq, dh=dh, bank=bank:
                                          v.scalar_tensor_tensor(
                            yac[:, 1024 * tq + 512 * dh: 1024 * tq + 512 * (dh + 1)],
                            bank[:], 1.0,
                            yac[:, 1024 * tq + 512 * dh: 1024 * tq + 512 * (dh + 1)],
                            ALU.mult, ALU.add), "v0"))
            w2_mmdone = em.c["t0"]
        c_s2v = em.c["v0"]

        tok_out = {}
        for ti in range(NTT):
            if ti >= 2:
                W("sync", "v0", tok_out[ti - 2])
            elif ti == 0:
                W("sync", "dg", em.c["dg"])
                W("sync", "t0", em.c["t0"])
            cx = D("sync", lambda sp, ti=ti: sp.dma_start(
                out=x1rd[:, 1024 * (ti % 2):1024 * (ti % 2 + 1)],
                in_=x1d[128 * ti:128 * (ti + 1), :]), "ds", 16)
            W("vector", "ds", cx)
            if ti == 0:
                W("vector", "v0", c_s2v)
            if ti >= 2:
                W("vector", "dz", 16 * (NTT + ti - 1))
            DV(lambda v, ti=ti: v.tensor_tensor(
                outr[:, 1024 * (ti % 2):1024 * (ti % 2 + 1)],
                yac[:, 1024 * ti:1024 * (ti + 1)], gbb[:], ALU.mult))
            tok_out[ti] = DV(lambda v, ti=ti: v.tensor_tensor(
                outr[:, 1024 * (ti % 2):1024 * (ti % 2 + 1)],
                outr[:, 1024 * (ti % 2):1024 * (ti % 2 + 1)],
                x1rd[:, 1024 * (ti % 2):1024 * (ti % 2 + 1)], ALU.add), "v0")
            W("sync", "v0", tok_out[ti])
            if dbg and ti == 0:
                W("gpsimd", "v0", tok_out[0])
                D("gpsimd", lambda gp: gp.dma_start(out=dbg_o["d_o0"], in_=outr[:, 0:1024]), "dg", 16)
                D("gpsimd", lambda gp: gp.dma_start(out=dbg_o["d_xr0"], in_=x1rd[:, 0:1024]), "dg", 16)
                W("sync", "dg", em.c["dg"])
            D("sync", lambda sp, ti=ti: sp.dma_start(
                out=oout[128 * ti:128 * (ti + 1), :],
                in_=outr[:, 1024 * (ti % 2):1024 * (ti % 2 + 1)]), "dz", 16)
        if dbg:
            W("gpsimd", "v0", em.c["v0"])
            D("gpsimd", lambda gp: gp.dma_start(out=dbg_o["d_y"], in_=yac[:, 0:1024]), "dg", 16)
            cm2 = D("gpsimd", lambda gp: gp.dma_start(out=dbg_o["d_mod"][:, 0:4], in_=mcols[:, 0:4]), "dg", 16)

        W("sync", "dz", em.c["dz"])
        W("gpsimd", "dg", em.c["dg"])

    with nc.Block() as block:
        block.sync(lambda en: program(Em(nc, sems, "sync", en)))
        block.gpsimd(lambda en: program(Em(nc, sems, "gpsimd", en)))
        block.tensor(lambda en: program(Em(nc, sems, "tensor", en)))
        block.vector(lambda en: program(Em(nc, sems, "vector", en)))
        block.scalar(lambda en: program(Em(nc, sems, "scalar", en)))
    return nc


def l2_inputs(core, ins, zparts):
    b, q = core // 4, core % 4
    f32 = np.float32
    sl = slice(2048 * q, 2048 * (q + 1))
    zp4 = np.stack([np.asarray(zparts[4 * b + j][sl], f32) for j in range(4)], axis=0)
    tr = lambda w: np.ascontiguousarray(np.asarray(w, f32).transpose(0, 2, 1))
    return {
        "zp4": np.ascontiguousarray(zp4),
        "x_sl": np.ascontiguousarray(np.asarray(ins["L0_attn_x"], f32)[b, sl]),
        "c_fmaj": np.ascontiguousarray(np.asarray(ins["L0_c"], f32)[b].reshape(8, 128).T),
        "wm2": np.ascontiguousarray(np.asarray(ins["L0_wmod"], f32)[:, 2048:6144]),
        "bo_row": np.ascontiguousarray(np.asarray(ins["L0_attn_bo"], f32).reshape(1, E)),
        "routerT": np.ascontiguousarray(np.asarray(ins["L0_ffn_router_w"], f32).T),
        "eb_row": np.ascontiguousarray(np.asarray(ins["L0_ffn_expert_bias"], f32).reshape(1, 8)),
        "w1T": tr(ins["L0_ffn_w1"]),
        "w3T": tr(ins["L0_ffn_w3"]),
        "w2T": tr(ins["L0_ffn_w2"]),
        "sw1T": tr(ins["L0_ffn_sw1"]),
        "sw3T": tr(ins["L0_ffn_sw3"]),
        "sw2T": tr(ins["L0_ffn_sw2"]),
    }


from concourse.bass_utils import run_bass_kernel_spmd

_CACHE = {}

def kernel(**inputs):
    if "l1" not in _CACHE:
        _CACHE["l1"] = build_l1(dbg=False)
    nc1 = _CACHE["l1"]
    im1 = [l1_inputs(c, inputs) for c in range(8)]
    r1 = run_bass_kernel_spmd(nc1, im1, core_ids=list(range(8)))
    zparts = [r1.results[c]["zpart"] for c in range(8)]

    if "l2" not in _CACHE:
        _CACHE["l2"] = build_l2(dbg=False)
    nc2 = _CACHE["l2"]
    im2 = [l2_inputs(c, inputs, zparts) for c in range(8)]
    r2 = run_bass_kernel_spmd(nc2, im2, core_ids=list(range(8)))

    out = np.empty((2, 8192, 1024), dtype=np.float32)
    for c in range(8):
        b, q = c // 4, c % 4
        out[b, 2048 * q:2048 * (q + 1), :] = r2.results[c]["oslice"]
    return out
